# version 10
# baseline (speedup 1.0000x reference)
import numpy as np
import ml_dtypes
from contextlib import ExitStack
import concourse.bass as bass
import concourse.mybir as mybir
from concourse.bass_utils import run_bass_kernel_spmd

F32 = mybir.dt.float32
BF16 = mybir.dt.bfloat16
FP8 = mybir.dt.float8e4
AF = mybir.ActivationFunctionType
ALU = mybir.AluOpType
AX = mybir.AxisListType

D = 2048
S = 2048
Q = 1024
H = 16
FF = 8192
NW = float(D * FF)
ISQ = float(1.0 / np.sqrt(128.0))
EPS1 = 1e-5
EPS2 = 1e-6
CLIPB = float((128.0 - 1e-6) / 128.0)

SB_TOP = 229344
QT_R = (SB_TOP - 32768, SB_TOP)
XH_R = (SB_TOP - 32768, SB_TOP)
AT_R = (SB_TOP - 65536, SB_TOP - 32768)
PT_R = [(SB_TOP - 131072 + i * 16384, SB_TOP - 131072 + (i + 1) * 16384)
        for i in range(4)]
Y_R = (SB_TOP - 131072, SB_TOP - 65536)
X2_R = (SB_TOP - 131072, SB_TOP)

ENGS = ["sync", "tensor", "vector", "scalar", "gpsimd"]
SEM_OF = {"sync": "dS", "tensor": "tsem", "vector": "vsem", "scalar": "ssem",
          "gpsimd": "gsem"}


class Prog:
    def __init__(self):
        self.blocks = []
        self.ev = {}

    def block(self):
        b = {e: [] for e in ENGS}
        self.blocks.append(b)
        return b

    def op(self, b, eng, fn, *, mark=None, sem=None, k=1, waits=()):
        b[eng].append((fn, mark, sem or SEM_OF[eng], k, tuple(waits)))

    def resolve(self):
        self.ev.clear()
        self.drains = {}
        self.selfw = {}
        self.blockstart = {}
        cnt = {}
        for bi, b in enumerate(self.blocks):
            self.blockstart[bi] = [(s, cnt.get(s, 0))
                                   for s in ("tsem", "vsem", "ssem", "gsem")
                                   if cnt.get(s, 0)]
            for eng in ENGS:
                seen = {}
                own = SEM_OF[eng]
                for oi, (fn, mark, sem, k, waits) in enumerate(b[eng]):
                    if eng in ("vector", "scalar", "gpsimd"):
                        self.selfw[(bi, eng, oi)] = cnt.get(own, 0)
                    cnt[sem] = cnt.get(sem, 0) + k
                    if mark is not None:
                        assert mark not in self.ev, f"duplicate mark {mark}"
                        self.ev[mark] = (sem, cnt[sem])
                    if sem not in ("tsem", "vsem", "ssem", "gsem"):
                        seen[sem] = cnt[sem]
                if seen:
                    self.drains[(bi, eng)] = sorted(seen.items())

    def emit(self, nc, sems):
        for bi, b in enumerate(self.blocks):
            with nc.Block() as blk:
                for eng in ENGS:
                    ops = b[eng]
                    if not ops:
                        continue
                    drains = self.drains.get((bi, eng), [])

                    def body(e, ops=ops, drains=drains, bi=bi, eng=eng):
                        own = sems.get(SEM_OF[eng])
                        for s_, v_ in self.blockstart.get(bi, []):
                            e.wait_ge(sems[s_], v_)
                        for oi, (fn, mark, sem, k, waits) in enumerate(ops):
                            sv = self.selfw.get((bi, eng, oi))
                            if sv:
                                e.wait_ge(own, sv)
                            for w in waits:
                                if w in self.ev:
                                    ws, wc = self.ev[w]
                                    e.wait_ge(sems[ws], wc)
                            ins = fn(e)
                            assert ins is not None
                            ins.then_inc(sems[sem], k)
                        for s_, v_ in drains:
                            e.wait_ge(sems[s_], v_)
                    getattr(blk, eng)(body)


def build_program(debug=False):
    nc = bass.Bass("TRN2", target_bir_lowering=False)
    gs = ExitStack()
    P = Prog()

    xt = nc.dram_tensor("xt", [D, S], F32, kind="ExternalInput")
    xq = nc.dram_tensor("xq", [D, Q], F32, kind="ExternalInput")
    wq = nc.dram_tensor("wq", [D, D], F32, kind="ExternalInput")
    wk = nc.dram_tensor("wk", [D, D], F32, kind="ExternalInput")
    wv = nc.dram_tensor("wv", [D, D], F32, kind="ExternalInput")
    wo = nc.dram_tensor("wo", [D, D], F32, kind="ExternalInput")
    sq = nc.dram_tensor("sq", [D, D], F32, kind="ExternalInput")
    sk = nc.dram_tensor("sk", [D, D], F32, kind="ExternalInput")
    sv = nc.dram_tensor("sv", [D, D], F32, kind="ExternalInput")
    so = nc.dram_tensor("so", [D, D], F32, kind="ExternalInput")
    bqd = nc.dram_tensor("bq", [128, 16], F32, kind="ExternalInput")
    bkd = nc.dram_tensor("bk", [128, 16], F32, kind="ExternalInput")
    bvd = nc.dram_tensor("bv", [128, 16], F32, kind="ExternalInput")
    bod = nc.dram_tensor("bo", [128, 16], F32, kind="ExternalInput")
    wfc = nc.dram_tensor("wfc", [D, FF], F32, kind="ExternalInput")
    wpr = nc.dram_tensor("wpr", [FF, D], F32, kind="ExternalInput")
    out = nc.dram_tensor("out", [D, Q], F32, kind="ExternalOutput")

    kd = {"kind": "ExternalOutput"} if debug else {}
    kspill = nc.dram_tensor("kspill", [D, S], BF16, **kd)
    vspill = nc.dram_tensor("vspill", [H, S, 128], BF16, **kd)
    tproj = nc.dram_tensor("tproj", [FF, D], FP8, **kd)
    hspill = nc.dram_tensor("hspill", [FF, Q], BF16, **kd)
    xnspill = nc.dram_tensor("xnspill", [D, Q], F32, **kd)
    y2spill = nc.dram_tensor("y2spill", [D, Q], F32, **kd)

    semnames = (["tsem", "vsem", "ssem", "gsem", "ost", "dX", "gX", "kSp", "xR"]
                + [f"{p}{i}" for p in ["gW", "vT", "vP", "kR", "vR", "tP",
                                       "xQ", "oW", "hW", "tW", "hR",
                                       "tR", "yW", "yR", "oD"] for i in range(2)]
                + [f"{p}{i}" for p in ["fW", "pW", "gG"] for i in range(4)])
    sems = {n: gs.enter_context(nc.semaphore(n)) for n in semnames}

    def sb(name, shape, dt, scope, **kw):
        return scope.enter_context(nc.sbuf_tensor(name, shape, dt, **kw))

    def psum(name, shape, scope):
        return scope.enter_context(nc.psum_tensor(name, shape, F32))

    ones_bf = sb("ones_bf", [128, 1], BF16, gs)
    ones_f = sb("ones_f", [128, 1], F32, gs)
    onesr_f = sb("onesr_f", [1, 128], F32, gs)
    id128 = sb("id128", [128, 128], F32, gs)
    id128b = sb("id128b", [128, 128], BF16, gs)
    sel8 = sb("sel8", [8, 1024], F32, gs)
    bq_s = sb("bq_s", [128, 16], F32, gs)
    bk_t = sb("bk_t", [128, 16], F32, gs)
    bv_t = sb("bv_t", [128, 16], F32, gs)
    bo_t = sb("bo_t", [128, 16], F32, gs)
    thf = sb("thf", [128, 1], F32, gs)
    nthf = sb("nthf", [128, 1], F32, gs)
    thp = sb("thp", [128, 1], F32, gs)
    nthp = sb("nthp", [128, 1], F32, gs)
    gfb = sb("gfb", [128, 1], F32, gs)
    gpb = sb("gpb", [128, 1], F32, gs)
    gacc_f = sb("gacc_f", [128, 64], F32, gs)
    gacc_p = sb("gacc_p", [128, 64], F32, gs)
    g1 = sb("g1", [1, 16], F32, gs)
    sums = None
    stn = ["sums1", "sumsq1", "mu", "var", "R", "rho", "gx", "cx", "ncx",
           "mu2", "R2", "c2", "nc2", "mu3", "R3", "tmp1", "tmp2", "tmp3",
           "mx", "mn"]
    stt = {n: sb("st_" + n, [128, 8], F32, gs) for n in stn}
    muB = sb("muB", [128, Q], F32, gs)
    RB = sb("RB", [128, Q], F32, gs)
    cB = sb("cB", [128, Q], F32, gs)
    ncB = sb("ncB", [128, Q], F32, gs)
    rhoB = sb("rhoB", [128, Q], F32, gs)
    rmax = sb("rmax", [128, Q], BF16, gs)
    rmin = sb("rmin", [128, Q], BF16, gs)
    rminf = rmin.bitcast(F32)
    sums = rminf[:, 336:464]
    trT = sb("trT", [8, 128], F32, gs)
    arena1 = sb("arena1", [128, 16384], F32, gs)
    arena2 = sb("arena2", [128, 16384], BF16, gs)
    arena3 = sb("arena3", [128, 16384], BF16, gs)
    a1b = arena1.bitcast(BF16)
    a2f = arena2.bitcast(F32)
    Qt = arena3.ap().rearrange("p (a c) -> p a c", a=16)
    attn = arena2.ap().rearrange("p (a c) -> p a c", a=16)
    PT = [a1b[:, i * 8192:(i + 1) * 8192].rearrange("p (kc i) -> p kc i", kc=16)
          for i in range(4)]
    y = arena1.ap().rearrange("p (a c) -> p a c", a=16)
    xh = arena3.ap().rearrange("p (a c) -> p a c", a=16)
    x2parts = [a1b.rearrange("p (a c) -> p a c", a=32),
               arena2.ap().rearrange("p (a c) -> p a c", a=16),
               arena3.ap().rearrange("p (a c) -> p a c", a=16)]

    def x2w(oc0):
        if oc0 < 32:
            return x2parts[0][:, oc0:oc0 + 2, :]
        if oc0 < 48:
            return x2parts[1][:, oc0 - 32:oc0 - 30, :]
        return x2parts[2][:, oc0 - 48:oc0 - 46, :]

    def x2sl(oc, c0, c1):
        if oc < 32:
            return x2parts[0][:, oc, c0:c1]
        if oc < 48:
            return x2parts[1][:, oc - 32, c0:c1]
        return x2parts[2][:, oc - 48, c0:c1]

    b = P.block()
    for dst, src in [(bq_s, bqd), (bk_t, bkd), (bv_t, bvd), (bo_t, bod)]:
        P.op(b, "sync", lambda e, d=dst, s=src: e.dma_start(d[:, :], s[:, :]),
             sem="dX", k=16, mark=("bld", dst.name))
    for t_ in [ones_f, onesr_f, ones_bf, id128]:
        P.op(b, "vector", lambda e, t_=t_: e.memset(t_[:, :], 1.0))
    P.op(b, "vector", lambda e: e.memset(sel8[:, :], 1.0), mark="consts")
    P.op(b, "vector",
         lambda e: e.tensor_scalar(bq_s[:, :], bq_s[:, :], ISQ, None, ALU.mult),
         waits=[("bld", "bo_t")], mark="bqs")
    P.op(b, "vector", lambda e: e.memset(id128b[:, :], 1.0), mark="constsb")
    P.op(b, "gpsimd",
         lambda e: e.affine_select(id128[:, :], id128[:, :], [[1, 128]],
                                   ALU.is_equal, 0.0, base=0, channel_multiplier=-1),
         waits=["consts", "constsb"], mark="id128")
    P.op(b, "gpsimd",
         lambda e: e.affine_select(id128b[:, :], id128b[:, :], [[1, 128]],
                                   ALU.is_equal, 0.0, base=0, channel_multiplier=-1),
         mark="id128b")
    P.op(b, "gpsimd",
         lambda e: e.affine_select(sel8.ap().rearrange("p (b c) -> p b c", b=8),
                                   sel8.ap().rearrange("p (b c) -> p b c", b=8),
                                   [[1, 8], [0, 128]], ALU.is_equal, 0.0,
                                   base=0, channel_multiplier=-1),
         mark="sel8")

    s1 = ExitStack()
    x_fm = a1b.rearrange("p (a c) -> p a c", a=16)
    kev = sb("kev", [128, 4, 512], BF16, s1)
    ones8c = sb("ones8c", [128, 8], BF16, s1)
    ones8r = sb("ones8r", [8, 128], BF16, s1)
    mk2b8 = sb("mk2b8", [8, 16], BF16, s1)
    k2c = sb("k2c", [128, 2, 512], BF16, s1)
    kmx8 = rminf
    q2c = sb("q2c", [128, 2, 128], BF16, s1)
    ckb = [sb(f"ckb{i}", [128, 16, 128], BF16, s1) for i in range(2)]
    skb = [sb(f"skb{i}", [128, 16, 128], BF16, s1) for i in range(2)]
    wkb = [sb(f"wkb{i}", [128, 16, 128], BF16, s1) for i in range(2)]
    vrow = [sb(f"vrow{i}", [128, 4, 512], BF16, s1) for i in range(2)]
    vtrb = [sb(f"vtrb{i}", [128, 16, 128], BF16, s1, is_transpose_dest=True)
            for i in range(2)]
    pk = psum("pk", [128, 4, 512], s1)
    pkn = psum("pkn", [8, 2, 512], s1)

    def wcon(mi):
        if mi < 16:
            return ("pkg", mi * 4 + 3)
        if mi < 32:
            return ("pkg", 64 + (mi - 16) * 4 + 3)
        return ("pkg", 128 + (mi - 32) * 2 + 1)

    b = P.block()
    P.op(b, "gpsimd",
         lambda e: e.dma_start(x_fm,
                               xt.ap().rearrange("(dc p) t -> p dc t", p=128)),
         sem="gX", k=16, mark="x_fm")
    mats = [(wk, sk), (wv, sv), (wq, sq)]
    for mi in range(48):
        m, fc = divmod(mi, 16)
        cod, scl = mats[m]
        wt = [wcon(mi - 2)] if mi >= 2 else []
        P.op(b, "gpsimd",
             lambda e, cod=cod, fc=fc, mi=mi: e.dma_start(
                 ckb[mi % 2].ap(),
                 cod[:, fc * 128:(fc + 1) * 128].rearrange("(dc p) f -> p dc f", p=128)),
             sem=f"gW{mi % 2}", k=16, waits=wt)
        P.op(b, "gpsimd",
             lambda e, scl=scl, fc=fc, mi=mi: e.dma_start(
                 skb[mi % 2].ap(),
                 scl[:, fc * 128:(fc + 1) * 128].rearrange("(dc p) f -> p dc f", p=128)),
             sem=f"gW{mi % 2}", k=16, mark=("wdma", mi))
    wbst = {"i": 0}

    def emit_wb(upto):
        while wbst["i"] <= min(upto, 47):
            mi = wbst["i"]
            wt = [("wdma", mi)] + ([wcon(mi - 2)] if mi >= 2 else [])
            P.op(b, "vector",
                 lambda e, mi=mi: e.tensor_tensor(
                     wkb[mi % 2].ap().rearrange("p a c -> p (a c)"),
                     ckb[mi % 2].ap().rearrange("p a c -> p (a c)"),
                     skb[mi % 2].ap().rearrange("p a c -> p (a c)"), ALU.mult),
                 mark=("wbuild", mi), waits=wt)
            wbst["i"] += 1
    P.op(b, "vector", lambda e: e.memset(ones8c[:, :], 1.0))
    P.op(b, "vector", lambda e: e.memset(ones8r[:, :], 1.0), mark="ones8")
    emit_wb(1)

    groups = []
    for fc in range(16):
        for th in range(4):
            groups.append((len(groups), fc, th * 512))
    for fc in range(16):
        for th in range(4):
            groups.append((len(groups), 16 + fc, th * 512))
    for fc in range(16):
        for th in range(2):
            groups.append((len(groups), 32 + fc, th * 512))

    for n, mi, tok0 in groups:
        def mmg(e, n=n, mi=mi, tok0=tok0):
            ins = None
            for dc in range(16):
                ins = e.matmul(pk[:, n % 4, :], wkb[mi % 2][:, dc, :],
                               x_fm[:, dc, tok0:tok0 + 512],
                               start=(dc == 0), stop=(dc == 15))
            return ins
        wt = ["x_fm", ("wbuild", mi)]
        if n >= 4:
            wt.append(("pkev", n - 4))
        P.op(b, "tensor", mmg, mark=("pkg", n), waits=wt)
        j = n - 1
        if 0 <= j < 64:
            wtk = [("k2q", j), "ones8"] + ([("kmxr", j - 2)] if j >= 2 else [])
            P.op(b, "tensor",
                 lambda e, j=j: e.matmul(pkn[:, j % 2, :], ones8c[:, :],
                                         k2c[:, j % 2, :], start=True, stop=True),
                 mark=("pknm", j), waits=wtk)

    gplan = [("f", i) for i in range(64)] + [("p", i) for i in range(64)]
    KMX = 7424
    MKH = 7488
    MKB = 7504
    MH2 = 7168
    MHP = 7296
    NMH = 8064
    for n in range(64):
        fc, th = divmod(n, 4)
        wt = [("pkg", n)]
        if th == 0 and fc >= 1:
            wt.append(("kspill", fc - 1))
        wtq = [("pkg", n)] + ([("pknm", n - 2)] if n >= 2 else [])
        P.op(b, "scalar",
             lambda e, fc=fc, th=th, n=n: e.activation(
                 k2c[:, n % 2, :], pk[:, n % 4, :], AF.Square,
                 bias=bk_t[:, fc:fc + 1]),
             mark=("k2q", n), waits=wtq)
        P.op(b, "scalar",
             lambda e, fc=fc, th=th, n=n: e.activation(
                 kev[:, th, :], pk[:, n % 4, :], AF.Identity, bias=bk_t[:, fc:fc + 1]),
             mark=("pkev", n), waits=wt)
        emit_wb(fc + 1)
        P.op(b, "vector",
             lambda e, n=n: e.reduce_max(kmx8[0:8, 128 + n:129 + n],
                                         pkn[0:8, n % 2, :], axis=AX.X),
             mark=("kmxr", n), waits=[("pknm", n)])
        if th == 3:
            P.op(b, "vector",
                 lambda e, fc=fc: e.reduce_max(kmx8[0:8, 192 + fc:193 + fc],
                                               kmx8[0:8, 128 + 4 * fc:132 + 4 * fc],
                                               axis=AX.X),
                 mark=("mkh", fc))
        if th == 3:
            P.op(b, "sync",
                 lambda e, fc=fc: e.dma_start(
                     kspill[fc * 128:(fc + 1) * 128, :],
                     kev.rearrange("p a c -> p (a c)")),
                 sem="kSp", k=16, mark=("kspill", fc), waits=[("pkev", fc * 4 + 3)])
    for n in range(64, 128):
        fc, th = divmod(n - 64, 4)
        wt = [("pkg", n)]
        if th == 0 and fc >= 2:
            wt.append(("vtr", fc - 2))
        P.op(b, "scalar",
             lambda e, fc=fc, th=th, n=n: e.activation(
                 vrow[fc % 2][:, th, :], pk[:, n % 4, :], AF.Identity,
                 bias=bv_t[:, fc:fc + 1]),
             mark=("pkev", n), waits=wt)
        emit_wb((n - 64) // 4 + 17)
        if th == 3:
            wt2 = [("pkev", n)] + ([("vsp", fc - 2)] if fc >= 2 else [])
            P.op(b, "sync",
                 lambda e, fc=fc: e.dma_start(
                     vtrb[fc % 2].ap(),
                     vrow[fc % 2].ap().rearrange("p a c -> p (a c)")
                     .rearrange("p (tc i) -> p tc i", tc=16),
                     transpose=True),
                 sem=f"vT{fc % 2}", k=16, mark=("vtr", fc), waits=wt2)
            P.op(b, "sync",
                 lambda e, fc=fc: e.dma_start(
                     vspill[fc].rearrange("(tc p) i -> p tc i", p=128),
                     vtrb[fc % 2].ap()),
                 sem=f"vP{fc % 2}", k=16, mark=("vsp", fc), waits=[("vtr", fc)])
    for n in range(128, 160):
        fc, th = divmod(n - 128, 2)
        P.op(b, "scalar",
             lambda e, fc=fc, th=th, n=n: e.activation(
                 Qt[:, fc, th * 512:(th + 1) * 512], pk[:, n % 4, :],
                 AF.Identity, scale=ISQ, bias=bq_s[:, fc:fc + 1]),
             mark=("pkev", n), waits=[("pkg", n)])
        emit_wb((n - 128) // 2 + 33)
    emit_wb(47)


    s2 = ExitStack()
    pB2 = psum("pB2", [128, 512], s2)
    pqn = psum("pqn", [128, 128], s2)
    b = P.block()
    P.op(b, "vector", lambda e: e.tensor_copy(mk2b8[:, :], kmx8[0:8, 192:208]),
         mark="mk2b")
    P.op(b, "tensor",
         lambda e: e.matmul(pB2[:, 256:272], ones8r[:, :], mk2b8[:, :],
                            start=True, stop=True),
         mark="mkbc", waits=["mk2b"])
    P.op(b, "scalar", lambda e: e.copy(a2f[:, MKB:MKB + 16], pB2[:, 256:272]),
         mark="mkev", waits=["mkbc"])
    for j2 in range(128):
        h_, qb_ = divmod(j2, 8)
        P.op(b, "vector",
             lambda e, j2=j2, h_=h_, qb_=qb_: e.tensor_tensor(
                 q2c[:, j2 % 2, :], Qt[:, h_, qb_ * 128:(qb_ + 1) * 128],
                 Qt[:, h_, qb_ * 128:(qb_ + 1) * 128], ALU.mult),
             mark=("q2c", j2), waits=[("qnm", j2 - 2)] if j2 >= 2 else [])
        P.op(b, "tensor",
             lambda e, j2=j2: e.matmul(pqn[:, j2:j2 + 1], q2c[:, j2 % 2, :],
                                       ones_bf[:, :], start=True, stop=True),
             mark=("qnm", j2), waits=[("q2c", j2)])
    for h_ in range(16):
        P.op(b, "vector",
             lambda e, h_=h_: e.tensor_scalar(
                 a2f[:, MH2 + h_ * 8:MH2 + h_ * 8 + 8], pqn[:, h_ * 8:h_ * 8 + 8],
                 a2f[:, MKB + h_:MKB + h_ + 1], None, ALU.mult),
             mark=("mh2", h_), waits=[("qnm", h_ * 8 + 7)] + (["mkev"] if h_ == 0 else []))
    P.op(b, "scalar",
         lambda e: e.activation(a2f[:, MHP:MHP + 128], a2f[:, MH2:MH2 + 128], AF.Sqrt),
         mark="mhs", waits=[("mh2", 15)])
    P.op(b, "vector",
         lambda e: e.tensor_scalar(a2f[:, NMH:NMH + 128], a2f[:, MHP:MHP + 128],
                                   -1.0, None, ALU.mult),
         mark="nmh", waits=["mhs"])

    s2.close()
    s1.close()

    s3 = ExitStack()
    kslb = [sb(f"kslb{i}", [128, S], BF16, s3) for i in range(2)]
    vslb = [sb(f"vslb{i}", [128, 16, 128], BF16, s3) for i in range(2)]
    Pt = [sb(f"Pt{i}", [128, S], BF16, s3) for i in range(2)]
    sums2 = rminf
    grt = sb("grt", [128, 2, 2048], F32, s3)
    invT = rminf
    Sinv = sb("Sinv", [128, 8], F32, s3)
    invb = [sb(f"invb{i}", [128, Q], F32, s3) for i in range(2)]
    pscore = psum("pscore", [128, 2048], s3)
    pav = psum("pav", [128, 2, 512], s3)
    pinvb = psum("pinvb", [128, Q], s3)

    b = P.block()

    def emit_ptr(h, qb):
        qh = 0 if qb < 4 else 1
        wt2 = []
        if h >= 2 and qb % 4 == 0:
            wt2.append(("avmm", h - 2, qh))

        def ptrf(e, h=h, qb=qb, qh=qh):
            return e.dma_start(
                PT[(h % 2) * 2 + qh][:, :, (qb % 4) * 128:(qb % 4 + 1) * 128],
                Pt[qb % 2].ap().rearrange("p (kc i) -> p kc i", kc=16),
                transpose=True)
        P.op(b, "sync", ptrf, sem=f"tP{qb % 2}", k=16, mark=("ptr", h, qb),
             waits=[("exB", h, qb)] + wt2)

    for h in range(H):
        if h >= 1:
            for qb in range(4):
                emit_ptr(h - 1, qb)
        P.op(b, "sync",
             lambda e, h=h: e.dma_start(kslb[h % 2][:, :],
                                        kspill[h * 128:(h + 1) * 128, :]),
             sem=f"kR{h % 2}", k=16, mark=("ksl", h),
             waits=[("smB", h - 2, 7)] if h >= 2 else [])
        if h >= 1:
            for qb in range(4, 8):
                emit_ptr(h - 1, qb)
        P.op(b, "sync",
             lambda e, h=h: e.dma_start(
                 vslb[h % 2].ap(),
                 vspill[h].rearrange("(tc p) i -> p tc i", p=128)),
             sem=f"vR{h % 2}", k=16, mark=("vsl", h),
             waits=[("avmm", h - 2, 1)] if h >= 2 else [])
    for qb in range(8):
        emit_ptr(H - 1, qb)

    def gb_ap(j):
        return grt[:, j % 2, :]

    for j in range(128):
        which, i = gplan[j]
        if which == "f":
            nn, mm_ = divmod(i, 4)
            ap = wfc.ap().rearrange("(n p) (m f) -> n m p f", p=128, f=2048)[nn, mm_]
        else:
            ap = wpr.ap().rearrange("(n p) f -> n p f", p=128)[i]
        wt = [("gtile", j - 2)] if j >= 2 else []
        P.op(b, "gpsimd",
             lambda e, ap=ap, j=j: e.dma_start(gb_ap(j), ap),
             sem=f"gG{j % 2}", k=16, mark=("gdma", j), waits=wt)

    def emit_inv_tr(h):
        wt = [("recip", h)] + ([("invev", h - 1)] if h >= 1 else ["id128"])
        P.op(b, "tensor",
             lambda e: e.transpose(pinvb[0:8, 0:128], Sinv[:, :], id128[:, :]),
             mark=("invtr", h), waits=wt)

    def emit_inv_bc(h):
        def f(e):
            ins = None
            for qb in range(8):
                ins = e.matmul(pinvb[:, qb * 128:(qb + 1) * 128],
                               sel8[:, qb * 128:(qb + 1) * 128], invT[0:8, 208:336],
                               start=True, stop=True)
            return ins
        P.op(b, "tensor", f, mark=("invbc", h), waits=[("invTe", h)])

    def emit_av(hp, qh, kc0, kc1, markkey=None):
        def f(e, hp=hp, qh=qh, kc0=kc0, kc1=kc1):
            ins = None
            for kc in range(kc0, kc1):
                ins = e.matmul(pav[:, qh, :], vslb[hp % 2][:, kc, :],
                               PT[(hp % 2) * 2 + qh][:, kc, :],
                               start=(kc == 0), stop=(kc == 15))
            return ins
        wt = []
        if kc0 == 0:
            wt = [("ptr", hp, qh * 4 + 2), ("ptr", hp, qh * 4 + 3), ("vsl", hp)]
            if hp >= 1:
                wt.append(("avev", hp - 1, qh))
        P.op(b, "tensor", f, mark=markkey, waits=wt)

    for h in range(H):
        for qb in range(8):
            pq = (h, qb - 1) if qb > 0 else (h - 1, 7)

            def smmh(e, h=h, qb=qb, j0=0):
                ins = None
                for j in range(j0, j0 + 2):
                    ins = e.matmul(pscore[:, j * 512:(j + 1) * 512],
                                   Qt[:, h, qb * 128:(qb + 1) * 128],
                                   kslb[h % 2][:, j * 512:(j + 1) * 512],
                                   start=True, stop=True)
                return ins
            wtA = [("ksl", h)]
            wtA.append(("exA",) + pq if (h, qb) != (0, 0) else ("pkev", 159))
            P.op(b, "tensor", lambda e, f=smmh: f(e, j0=0), mark=("smA", h, qb),
                 waits=wtA)
            if h >= 1:
                hp, qh = h - 1, (0 if qb < 4 else 1)
                kc0 = (qb % 4) * 4
                emit_av(hp, qh, kc0, kc0 + 4,
                        markkey=("avmm", hp, qh) if qb % 4 == 3 else None)
            wtB = [("exB",) + pq] if (h, qb) != (0, 0) else []
            P.op(b, "tensor", lambda e, f=smmh: f(e, j0=2), mark=("smB", h, qb),
                 waits=wtB)
            if h >= 1 and qb == 0:
                emit_inv_tr(h - 1)
            if h >= 1 and qb == 1:
                emit_inv_bc(h - 1)
    hl = H - 1
    emit_inv_tr(hl)
    emit_inv_bc(hl)
    for qh in range(2):
        emit_av(hl, qh, 0, 16, markkey=("avmm", hl, qh))

    for h in range(H):
        if h >= 1:
            P.op(b, "vector",
                 lambda e, h=h: e.tensor_copy(invb[(h - 1) % 2][:, :], pinvb[:, :]),
                 mark=("invev", h - 1), waits=[("invbc", h - 1)])
            for qh in range(2):
                P.op(b, "vector",
                     lambda e, h=h, qh=qh: e.tensor_tensor(
                         attn[:, h - 1, qh * 512:(qh + 1) * 512], pav[:, qh, :],
                         invb[(h - 1) % 2][:, qh * 512:(qh + 1) * 512], ALU.mult),
                     mark=("avev", h - 1, qh),
                     waits=[("avmm", h - 1, qh), ("invev", h - 1)])
        P.op(b, "vector",
             lambda e, h=h: e.tensor_tensor(sums[:, h * 8:(h + 1) * 8],
                                            sums[:, h * 8:(h + 1) * 8],
                                            sums2[:, h * 8:(h + 1) * 8], ALU.add),
             mark=("scomb", h), waits=[("exB", h, 7)])
        P.op(b, "vector",
             lambda e, h=h: e.reciprocal(Sinv[:, :], sums[:, h * 8:(h + 1) * 8]),
             mark=("recip", h), waits=[("scomb", h)])
        for j in range(h * 8, h * 8 + 8):
            acc = gacc_f if gplan[j][0] == "f" else gacc_p
            P.op(b, "vector",
                 lambda e, j=j, acc=acc, i=gplan[j][1]: e.reduce_sum(
                     acc[:, i:i + 1], gb_ap(j), axis=AX.X,
                     apply_absolute_value=True),
                 mark=("gtile", j), waits=[("gdma", j)])
    P.op(b, "vector",
         lambda e: e.tensor_copy(invb[hl % 2][:, :], pinvb[:, :]),
         mark=("invev", hl), waits=[("invbc", hl)])
    for qh in range(2):
        P.op(b, "vector",
             lambda e, qh=qh: e.tensor_tensor(
                 attn[:, hl, qh * 512:(qh + 1) * 512], pav[:, qh, :],
                 invb[hl % 2][:, qh * 512:(qh + 1) * 512], ALU.mult),
             mark=("avev", hl, qh),
             waits=[("avmm", hl, qh), ("invev", hl)])

    for h in range(H):
        for qb in range(8):
            wt = [("smA", h, qb)]
            if h * 8 + qb >= 2:
                pq = (h, qb - 2) if qb >= 2 else (h - 1, 6 + qb)
                wt.append(("ptr",) + pq)
            j2 = h * 8 + qb
            P.op(b, "scalar",
                 lambda e, h=h, qb=qb, j2=j2: e.activation(
                     Pt[qb % 2][:, 0:1024], pscore[:, 0:1024], AF.Exp,
                     bias=a2f[:, NMH + j2:NMH + j2 + 1],
                     accum_out=sums[:, j2:j2 + 1]),
                 mark=("exA", h, qb), waits=wt)
            P.op(b, "scalar",
                 lambda e, h=h, qb=qb, j2=j2: e.activation(
                     Pt[qb % 2][:, 1024:2048], pscore[:, 1024:2048], AF.Exp,
                     bias=a2f[:, NMH + j2:NMH + j2 + 1],
                     accum_out=sums2[:, j2:j2 + 1]),
                 mark=("exB", h, qb), waits=[("smB", h, qb)])
        P.op(b, "scalar", lambda e: e.copy(invT[0:8, 208:336], pinvb[0:8, 0:128]),
             mark=("invTe", h), waits=[("invtr", h)])

    s3.close()

    s4 = ExitStack()
    cob = [sb(f"cob{i}", [128, 16, 128], BF16, s4) for i in range(2)]
    sob = [sb(f"sob{i}", [128, 16, 128], BF16, s4) for i in range(2)]
    wow = [sb(f"wow{i}", [128, 16, 128], BF16, s4) for i in range(2)]
    xqt = [sb(f"xqt{i}", [128, Q], F32, s4) for i in range(2)]
    t1t = [sb(f"t1t{i}", [128, Q], F32, s4) for i in range(2)]
    ybf = [sb(f"ybf{i}", [128, Q], BF16, s4) for i in range(2)]
    yb2 = sb("yb2", [128, Q], BF16, s4)
    poc = psum("poc", [128, 2, 2, 512], s4)
    pst1 = psum("pst1", [128, 512], s4)
    pst2 = psum("pst2", [128, 512], s4)

    b = P.block()
    for doc in range(16):
        P.op(b, "gpsimd",
             lambda e, doc=doc: e.dma_start(
                 cob[doc % 2].ap(),
                 wo[:, doc * 128:(doc + 1) * 128].rearrange("(fc p) f -> p fc f", p=128)),
             sem=f"oW{doc % 2}", k=16, waits=[("wob", doc - 2)] if doc >= 2 else [])
        P.op(b, "gpsimd",
             lambda e, doc=doc: e.dma_start(
                 sob[doc % 2].ap(),
                 so[:, doc * 128:(doc + 1) * 128].rearrange("(fc p) f -> p fc f", p=128)),
             sem=f"oW{doc % 2}", k=16, mark=("wod", doc))
        P.op(b, "sync",
             lambda e, doc=doc: e.dma_start(xqt[doc % 2][:, :],
                                            xq[doc * 128:(doc + 1) * 128, :]),
             sem=f"xQ{doc % 2}", k=16, mark=("xqd", doc),
             waits=[("yadd", doc - 2)] if doc >= 2 else [])
    def emit_wob(doc):
        P.op(b, "vector",
             lambda e, doc=doc: e.tensor_tensor(
                 wow[doc % 2].ap().rearrange("p a c -> p (a c)"),
                 cob[doc % 2].ap().rearrange("p a c -> p (a c)"),
                 sob[doc % 2].ap().rearrange("p a c -> p (a c)"), ALU.mult),
             mark=("wob", doc),
             waits=[("wod", doc)] + ([("omm", doc - 2)] if doc >= 2 else []))

    emit_wob(0)
    emit_wob(1)
    for doc in range(16):
        def ommf(e, doc=doc):
            ins = None
            for qh in range(2):
                for fc in range(16):
                    ins = e.matmul(poc[:, doc % 2, qh, :], wow[doc % 2][:, fc, :],
                                   attn[:, fc, qh * 512:(qh + 1) * 512],
                                   start=(fc == 0), stop=(fc == 15))
            return ins
        P.op(b, "tensor", ommf, mark=("omm", doc),
             waits=[("wob", doc)] + ([("oev", doc - 2)] if doc >= 2 else []))
        if doc >= 1:
            dp = doc - 1

            def stf(e, dp=dp):
                ins = None
                for qb in range(8):
                    ins = e.matmul(pst1[:, qb:qb + 1],
                                   ybf[dp % 2][:, qb * 128:(qb + 1) * 128],
                                   ones_bf[:, :], start=(dp == 0 and qb == 0), stop=False)
                    ins = e.matmul(pst2[:, qb:qb + 1],
                                   yb2[:, qb * 128:(qb + 1) * 128],
                                   ones_bf[:, :], start=(dp == 0 and qb == 0), stop=False)
                return ins
            P.op(b, "tensor", stf, mark=("stmm", dp), waits=[("yb2", dp)])

    def stfl(e):
        ins = None
        for qb in range(8):
            ins = e.matmul(pst1[:, qb:qb + 1], ybf[15 % 2][:, qb * 128:(qb + 1) * 128],
                           ones_bf[:, :], start=False, stop=(qb == 7))
            ins = e.matmul(pst2[:, qb:qb + 1], yb2[:, qb * 128:(qb + 1) * 128],
                           ones_bf[:, :], start=False, stop=(qb == 7))
        return ins
    P.op(b, "tensor", stfl, mark=("stmm", 15), waits=[("yb2", 15)])

    for doc in range(16):
        P.op(b, "scalar",
             lambda e, doc=doc: e.activation(
                 t1t[doc % 2][:, :],
                 poc.ap()[:, doc % 2, :, :].rearrange("p a c -> p (a c)"),
                 AF.Identity, bias=bo_t[:, doc:doc + 1]),
             mark=("oev", doc),
             waits=[("omm", doc)] + ([("yadd", doc - 2)] if doc >= 2 else []))
    for doc in range(16):
        P.op(b, "vector",
             lambda e, doc=doc: e.tensor_tensor(y[:, doc, :], t1t[doc % 2][:, :],
                                                xqt[doc % 2][:, :], ALU.add),
             mark=("yadd", doc), waits=[("oev", doc), ("xqd", doc)])
        P.op(b, "vector",
             lambda e, doc=doc: e.tensor_copy(ybf[doc % 2][:, :], y[:, doc, :]),
             mark=("ybf", doc),
             waits=[("stmm", doc - 2)] if doc >= 2 else [])
        P.op(b, "vector",
             lambda e, doc=doc: e.tensor_tensor(yb2[:, :], ybf[doc % 2][:, :],
                                                ybf[doc % 2][:, :], ALU.mult),
             mark=("yb2", doc),
             waits=[("stmm", doc - 1)] if doc >= 1 else [])
        if doc == 0:
            P.op(b, "vector", lambda e: e.tensor_copy(rmax[:, :], ybf[0][:, :]))
            P.op(b, "vector", lambda e: e.tensor_copy(rmin[:, :], ybf[0][:, :]),
                 mark=("tree", 0))
        else:
            P.op(b, "vector",
                 lambda e, doc=doc: e.tensor_tensor(rmax[:, :], rmax[:, :],
                                                    ybf[doc % 2][:, :], ALU.max))
            P.op(b, "vector",
                 lambda e, doc=doc: e.tensor_tensor(rmin[:, :], rmin[:, :],
                                                    ybf[doc % 2][:, :], ALU.min),
                 mark=("tree", doc))
        if doc + 2 < 16:
            emit_wob(doc + 2)

    P.op(b, "scalar", lambda e: e.copy(stt["sums1"][:, :], pst1[:, 0:8]),
         waits=[("stmm", 15)])
    P.op(b, "scalar", lambda e: e.copy(stt["sumsq1"][:, :], pst2[:, 0:8]),
         mark="ln1ev")

    s4.close()

    s5 = ExitStack()
    txn = [sb(f"txn{i}", [128, Q], F32, s5) for i in range(2)]
    ptr_ = psum("ptr_", [128, 1024], s5)
    pbc = psum("pbc", [128, 1024], s5)

    b = P.block()

    def ln_chain(nfeat, mu_t, R_t, composed, pre, wait0):
        P.op(b, "vector", lambda e, mu_t=mu_t: e.tensor_scalar(
            mu_t[:, :], stt["sums1"][:, :], 1.0 / nfeat, None, ALU.mult),
            waits=[wait0])
        P.op(b, "vector", lambda e: e.tensor_scalar(
            stt["tmp1"][:, :], stt["sumsq1"][:, :], 1.0 / nfeat, None, ALU.mult))
        P.op(b, "vector", lambda e, mu_t=mu_t: e.tensor_tensor(
            stt["tmp2"][:, :], mu_t[:, :], mu_t[:, :], ALU.mult))
        P.op(b, "vector", lambda e: e.tensor_tensor(
            stt["var"][:, :], stt["tmp1"][:, :], stt["tmp2"][:, :], ALU.subtract))
        if composed:
            P.op(b, "vector", lambda e: e.tensor_scalar(
                stt["tmp1"][:, :], stt["var"][:, :], 1.0 + EPS2, EPS1 * EPS2,
                ALU.mult, ALU.add), mark=pre + "a")
        else:
            eps = EPS2 if nfeat == FF else EPS1
            P.op(b, "vector", lambda e, eps=eps: e.tensor_scalar(
                stt["tmp1"][:, :], stt["var"][:, :], eps, None, ALU.add),
                mark=pre + "a")
        P.op(b, "scalar", lambda e: e.activation(stt["tmp2"][:, :], stt["tmp1"][:, :],
                                                 AF.Sqrt),
             mark=pre + "sq", waits=[pre + "a"])
        P.op(b, "vector", lambda e, R_t=R_t: e.reciprocal(R_t[:, :], stt["tmp2"][:, :]),
             mark=pre + "R", waits=[pre + "sq"])

    ln_chain(D, stt["mu"], stt["R"], True, "ln1", "ln1ev")
    P.op(b, "vector", lambda e: e.tensor_scalar(stt["tmp3"][:, :], stt["var"][:, :],
                                                EPS1, None, ALU.add),
         mark="ln1add", waits=["ln1R"])
    P.op(b, "scalar", lambda e: e.activation(stt["tmp3"][:, :], stt["tmp3"][:, :],
                                             AF.Sqrt), mark="ln1sA", waits=["ln1add"])
    P.op(b, "vector", lambda e: e.reciprocal(stt["tmp1"][:, :], stt["tmp3"][:, :]),
         waits=["ln1sA"])
    P.op(b, "vector", lambda e: e.tensor_tensor(stt["rho"][:, :], stt["tmp2"][:, :],
                                                stt["tmp1"][:, :], ALU.mult),
         mark="rho")

    def trred(src, dst_t, op, mk, wt, pp):
        ppb = pp.bitcast(BF16)
        def trf(e, src=src, ppb=ppb):
            ins = None
            for j in range(8):
                ins = e.transpose(ppb[:, j * 128:(j + 1) * 128],
                                  src[:, j * 128:(j + 1) * 128], id128b[:, :])
            return ins
        P.op(b, "tensor", trf, mark=mk + "t", waits=wt)

        def rf(e, dst_t=dst_t, op=op, ppb=ppb):
            ins = None
            for j in range(8):
                ins = e.tensor_reduce(dst_t[:, j:j + 1], ppb[:, j * 128:(j + 1) * 128],
                                      AX.X, op)
            return ins
        P.op(b, "vector", rf, mark=mk + "r", waits=[mk + "t"])

    trred(rmax, stt["mx"], ALU.max, "mx1", [("tree", 15)], ptr_)
    trred(rmin, stt["mn"], ALU.min, "mn1", ["mx1r"], ptr_)

    P.op(b, "vector", lambda e: e.tensor_tensor(stt["tmp1"][:, :], stt["mx"][:, :],
                                                stt["mu"][:, :], ALU.subtract),
         waits=["mn1r", "rho"])
    P.op(b, "vector", lambda e: e.tensor_tensor(stt["tmp2"][:, :], stt["mu"][:, :],
                                                stt["mn"][:, :], ALU.subtract))
    P.op(b, "vector", lambda e: e.tensor_tensor(stt["tmp1"][:, :], stt["tmp1"][:, :],
                                                stt["tmp2"][:, :], ALU.max))
    P.op(b, "vector", lambda e: e.tensor_tensor(stt["gx"][:, :], stt["tmp1"][:, :],
                                                stt["R"][:, :], ALU.mult))
    P.op(b, "vector", lambda e: e.tensor_scalar(stt["cx"][:, :], stt["gx"][:, :],
                                                CLIPB, None, ALU.mult))
    P.op(b, "vector", lambda e: e.tensor_scalar(stt["ncx"][:, :], stt["gx"][:, :],
                                                -CLIPB, None, ALU.mult), mark="cx1")

    def bcast(stat, dstB, key, wkey, pp):
        P.op(b, "tensor", lambda e, stat=stat: e.transpose(pp[0:8, 0:128],
                                                           stat[:, :], id128[:, :]),
             mark=key + "t", waits=[wkey])
        P.op(b, "scalar", lambda e: e.copy(trT[:, :], pp[0:8, 0:128]),
             mark=key + "e", waits=[key + "t"])

        def bmmf(e):
            ins = None
            for qb in range(8):
                ins = e.matmul(pp[:, qb * 128:(qb + 1) * 128],
                               sel8[:, qb * 128:(qb + 1) * 128], trT[:, :],
                               start=True, stop=True)
            return ins
        P.op(b, "tensor", bmmf, mark=key + "m", waits=[key + "e"])
        P.op(b, "vector", lambda e, dstB=dstB: e.tensor_copy(dstB[:, :], pp[:, :]),
             mark=key + "b", waits=[key + "m"])

    bcast(stt["mu"], muB, "bmu", "cx1", pbc)
    bcast(stt["R"], RB, "bR", "bmub", pbc)
    bcast(stt["cx"], cB, "bcx", "bRb", pbc)
    bcast(stt["ncx"], ncB, "bncx", "bcxb", pbc)
    bcast(stt["rho"], rhoB, "brho", "bncxb", pbc)

    for doc in range(16):
        P.op(b, "vector",
             lambda e, doc=doc: e.tensor_tensor(txn[doc % 2][:, :], y[:, doc, :],
                                                muB[:, :], ALU.subtract),
             waits=(["brhob"] if doc == 0 else []))
        P.op(b, "vector",
             lambda e, doc=doc: e.tensor_tensor(y[:, doc, :], txn[doc % 2][:, :],
                                                RB[:, :], ALU.mult),
             mark=("xn", doc))
        P.op(b, "vector",
             lambda e, doc=doc: e.tensor_tensor(txn[doc % 2][:, :], y[:, doc, :],
                                                cB[:, :], ALU.min))
        P.op(b, "vector",
             lambda e, doc=doc: e.tensor_tensor(xh[:, doc, :], txn[doc % 2][:, :],
                                                ncB[:, :], ALU.max),
             mark=("xhat", doc))
        P.op(b, "sync",
             lambda e, doc=doc: e.dma_start(xnspill[doc * 128:(doc + 1) * 128, :],
                                            y[:, doc, :]),
             sem="dX", k=16, mark=("xnsp", doc), waits=[("xn", doc)])

    s5.close()

    s6 = ExitStack()
    wfw = [arena1[:, i * 2048:(i + 1) * 2048].rearrange("p (a c) -> p a c", a=16)
           for i in range(4)]
    wps = [arena1[:, 8192 + i * 2048:8192 + (i + 1) * 2048] for i in range(4)]
    pf = sb("pf", [128, 2048], BF16, s6)
    nf = sb("nf", [128, 2048], BF16, s6)
    tf = [sb(f"tf{i}", [128, 16, 128], BF16, s6) for i in range(2)]
    pa = sb("pa", [128, 2048], BF16, s6)
    pb_ = sb("pb_", [128, 2048], BF16, s6)
    tp8 = [sb(f"tp8{i}", [128, 2048], FP8, s6) for i in range(2)]
    hsl = [sb(f"hsl{i}", [128, Q], BF16, s6) for i in range(2)]
    h2t = sb("h2t", [128, Q], BF16, s6)
    ph = psum("ph", [128, 2, 1024], s6)
    psh1 = psum("psh1", [128, 512], s6)
    psh2 = psum("psh2", [128, 512], s6)
    pg6 = psum("pg6", [128, 8], s6)

    b = P.block()
    P.op(b, "vector", lambda e: e.reduce_sum(gacc_f[:, 0:1], gacc_f[:, :], axis=AX.X),
         waits=[("gtile", 127)])
    P.op(b, "vector", lambda e: e.reduce_sum(gacc_p[:, 0:1], gacc_p[:, :], axis=AX.X),
         mark="gred")
    P.op(b, "tensor", lambda e: e.matmul(pg6[0:1, 0:1], gacc_f[:, 0:1], ones_f[:, :],
                                         start=True, stop=False), waits=["gred"])
    P.op(b, "tensor", lambda e: e.matmul(pg6[0:1, 1:2], gacc_p[:, 0:1], ones_f[:, :],
                                         start=False, stop=True), mark="gsum")
    P.op(b, "scalar", lambda e: e.copy(g1[0:1, 0:2], pg6[0:1, 0:2]),
         mark="g1ev", waits=["gsum"])
    P.op(b, "vector", lambda e: e.tensor_scalar(g1[0:1, 2:4], g1[0:1, 0:2],
                                                1.0 / NW, None, ALU.mult),
         waits=["g1ev"])
    P.op(b, "vector", lambda e: e.tensor_scalar(g1[0:1, 4:6], g1[0:1, 2:4],
                                                1e-6, 0.5, ALU.add, ALU.mult))
    P.op(b, "vector", lambda e: e.tensor_scalar(g1[0:1, 6:8], g1[0:1, 4:6],
                                                -1.0, None, ALU.mult))
    P.op(b, "vector", lambda e: e.tensor_scalar(g1[0:1, 8:10], g1[0:1, 2:4],
                                                0.5, None, ALU.mult), mark="gscal")
    gcols = [(4, thf), (5, thp), (6, nthf), (7, nthp), (2, gfb), (9, gpb)]
    for i, (col, dst) in enumerate(gcols):
        P.op(b, "tensor",
             lambda e, i=i, col=col: e.matmul(pg6[:, 2 + i:3 + i], onesr_f[:, :],
                                              g1[0:1, col:col + 1], start=True, stop=True),
             mark=("gbc", i), waits=(["gscal"] if i == 0 else []))
    for i, (col, dst) in enumerate(gcols):
        P.op(b, "scalar", lambda e, i=i, dst=dst: e.copy(dst[:, :], pg6[:, 2 + i:3 + i]),
             mark=("gbcev", i), waits=([("gbc", 5)] if i == 0 else []))

    def emit_wfd(oc):
        P.op(b, "sync",
             lambda e, oc=oc: e.dma_start(
                 wfw[oc % 4],
                 wfc[:, oc * 128:(oc + 1) * 128].rearrange("(dc p) f -> p dc f", p=128)),
             sem=f"fW{oc % 4}", k=16, mark=("wfd", oc),
             waits=([("nfc", oc - 4)] if oc >= 4 else []))

    def emit_trio(oc):
        P.op(b, "vector",
             lambda e, oc=oc: e.tensor_scalar(
                 pf[:, :], wfw[oc % 4].rearrange("p a c -> p (a c)"),
                 thf[:, :], None, ALU.is_gt),
             mark=("pfc", oc),
             waits=[("wfd", oc)] + ([("fmm", oc - 2)] if oc >= 2 else [("gbcev", 5)]))
        P.op(b, "vector",
             lambda e, oc=oc: e.tensor_scalar(
                 nf[:, :], wfw[oc % 4].rearrange("p a c -> p (a c)"),
                 nthf[:, :], None, ALU.is_lt),
             mark=("nfc", oc))
        P.op(b, "vector",
             lambda e, oc=oc: e.tensor_tensor(
                 tf[oc % 2].ap().rearrange("p a c -> p (a c)"),
                 pf[:, :], nf[:, :], ALU.subtract),
             mark=("tfc", oc))

    def emit_slow(j):
        P.op(b, "vector",
             lambda e, j=j: e.tensor_tensor(tp8[j % 2][:, :], pa[:, :], pb_[:, :],
                                            ALU.add),
             mark=("tpc", j),
             waits=[("sgb", j)] + ([("tpw", j - 2)] if j >= 2 else []))
        if j >= 1:
            P.op(b, "vector",
                 lambda e, j=j: e.tensor_tensor(h2t[:, :], hsl[(j - 1) % 2][:, :],
                                                hsl[(j - 1) % 2][:, :], ALU.mult),
                 mark=("h2", j - 1),
                 waits=[("hev", j - 1)] + ([("hst", j - 2)] if j >= 2 else []))
        if j == 0:
            P.op(b, "vector", lambda e: e.tensor_copy(rmax[:, :], hsl[0][:, :]),
                 waits=[("hev", 0)])
            P.op(b, "vector", lambda e: e.tensor_copy(rmin[:, :], hsl[0][:, :]),
                 mark=("htree", 0))
        else:
            P.op(b, "vector",
                 lambda e, j=j: e.tensor_tensor(rmax[:, :], rmax[:, :],
                                                hsl[j % 2][:, :], ALU.max),
                 waits=[("hev", j)])
            P.op(b, "vector",
                 lambda e, j=j: e.tensor_tensor(rmin[:, :], rmin[:, :],
                                                hsl[j % 2][:, :], ALU.min),
                 mark=("htree", j))

    emit_wfd(0)
    emit_wfd(1)
    for oc in range(64):
        if oc + 2 < 64:
            emit_wfd(oc + 2)
        P.op(b, "gpsimd",
             lambda e, oc=oc: e.dma_start(wps[oc % 4][:, :],
                                          wpr[oc * 128:(oc + 1) * 128, :]),
             sem=f"pW{oc % 4}", k=16, mark=("wpd", oc),
             waits=[("sgb", oc - 4)] if oc >= 4 else [])
        emit_trio(oc)
        if oc >= 1:
            emit_slow(oc - 1)
        P.op(b, "scalar",
             lambda e, oc=oc: e.activation(pa[:, :], wps[oc % 4][:, :], AF.Sign,
                                           bias=nthp[:, :]),
             waits=[("wpd", oc)] + ([("tpc", oc - 1)] if oc >= 1 else [("gbcev", 5)]))
        P.op(b, "scalar",
             lambda e, oc=oc: e.activation(pb_[:, :], wps[oc % 4][:, :], AF.Sign,
                                           bias=thp[:, :]),
             mark=("sgb", oc))

        def fmmf(e, oc=oc):
            ins = None
            for qh in range(2):
                for dc in range(16):
                    ins = e.matmul(ph[:, oc % 2, qh * 512:(qh + 1) * 512],
                                   tf[oc % 2][:, dc, :],
                                   xh[:, dc, qh * 512:(qh + 1) * 512],
                                   start=(dc == 0), stop=(dc == 15))
            return ins
        P.op(b, "tensor", fmmf, mark=("fmm", oc),
             waits=[("tfc", oc)] + ([("hev", oc - 2)] if oc >= 2 else []))
        if oc >= 1:
            def hstf(e, op=oc - 1):
                ins = None
                for qb in range(8):
                    ins = e.matmul(psh1[:, qb:qb + 1],
                                   hsl[op % 2][:, qb * 128:(qb + 1) * 128],
                                   ones_bf[:, :], start=(op == 0 and qb == 0), stop=False)
                    ins = e.matmul(psh2[:, qb:qb + 1],
                                   h2t[:, qb * 128:(qb + 1) * 128],
                                   ones_bf[:, :], start=(op == 0 and qb == 0), stop=False)
                return ins
            P.op(b, "tensor", hstf, mark=("hst", oc - 1), waits=[("h2", oc - 1)])
        P.op(b, "scalar",
             lambda e, oc=oc: e.activation(hsl[oc % 2][:, :],
                                           ph.ap()[:, oc % 2, :], AF.Gelu,
                                           scale=gfb[:, :]),
             mark=("hev", oc),
             waits=[("fmm", oc)] + ([("hsp", oc - 2), ("hst", oc - 2),
                                     ("htree", oc - 2)] if oc >= 2 else []))
        P.op(b, "sync",
             lambda e, oc=oc: e.dma_start(hspill[oc * 128:(oc + 1) * 128, :],
                                          hsl[oc % 2][:, :]),
             sem=f"hW{oc % 2}", k=16, mark=("hsp", oc), waits=[("hev", oc)])
        if oc >= 1:
            P.op(b, "sync",
                 lambda e, oc=oc: e.dma_start(tproj[(oc - 1) * 128:oc * 128, :],
                                              tp8[(oc - 1) % 2][:, :]),
                 sem=f"tW{(oc - 1) % 2}", k=16, mark=("tpw", oc - 1),
                 waits=[("tpc", oc - 1)])
    emit_slow(63)
    P.op(b, "sync",
         lambda e: e.dma_start(tproj[63 * 128:64 * 128, :], tp8[63 % 2][:, :]),
         sem=f"tW{63 % 2}", k=16, mark=("tpw", 63), waits=[("tpc", 63)])
    P.op(b, "vector",
         lambda e: e.tensor_tensor(h2t[:, :], hsl[63 % 2][:, :], hsl[63 % 2][:, :],
                                   ALU.mult),
         mark=("h2", 63), waits=[("hev", 63), ("hst", 62)])

    def hstl(e):
        ins = None
        for qb in range(8):
            ins = e.matmul(psh1[:, qb:qb + 1], hsl[63 % 2][:, qb * 128:(qb + 1) * 128],
                           ones_bf[:, :], start=False, stop=(qb == 7))
            ins = e.matmul(psh2[:, qb:qb + 1], h2t[:, qb * 128:(qb + 1) * 128],
                           ones_bf[:, :], start=False, stop=(qb == 7))
        return ins
    P.op(b, "tensor", hstl, mark=("hst", 63), waits=[("h2", 63)])
    P.op(b, "scalar", lambda e: e.copy(stt["sums1"][:, :], psh1[:, 0:8]),
         waits=[("hst", 63)])
    P.op(b, "scalar", lambda e: e.copy(stt["sumsq1"][:, :], psh2[:, 0:8]),
         mark="ln2ev")

    s6.close()

    s7 = ExitStack()
    hrb = [sb(f"hrb{i}", [128, 2, Q], BF16, s7) for i in range(2)]
    th1 = [sb(f"th1{i}", [128, 2, Q], F32, s7) for i in range(1)]
    tpb = [sb(f"tpb{i}", [128, 8, 256], BF16, s7) for i in range(2)]
    xnr = sb("xnr", [128, Q], F32, s7)
    t2t = sb("t2t", [128, Q], F32, s7)
    y2t = [sb(f"y2t{i}", [128, Q], F32, s7) for i in range(2)]
    y2b = sb("y2b", [128, Q], BF16, s7)
    y2b2 = sb("y2b2", [128, Q], BF16, s7)
    py = psum("py", [128, 2, 2, 512], s7)
    psy1 = psum("psy1", [128, 512], s7)
    psy2 = psum("psy2", [128, 512], s7)
    pbc7 = psum("pbc7", [128, 1024], s7)

    b = P.block()
    ln_chain(FF, stt["mu2"], stt["R2"], False, "ln2", "ln2ev")
    trred(rmax, stt["mx"], ALU.max, "mx2", [("htree", 63)], pbc7)
    trred(rmin, stt["mn"], ALU.min, "mn2", ["mx2r"], pbc7)
    P.op(b, "vector", lambda e: e.tensor_tensor(stt["tmp1"][:, :], stt["mx"][:, :],
                                                stt["mu2"][:, :], ALU.subtract),
         waits=["mn2r", "ln2R"])
    P.op(b, "vector", lambda e: e.tensor_tensor(stt["tmp2"][:, :], stt["mu2"][:, :],
                                                stt["mn"][:, :], ALU.subtract))
    P.op(b, "vector", lambda e: e.tensor_tensor(stt["tmp1"][:, :], stt["tmp1"][:, :],
                                                stt["tmp2"][:, :], ALU.max))
    P.op(b, "vector", lambda e: e.tensor_tensor(stt["gx"][:, :], stt["tmp1"][:, :],
                                                stt["R2"][:, :], ALU.mult))
    P.op(b, "vector", lambda e: e.tensor_scalar(stt["c2"][:, :], stt["gx"][:, :],
                                                CLIPB, None, ALU.mult))
    P.op(b, "vector", lambda e: e.tensor_scalar(stt["nc2"][:, :], stt["gx"][:, :],
                                                -CLIPB, None, ALU.mult), mark="cx2")
    bcast(stt["mu2"], muB, "cmu", "cx2", pbc7)
    bcast(stt["R2"], RB, "cR", "cmub", pbc7)
    bcast(stt["c2"], cB, "ccx", "cRb", pbc7)
    bcast(stt["nc2"], ncB, "cncx", "ccxb", pbc7)

    def emit_tpr(dw, ocg):
        if ocg >= 2:
            wt = [("pmm", dw, ocg - 2)]
        elif dw >= 1:
            wt = [("pmm", dw - 1, 6 + ocg)]
        else:
            wt = [("tpw", 63)]
        P.op(b, "gpsimd",
             lambda e, dw=dw, ocg=ocg: e.dma_start(
                 tpb[ocg % 2].ap(),
                 tproj[ocg * 1024:(ocg + 1) * 1024, dw * 256:(dw + 1) * 256]
                 .rearrange("(g p) d -> p g d", p=128)),
             sem=f"tR{ocg % 2}", k=16, mark=("tpr", dw, ocg), waits=wt)

    BC = lambda t: t.ap().unsqueeze(1).broadcast_to([128, 2, Q])
    for tl in range(32):
        oc0 = tl * 2
        P.op(b, "sync",
             lambda e, tl=tl, oc0=oc0: e.dma_start(
                 hrb[tl % 2].ap(),
                 hspill[oc0 * 128:(oc0 + 2) * 128, :]
                 .rearrange("(a p) c -> p a c", p=128)),
             sem=f"hR{tl % 2}", k=16, mark=("hrd", tl),
             waits=([("x2a", tl - 2)] if tl >= 2 else []))
        buf = th1[0]
        P.op(b, "vector",
             lambda e, tl=tl, buf=buf: e.tensor_tensor(
                 buf[:, :, :], hrb[tl % 2][:, :, :], BC(muB), ALU.subtract),
             waits=[("hrd", tl), "cncxb"])
        P.op(b, "vector",
             lambda e, buf=buf: e.tensor_tensor(buf[:, :, :], buf[:, :, :],
                                                BC(RB), ALU.mult))
        P.op(b, "vector",
             lambda e, buf=buf: e.tensor_tensor(buf[:, :, :], buf[:, :, :],
                                                BC(cB), ALU.min))
        P.op(b, "vector",
             lambda e, tl=tl, oc0=oc0, buf=buf: e.tensor_tensor(
                 x2w(oc0), buf[:, :, :], BC(ncB), ALU.max),
             mark=("x2a", tl))
        if tl % 4 == 3:
            emit_tpr(0, tl // 4)
    for dw in range(8):
        for ocg in range(8):
            if dw >= 1:
                emit_tpr(dw, ocg)

            def pmmf(e, dw=dw, ocg=ocg):
                ins = None
                for gi in range(8):
                    oc = ocg * 8 + gi
                    for dj in range(2):
                        for qh in range(2):
                            ins = e.matmul(py[:, dj, qh, :],
                                           tpb[ocg % 2][:, gi, dj * 128:(dj + 1) * 128],
                                           x2sl(oc, qh * 512, (qh + 1) * 512),
                                           start=(oc == 0), stop=(oc == 63))
                return ins
            wt2 = [("tpr", dw, ocg), ("x2a", ocg * 4 + 3)]
            if ocg == 0 and dw >= 1:
                wt2.append(("y2e", dw - 1, 1))
            P.op(b, "tensor", pmmf, mark=("pmm", dw, ocg), waits=wt2)
        for dj in range(2):
            doc = dw * 2 + dj
            P.op(b, "scalar",
                 lambda e, dj=dj: e.activation(
                     t2t[:, :], py.ap()[:, dj, :, :].rearrange("p a c -> p (a c)"),
                     AF.Identity, scale=gpb[:, :]),
                 mark=("y2e", dw, dj),
                 waits=[("pmm", dw, 7)] + ([("y2add", doc - 1)] if doc >= 1 else []))
            P.op(b, "sync",
                 lambda e, doc=doc: e.dma_start(xnr[:, :],
                                                xnspill[doc * 128:(doc + 1) * 128, :]),
                 sem="xR", k=16, mark=("xnr", doc),
                 waits=[("y2add", doc - 1)] if doc >= 1 else [])
            P.op(b, "vector",
                 lambda e, doc=doc: e.tensor_tensor(y2t[doc % 2][:, :], xnr[:, :],
                                                    rhoB[:, :], ALU.mult),
                 waits=[("xnr", doc), ("y2e", dw, dj)] +
                       ([("y2sp", doc - 2)] if doc >= 2 else []))
            P.op(b, "vector",
                 lambda e, doc=doc: e.tensor_tensor(y2t[doc % 2][:, :],
                                                    y2t[doc % 2][:, :], t2t[:, :],
                                                    ALU.add),
                 mark=("y2add", doc))
            P.op(b, "vector",
                 lambda e, doc=doc: e.tensor_copy(y2b[:, :], y2t[doc % 2][:, :]),
                 waits=[("y2st", doc - 1)] if doc >= 1 else [])
            P.op(b, "vector",
                 lambda e: e.tensor_tensor(y2b2[:, :], y2b[:, :], y2b[:, :], ALU.mult),
                 mark=("y2b2", doc))
            if doc == 0:
                P.op(b, "vector", lambda e: e.tensor_copy(rmax[:, :], y2b[:, :]),
                     waits=["mn2t"])
                P.op(b, "vector", lambda e: e.tensor_copy(rmin[:, :], y2b[:, :]),
                     mark=("y2tree", 0))
            else:
                P.op(b, "vector",
                     lambda e: e.tensor_tensor(rmax[:, :], rmax[:, :], y2b[:, :],
                                               ALU.max))
                P.op(b, "vector",
                     lambda e: e.tensor_tensor(rmin[:, :], rmin[:, :], y2b[:, :],
                                               ALU.min),
                     mark=("y2tree", doc))

            def y2stf(e, doc=doc):
                ins = None
                for qb in range(8):
                    ins = e.matmul(psy1[:, qb:qb + 1], y2b[:, qb * 128:(qb + 1) * 128],
                                   ones_bf[:, :], start=(doc == 0 and qb == 0),
                                   stop=(doc == 15 and qb == 7))
                    ins = e.matmul(psy2[:, qb:qb + 1], y2b2[:, qb * 128:(qb + 1) * 128],
                                   ones_bf[:, :], start=(doc == 0 and qb == 0),
                                   stop=(doc == 15 and qb == 7))
                return ins
            P.op(b, "tensor", y2stf, mark=("y2st", doc), waits=[("y2b2", doc)])
            P.op(b, "sync",
                 lambda e, doc=doc: e.dma_start(y2spill[doc * 128:(doc + 1) * 128, :],
                                                y2t[doc % 2][:, :]),
                 sem=f"yW{doc % 2}", k=16, mark=("y2sp", doc), waits=[("y2add", doc)])

    P.op(b, "scalar", lambda e: e.copy(stt["sums1"][:, :], psy1[:, 0:8]),
         waits=[("y2st", 15)])
    P.op(b, "scalar", lambda e: e.copy(stt["sumsq1"][:, :], psy2[:, 0:8]),
         mark="ln3ev")

    s7.close()

    s9 = ExitStack()
    y2r = [sb(f"y2r{i}", [128, Q], F32, s9) for i in range(2)]
    to1 = [sb(f"to1{i}", [128, Q], F32, s9) for i in range(2)]
    pbc9 = psum("pbc9", [128, 1024], s9)

    b = P.block()
    ln_chain(D, stt["mu3"], stt["R3"], False, "ln3", "ln3ev")
    bcast(stt["mu3"], muB, "dmu", "ln3R", pbc9)
    bcast(stt["R3"], RB, "dR", "dmub", pbc9)

    for doc in range(16):
        P.op(b, "sync",
             lambda e, doc=doc: e.dma_start(y2r[doc % 2][:, :],
                                            y2spill[doc * 128:(doc + 1) * 128, :]),
             sem=f"yR{doc % 2}", k=16, mark=("y2r", doc),
             waits=[("outv", doc - 2)] if doc >= 2 else [])
        P.op(b, "vector",
             lambda e, doc=doc: e.tensor_tensor(to1[doc % 2][:, :], y2r[doc % 2][:, :],
                                                muB[:, :], ALU.subtract),
             waits=[("y2r", doc), "dRb"] + ([("outd", doc - 2)] if doc >= 2 else []))
        P.op(b, "vector",
             lambda e, doc=doc: e.tensor_tensor(to1[doc % 2][:, :], to1[doc % 2][:, :],
                                                RB[:, :], ALU.mult),
             mark=("outv", doc))
        P.op(b, "sync",
             lambda e, doc=doc: e.dma_start(out[doc * 128:(doc + 1) * 128, :],
                                            to1[doc % 2][:, :]),
             sem=f"oD{doc % 2}", k=16, mark=("outd", doc), waits=[("outv", doc)])
    P.op(b, "sync", lambda e: e.nop(), sem="dX", waits=[("outd", 14), ("outd", 15)])

    s9.close()
    P.resolve()
    P.emit(nc, sems)
    gs.close()
    return nc


_NC_CACHE = {}


def _get_nc():
    if "nc" not in _NC_CACHE:
        _NC_CACHE["nc"] = build_program()
    return _NC_CACHE["nc"]


def kernel(L0_attn_x, L0_attn_wq_codes, L0_attn_wq_scales, L0_attn_bq,
           L0_attn_wk_codes, L0_attn_wk_scales, L0_attn_bk,
           L0_attn_wv_codes, L0_attn_wv_scales, L0_attn_bv,
           L0_attn_wo_codes, L0_attn_wo_scales, L0_attn_bo,
           L0_ffn_w_fc, L0_ffn_w_proj):
    x = np.asarray(L0_attn_x, dtype=np.float32)
    B = x.shape[0]
    rep = lambda s: np.ascontiguousarray(np.repeat(np.asarray(s, np.float32), 32, axis=0))
    bias_t = lambda v: np.ascontiguousarray(np.asarray(v, np.float32).reshape(16, 128).T)
    common = {
        "wq": np.ascontiguousarray(np.asarray(L0_attn_wq_codes, np.float32)),
        "wk": np.ascontiguousarray(np.asarray(L0_attn_wk_codes, np.float32)),
        "wv": np.ascontiguousarray(np.asarray(L0_attn_wv_codes, np.float32)),
        "wo": np.ascontiguousarray(np.asarray(L0_attn_wo_codes, np.float32)),
        "sq": rep(L0_attn_wq_scales), "sk": rep(L0_attn_wk_scales),
        "sv": rep(L0_attn_wv_scales), "so": rep(L0_attn_wo_scales),
        "bq": bias_t(L0_attn_bq), "bk": bias_t(L0_attn_bk),
        "bv": bias_t(L0_attn_bv), "bo": bias_t(L0_attn_bo),
        "wfc": np.ascontiguousarray(np.asarray(L0_ffn_w_fc, np.float32).T),
        "wpr": np.ascontiguousarray(np.asarray(L0_ffn_w_proj, np.float32).T),
    }
    in_maps = []
    for c in range(8):
        bb, half = c // 2, c % 2
        xT = x[bb].T
        xr = np.ascontiguousarray(np.concatenate(
            [xT[:, half * Q:], xT[:, :half * Q]], axis=1))
        m = dict(common)
        m["xt"] = xr
        m["xq"] = np.ascontiguousarray(xr[:, :Q])
        in_maps.append(m)
    nc = _get_nc()
    res = run_bass_kernel_spmd(nc, in_maps, core_ids=list(range(8)))
    y_out = np.empty((B, S, D), dtype=np.float32)
    for c in range(8):
        bb, half = c // 2, c % 2
        y_out[bb, half * Q:(half + 1) * Q, :] = np.asarray(res.results[c]["out"]).T
    return y_out


# version 12
# speedup vs baseline: 1.0809x; 1.0809x over previous
import numpy as np
import ml_dtypes
from contextlib import ExitStack
import concourse.bass as bass
import concourse.mybir as mybir
from concourse.bass_utils import run_bass_kernel_spmd

F32 = mybir.dt.float32
BF16 = mybir.dt.bfloat16
FP8 = mybir.dt.float8e4
AF = mybir.ActivationFunctionType
ALU = mybir.AluOpType
AX = mybir.AxisListType

D = 2048
S = 2048
Q = 1024
H = 16
FF = 8192
NW = float(D * FF)
ISQ = float(1.0 / np.sqrt(128.0))
EPS1 = 1e-5
EPS2 = 1e-6
CLIPB = float((128.0 - 1e-6) / 128.0)

SB_TOP = 229344
QT_R = (SB_TOP - 32768, SB_TOP)
XH_R = (SB_TOP - 32768, SB_TOP)
AT_R = (SB_TOP - 65536, SB_TOP - 32768)
PT_R = [(SB_TOP - 131072 + i * 16384, SB_TOP - 131072 + (i + 1) * 16384)
        for i in range(4)]
Y_R = (SB_TOP - 131072, SB_TOP - 65536)
X2_R = (SB_TOP - 131072, SB_TOP)

ENGS = ["sync", "tensor", "vector", "scalar", "gpsimd"]
SEM_OF = {"sync": "dS", "tensor": "tsem", "vector": "vsem", "scalar": "ssem",
          "gpsimd": "gsem"}


class Prog:
    def __init__(self):
        self.blocks = []
        self.ev = {}

    def block(self):
        b = {e: [] for e in ENGS}
        self.blocks.append(b)
        return b

    def op(self, b, eng, fn, *, mark=None, sem=None, k=1, waits=()):
        b[eng].append((fn, mark, sem or SEM_OF[eng], k, tuple(waits)))

    def resolve(self):
        self.ev.clear()
        self.drains = {}
        self.selfw = {}
        self.blockstart = {}
        cnt = {}
        for bi, b in enumerate(self.blocks):
            self.blockstart[bi] = [(s, cnt.get(s, 0))
                                   for s in ("tsem", "vsem", "ssem", "gsem")
                                   if cnt.get(s, 0)]
            for eng in ENGS:
                seen = {}
                own = SEM_OF[eng]
                for oi, (fn, mark, sem, k, waits) in enumerate(b[eng]):
                    if eng in ("vector", "scalar", "gpsimd"):
                        self.selfw[(bi, eng, oi)] = cnt.get(own, 0)
                    cnt[sem] = cnt.get(sem, 0) + k
                    if mark is not None:
                        assert mark not in self.ev, f"duplicate mark {mark}"
                        self.ev[mark] = (sem, cnt[sem])
                    if sem not in ("tsem", "vsem", "ssem", "gsem"):
                        seen[sem] = cnt[sem]
                if seen:
                    self.drains[(bi, eng)] = sorted(seen.items())

    def emit(self, nc, sems):
        for bi, b in enumerate(self.blocks):
            with nc.Block() as blk:
                for eng in ENGS:
                    ops = b[eng]
                    if not ops:
                        continue
                    drains = self.drains.get((bi, eng), [])

                    def body(e, ops=ops, drains=drains, bi=bi, eng=eng):
                        own = sems.get(SEM_OF[eng])
                        for s_, v_ in self.blockstart.get(bi, []):
                            e.wait_ge(sems[s_], v_)
                        for oi, (fn, mark, sem, k, waits) in enumerate(ops):
                            sv = self.selfw.get((bi, eng, oi))
                            if sv:
                                e.wait_ge(own, sv)
                            for w in waits:
                                if w in self.ev:
                                    ws, wc = self.ev[w]
                                    e.wait_ge(sems[ws], wc)
                            ins = fn(e)
                            assert ins is not None
                            ins.then_inc(sems[sem], k)
                        for s_, v_ in drains:
                            e.wait_ge(sems[s_], v_)
                    getattr(blk, eng)(body)


def build_program(debug=False):
    nc = bass.Bass("TRN2", target_bir_lowering=False)
    gs = ExitStack()
    P = Prog()

    xt = nc.dram_tensor("xt", [D, S], F32, kind="ExternalInput")
    xq = nc.dram_tensor("xq", [D, Q], F32, kind="ExternalInput")
    wq = nc.dram_tensor("wq", [D, D], F32, kind="ExternalInput")
    wk = nc.dram_tensor("wk", [D, D], F32, kind="ExternalInput")
    wv = nc.dram_tensor("wv", [D, D], F32, kind="ExternalInput")
    wo = nc.dram_tensor("wo", [D, D], F32, kind="ExternalInput")
    sq = nc.dram_tensor("sq", [D, D], F32, kind="ExternalInput")
    sk = nc.dram_tensor("sk", [D, D], F32, kind="ExternalInput")
    sv = nc.dram_tensor("sv", [D, D], F32, kind="ExternalInput")
    so = nc.dram_tensor("so", [D, D], F32, kind="ExternalInput")
    bqd = nc.dram_tensor("bq", [128, 16], F32, kind="ExternalInput")
    bkd = nc.dram_tensor("bk", [128, 16], F32, kind="ExternalInput")
    bvd = nc.dram_tensor("bv", [128, 16], F32, kind="ExternalInput")
    bod = nc.dram_tensor("bo", [128, 16], F32, kind="ExternalInput")
    wfc = nc.dram_tensor("wfc", [D, FF], F32, kind="ExternalInput")
    wpr = nc.dram_tensor("wpr", [FF, D], F32, kind="ExternalInput")
    out = nc.dram_tensor("out", [D, Q], F32, kind="ExternalOutput")

    kd = {"kind": "ExternalOutput"} if debug else {}
    kspill = nc.dram_tensor("kspill", [D, S], BF16, **kd)
    vspill = nc.dram_tensor("vspill", [H, S, 128], BF16, **kd)
    tproj = nc.dram_tensor("tproj", [FF, D], FP8, **kd)
    hspill = nc.dram_tensor("hspill", [FF, Q], BF16, **kd)
    xnspill = nc.dram_tensor("xnspill", [D, Q], F32, **kd)
    y2spill = nc.dram_tensor("y2spill", [D, Q], F32, **kd)

    semnames = (["tsem", "vsem", "ssem", "gsem", "ost", "dX", "gX", "kSp", "xR"]
                + [f"{p}{i}" for p in ["gW", "vT", "vP", "kR", "vR", "tP",
                                       "xQ", "oW", "hW", "tW", "hR",
                                       "tR", "yW", "yR", "oD"] for i in range(2)]
                + [f"{p}{i}" for p in ["fW", "pW", "gG", "gS"] for i in range(4)])
    sems = {n: gs.enter_context(nc.semaphore(n)) for n in semnames}

    def sb(name, shape, dt, scope, **kw):
        return scope.enter_context(nc.sbuf_tensor(name, shape, dt, **kw))

    def psum(name, shape, scope):
        return scope.enter_context(nc.psum_tensor(name, shape, F32))

    ones_bf = sb("ones_bf", [128, 1], BF16, gs)
    ones_f = sb("ones_f", [128, 1], F32, gs)
    onesr_f = sb("onesr_f", [1, 128], F32, gs)
    id128 = sb("id128", [128, 128], F32, gs)
    id128b = sb("id128b", [128, 128], BF16, gs)
    sel8 = sb("sel8", [8, 1024], F32, gs)
    bq_s = sb("bq_s", [128, 16], F32, gs)
    bk_t = sb("bk_t", [128, 16], F32, gs)
    bv_t = sb("bv_t", [128, 16], F32, gs)
    bo_t = sb("bo_t", [128, 16], F32, gs)
    thf = sb("thf", [128, 1], F32, gs)
    nthf = sb("nthf", [128, 1], F32, gs)
    thp = sb("thp", [128, 1], F32, gs)
    nthp = sb("nthp", [128, 1], F32, gs)
    gfb = sb("gfb", [128, 1], F32, gs)
    gpb = sb("gpb", [128, 1], F32, gs)
    gacc_f = sb("gacc_f", [128, 64], F32, gs)
    gacc_p = sb("gacc_p", [128, 64], F32, gs)
    g1 = sb("g1", [1, 16], F32, gs)
    sums = None
    stn = ["sums1", "sumsq1", "mu", "var", "R", "rho", "gx", "cx", "ncx",
           "mu2", "R2", "c2", "nc2", "mu3", "R3", "tmp1", "tmp2", "tmp3",
           "mx", "mn"]
    stt = {n: sb("st_" + n, [128, 8], F32, gs) for n in stn}
    muB = sb("muB", [128, Q], F32, gs)
    RB = sb("RB", [128, Q], F32, gs)
    cB = sb("cB", [128, Q], F32, gs)
    ncB = sb("ncB", [128, Q], F32, gs)
    rhoB = sb("rhoB", [128, Q], F32, gs)
    rmax = sb("rmax", [128, Q], BF16, gs)
    rmin = sb("rmin", [128, Q], BF16, gs)
    rminf = rmin.bitcast(F32)
    sums = rminf[:, 336:464]
    trT = sb("trT", [8, 128], F32, gs)
    arena1 = sb("arena1", [128, 16384], F32, gs)
    arena2 = sb("arena2", [128, 16384], BF16, gs)
    arena3 = sb("arena3", [128, 16384], BF16, gs)
    a1b = arena1.bitcast(BF16)
    a2f = arena2.bitcast(F32)
    Qt = arena3.ap().rearrange("p (a c) -> p a c", a=16)
    attn = arena2.ap().rearrange("p (a c) -> p a c", a=16)
    PT = [a1b[:, i * 8192:(i + 1) * 8192].rearrange("p (kc i) -> p kc i", kc=16)
          for i in range(4)]
    y = arena1.ap().rearrange("p (a c) -> p a c", a=16)
    xh = arena3.ap().rearrange("p (a c) -> p a c", a=16)
    x2parts = [a1b.rearrange("p (a c) -> p a c", a=32),
               arena2.ap().rearrange("p (a c) -> p a c", a=16),
               arena3.ap().rearrange("p (a c) -> p a c", a=16)]

    def x2w(oc0):
        if oc0 < 32:
            return x2parts[0][:, oc0:oc0 + 2, :]
        if oc0 < 48:
            return x2parts[1][:, oc0 - 32:oc0 - 30, :]
        return x2parts[2][:, oc0 - 48:oc0 - 46, :]

    def x2sl(oc, c0, c1):
        if oc < 32:
            return x2parts[0][:, oc, c0:c1]
        if oc < 48:
            return x2parts[1][:, oc - 32, c0:c1]
        return x2parts[2][:, oc - 48, c0:c1]

    b = P.block()
    for dst, src in [(bq_s, bqd), (bk_t, bkd), (bv_t, bvd), (bo_t, bod)]:
        P.op(b, "sync", lambda e, d=dst, s=src: e.dma_start(d[:, :], s[:, :]),
             sem="dX", k=16, mark=("bld", dst.name))
    for t_ in [ones_f, onesr_f, ones_bf, id128]:
        P.op(b, "vector", lambda e, t_=t_: e.memset(t_[:, :], 1.0))
    P.op(b, "vector", lambda e: e.memset(sel8[:, :], 1.0), mark="consts")
    P.op(b, "vector",
         lambda e: e.tensor_scalar(bq_s[:, :], bq_s[:, :], ISQ, None, ALU.mult),
         waits=[("bld", "bo_t")], mark="bqs")
    P.op(b, "vector", lambda e: e.memset(id128b[:, :], 1.0), mark="constsb")
    P.op(b, "gpsimd",
         lambda e: e.affine_select(id128[:, :], id128[:, :], [[1, 128]],
                                   ALU.is_equal, 0.0, base=0, channel_multiplier=-1),
         waits=["consts", "constsb"], mark="id128")
    P.op(b, "gpsimd",
         lambda e: e.affine_select(id128b[:, :], id128b[:, :], [[1, 128]],
                                   ALU.is_equal, 0.0, base=0, channel_multiplier=-1),
         mark="id128b")
    P.op(b, "gpsimd",
         lambda e: e.affine_select(sel8.ap().rearrange("p (b c) -> p b c", b=8),
                                   sel8.ap().rearrange("p (b c) -> p b c", b=8),
                                   [[1, 8], [0, 128]], ALU.is_equal, 0.0,
                                   base=0, channel_multiplier=-1),
         mark="sel8")

    s1 = ExitStack()
    x_fm = a1b.rearrange("p (a c) -> p a c", a=16)
    kev = sb("kev", [128, 4, 512], BF16, s1)
    ones8c = sb("ones8c", [128, 8], BF16, s1)
    ones8r = sb("ones8r", [8, 128], BF16, s1)
    mk2b8 = sb("mk2b8", [8, 16], BF16, s1)
    k2c = sb("k2c", [128, 2, 512], BF16, s1)
    kmx8 = rminf
    q2c = sb("q2c", [128, 2, 128], BF16, s1)
    ckb = [sb(f"ckb{i}", [128, 16, 128], BF16, s1) for i in range(2)]
    skb = [sb(f"skb{i}", [128, 16, 128], BF16, s1) for i in range(2)]
    wkb = [sb(f"wkb{i}", [128, 16, 128], BF16, s1) for i in range(2)]
    vrow = [sb(f"vrow{i}", [128, 4, 512], BF16, s1) for i in range(2)]
    vtrb = [sb(f"vtrb{i}", [128, 16, 128], BF16, s1, is_transpose_dest=True)
            for i in range(2)]
    pk = psum("pk", [128, 4, 512], s1)
    pkn = psum("pkn", [8, 2, 512], s1)

    def wcon(mi):
        if mi < 16:
            return ("pkg", mi * 4 + 3)
        if mi < 32:
            return ("pkg", 64 + (mi - 16) * 4 + 3)
        return ("pkg", 128 + (mi - 32) * 2 + 1)

    b = P.block()
    P.op(b, "gpsimd",
         lambda e: e.dma_start(x_fm,
                               xt.ap().rearrange("(dc p) t -> p dc t", p=128)),
         sem="gX", k=16, mark="x_fm")
    mats = [(wk, sk), (wv, sv), (wq, sq)]
    wdst = {"i": 0}

    def emit_wd(upto):
        while wdst["i"] <= min(upto, 47):
            mi = wdst["i"]
            m, fc = divmod(mi, 16)
            cod, scl = mats[m]
            wt = [wcon(mi - 2)] if mi >= 2 else []
            P.op(b, "gpsimd",
                 lambda e, cod=cod, fc=fc, mi=mi: e.dma_start(
                     ckb[mi % 2].ap(),
                     cod[:, fc * 128:(fc + 1) * 128]
                     .rearrange("(dc p) f -> p dc f", p=128)),
                 sem=f"gW{mi % 2}", k=16, waits=wt)
            P.op(b, "gpsimd",
                 lambda e, scl=scl, fc=fc, mi=mi: e.dma_start(
                     skb[mi % 2].ap(),
                     scl[:, fc * 128:(fc + 1) * 128]
                     .rearrange("(dc p) f -> p dc f", p=128)),
                 sem=f"gW{mi % 2}", k=16, mark=("wdma", mi))
            wdst["i"] += 1
    wbst = {"i": 0}

    def emit_wb(upto):
        while wbst["i"] <= min(upto, 47):
            mi = wbst["i"]
            wt = [("wdma", mi)] + ([wcon(mi - 2)] if mi >= 2 else [])
            P.op(b, "vector",
                 lambda e, mi=mi: e.tensor_tensor(
                     wkb[mi % 2].ap().rearrange("p a c -> p (a c)"),
                     ckb[mi % 2].ap().rearrange("p a c -> p (a c)"),
                     skb[mi % 2].ap().rearrange("p a c -> p (a c)"), ALU.mult),
                 mark=("wbuild", mi), waits=wt)
            wbst["i"] += 1
    P.op(b, "vector", lambda e: e.memset(ones8c[:, :], 1.0))
    P.op(b, "vector", lambda e: e.memset(ones8r[:, :], 1.0), mark="ones8")
    emit_wd(2)
    emit_wb(1)

    groups = []
    for fc in range(16):
        for th in range(4):
            groups.append((len(groups), fc, th * 512))
    for fc in range(16):
        for th in range(4):
            groups.append((len(groups), 16 + fc, th * 512))
    for fc in range(16):
        for th in range(2):
            groups.append((len(groups), 32 + fc, th * 512))

    for n, mi, tok0 in groups:
        def mmg(e, n=n, mi=mi, tok0=tok0):
            ins = None
            for dc in range(16):
                ins = e.matmul(pk[:, n % 4, :], wkb[mi % 2][:, dc, :],
                               x_fm[:, dc, tok0:tok0 + 512],
                               start=(dc == 0), stop=(dc == 15))
            return ins
        wt = ["x_fm", ("wbuild", mi)]
        if n >= 4:
            wt.append(("pkev", n - 4))
        P.op(b, "tensor", mmg, mark=("pkg", n), waits=wt)
        j = n - 1
        if 0 <= j < 64:
            wtk = [("k2q", j), "ones8"] + ([("kmxr", j - 2)] if j >= 2 else [])
            P.op(b, "tensor",
                 lambda e, j=j: e.matmul(pkn[:, j % 2, :], ones8c[:, :],
                                         k2c[:, j % 2, :], start=True, stop=True),
                 mark=("pknm", j), waits=wtk)

    gplan = [("f", i) for i in range(64)] + [("p", i) for i in range(64)]

    a3f = arena3.bitcast(F32)

    def g_src(j):
        which, i = gplan[j]
        if which == "f":
            nn, mm_ = divmod(i, 4)
            return wfc.ap().rearrange("(n p) (m f) -> n m p f", p=128, f=2048)[nn, mm_]
        return wpr.ap().rearrange("(n p) f -> n p f", p=128)[i]

    def g_buf(j):
        if j < 40:
            t = j % 4
            return a2f[:, t * 2048:(t + 1) * 2048]
        if j < 64:
            t = (j - 40) % 3
            return a2f[:, t * 2048:(t + 1) * 2048]
        if j < 96:
            t = j % 4
            return a3f[:, t * 2048:(t + 1) * 2048]
        return a3f[:, (2 + j % 2) * 2048:(3 + j % 2) * 2048]

    def g_prev(j):
        if j >= 96:
            return j - 2
        if j >= 43:
            return j - 3 if j < 64 else j - 4
        return j - 4

    def emit_gdma(j, eng):
        wt = [("gtile", g_prev(j))] if g_prev(j) >= 0 else []
        fam = "gG" if eng == "gpsimd" else "gS"
        P.op(b, eng, lambda e, j=j: e.dma_start(g_buf(j), g_src(j)),
             sem=f"{fam}{j % 4}", k=16, mark=("gdma", j), waits=wt)

    def emit_gred_v(j):
        which, i = gplan[j]
        acc = gacc_f if which == "f" else gacc_p
        P.op(b, "vector",
             lambda e, j=j, acc=acc, i=i: e.reduce_sum(
                 acc[:, i:i + 1], g_buf(j), axis=AX.X,
                 apply_absolute_value=True),
             mark=("gtile", j), waits=[("gdma", j)])

    def emit_gred_s(j):
        which, i = gplan[j]
        acc = gacc_f if which == "f" else gacc_p
        P.op(b, "scalar",
             lambda e, j=j, acc=acc, i=i: e.activation(
                 g_buf(j), g_buf(j), AF.Abs, accum_out=acc[:, i:i + 1]),
             mark=("gtile", j), waits=[("gdma", j)])

    gfst = {"d": 0, "r": 0}

    def g_dma_upto(upto):
        while gfst["d"] < min(upto, 40):
            emit_gdma(gfst["d"], "sync")
            gfst["d"] += 1

    def g_red_upto(upto):
        while gfst["r"] < min(upto, 40):
            emit_gred_v(gfst["r"])
            gfst["r"] += 1
    KMX = 7424
    MKH = 7488
    MKB = 7504
    MH2 = 7168
    MHP = 7296
    NMH = 8064
    for n in range(64):
        fc, th = divmod(n, 4)
        wt = [("pkg", n)]
        if th == 0 and fc >= 1:
            wt.append(("kspill", fc - 1))
        wtq = [("pkg", n)] + ([("pknm", n - 2)] if n >= 2 else [])
        P.op(b, "scalar",
             lambda e, fc=fc, th=th, n=n: e.activation(
                 k2c[:, n % 2, :], pk[:, n % 4, :], AF.Square,
                 bias=bk_t[:, fc:fc + 1]),
             mark=("k2q", n), waits=wtq)
        P.op(b, "scalar",
             lambda e, fc=fc, th=th, n=n: e.activation(
                 kev[:, th, :], pk[:, n % 4, :], AF.Identity, bias=bk_t[:, fc:fc + 1]),
             mark=("pkev", n), waits=wt)
        emit_wb(fc + 1)
        P.op(b, "vector",
             lambda e, n=n: e.reduce_max(kmx8[0:8, 128 + n:129 + n],
                                         pkn[0:8, n % 2, :], axis=AX.X),
             mark=("kmxr", n), waits=[("pknm", n)])
        if th == 3:
            P.op(b, "vector",
                 lambda e, fc=fc: e.reduce_max(kmx8[0:8, 192 + fc:193 + fc],
                                               kmx8[0:8, 128 + 4 * fc:132 + 4 * fc],
                                               axis=AX.X),
                 mark=("mkh", fc))
        if th == 3:
            P.op(b, "sync",
                 lambda e, fc=fc: e.dma_start(
                     kspill[fc * 128:(fc + 1) * 128, :],
                     kev.rearrange("p a c -> p (a c)")),
                 sem="kSp", k=16, mark=("kspill", fc), waits=[("pkev", fc * 4 + 3)])
        emit_wd(n // 4 + 3)
        g_dma_upto(n // 2 + 1)
        g_red_upto(n // 2 - 2)
    for n in range(64, 128):
        fc, th = divmod(n - 64, 4)
        wt = [("pkg", n)]
        if th == 0 and fc >= 2:
            wt.append(("vtr", fc - 2))
        P.op(b, "scalar",
             lambda e, fc=fc, th=th, n=n: e.activation(
                 vrow[fc % 2][:, th, :], pk[:, n % 4, :], AF.Identity,
                 bias=bv_t[:, fc:fc + 1]),
             mark=("pkev", n), waits=wt)
        emit_wd((n - 64) // 4 + 19)
        emit_wb((n - 64) // 4 + 17)
        g_dma_upto(32 + (n - 64) // 2)
        g_red_upto(30 + (n - 64) // 2)
        if th == 3:
            wt2 = [("pkev", n)] + ([("vsp", fc - 2)] if fc >= 2 else [])
            P.op(b, "sync",
                 lambda e, fc=fc: e.dma_start(
                     vtrb[fc % 2].ap(),
                     vrow[fc % 2].ap().rearrange("p a c -> p (a c)")
                     .rearrange("p (tc i) -> p tc i", tc=16),
                     transpose=True),
                 sem=f"vT{fc % 2}", k=16, mark=("vtr", fc), waits=wt2)
            P.op(b, "sync",
                 lambda e, fc=fc: e.dma_start(
                     vspill[fc].rearrange("(tc p) i -> p tc i", p=128),
                     vtrb[fc % 2].ap()),
                 sem=f"vP{fc % 2}", k=16, mark=("vsp", fc), waits=[("vtr", fc)])
    for n in range(128, 160):
        fc, th = divmod(n - 128, 2)
        P.op(b, "scalar",
             lambda e, fc=fc, th=th, n=n: e.activation(
                 Qt[:, fc, th * 512:(th + 1) * 512], pk[:, n % 4, :],
                 AF.Identity, scale=ISQ, bias=bq_s[:, fc:fc + 1]),
             mark=("pkev", n), waits=[("pkg", n)])
        emit_wd((n - 128) // 2 + 35)
        emit_wb((n - 128) // 2 + 33)
        g_red_upto(34 + (n - 128) // 2)
    emit_wd(47)
    emit_wb(47)
    g_red_upto(40)


    s2 = ExitStack()
    pB2 = psum("pB2", [128, 512], s2)
    pqn = psum("pqn", [128, 128], s2)
    b = P.block()
    P.op(b, "vector", lambda e: e.tensor_copy(mk2b8[:, :], kmx8[0:8, 192:208]),
         mark="mk2b")
    P.op(b, "tensor",
         lambda e: e.matmul(pB2[:, 256:272], ones8r[:, :], mk2b8[:, :],
                            start=True, stop=True),
         mark="mkbc", waits=["mk2b"])
    P.op(b, "scalar", lambda e: e.copy(a2f[:, MKB:MKB + 16], pB2[:, 256:272]),
         mark="mkev", waits=["mkbc"])
    for j2 in range(128):
        h_, qb_ = divmod(j2, 8)
        P.op(b, "vector",
             lambda e, j2=j2, h_=h_, qb_=qb_: e.tensor_tensor(
                 q2c[:, j2 % 2, :], Qt[:, h_, qb_ * 128:(qb_ + 1) * 128],
                 Qt[:, h_, qb_ * 128:(qb_ + 1) * 128], ALU.mult),
             mark=("q2c", j2), waits=[("qnm", j2 - 2)] if j2 >= 2 else [])
        P.op(b, "tensor",
             lambda e, j2=j2: e.matmul(pqn[:, j2:j2 + 1], q2c[:, j2 % 2, :],
                                       ones_bf[:, :], start=True, stop=True),
             mark=("qnm", j2), waits=[("q2c", j2)])
        if j2 % 5 == 0 and j2 // 5 < 24:
            emit_gdma(40 + j2 // 5, "gpsimd")
            if j2 // 5 >= 2:
                emit_gred_v(38 + j2 // 5)
    for h_ in range(16):
        P.op(b, "vector",
             lambda e, h_=h_: e.tensor_scalar(
                 a2f[:, MH2 + h_ * 8:MH2 + h_ * 8 + 8], pqn[:, h_ * 8:h_ * 8 + 8],
                 a2f[:, MKB + h_:MKB + h_ + 1], None, ALU.mult),
             mark=("mh2", h_), waits=[("qnm", h_ * 8 + 7)] + (["mkev"] if h_ == 0 else []))
    P.op(b, "scalar",
         lambda e: e.activation(a2f[:, MHP:MHP + 128], a2f[:, MH2:MH2 + 128], AF.Sqrt),
         mark="mhs", waits=[("mh2", 15)])
    P.op(b, "vector",
         lambda e: e.tensor_scalar(a2f[:, NMH:NMH + 128], a2f[:, MHP:MHP + 128],
                                   -1.0, None, ALU.mult),
         mark="nmh", waits=["mhs"])
    for j in range(62, 64):
        emit_gred_v(j)

    s2.close()
    s1.close()

    s3 = ExitStack()
    kslb = [sb(f"kslb{i}", [128, S], BF16, s3) for i in range(2)]
    vslb = [sb(f"vslb{i}", [128, 16, 128], BF16, s3) for i in range(2)]
    Pt = [sb(f"Pt{i}", [128, S], BF16, s3) for i in range(2)]
    sums2 = rminf
    invT = rminf
    Sinv = sb("Sinv", [128, 8], F32, s3)
    invb = [sb(f"invb{i}", [128, Q], F32, s3) for i in range(2)]
    pscore = psum("pscore", [128, 2048], s3)
    pav = psum("pav", [128, 2, 512], s3)
    pinvb = psum("pinvb", [128, Q], s3)

    b = P.block()

    def emit_ptr(h, qb):
        qh = 0 if qb < 4 else 1
        wt2 = []
        if h >= 2 and qb % 4 == 0:
            wt2.append(("avmm", h - 2, qh))

        def ptrf(e, h=h, qb=qb, qh=qh):
            return e.dma_start(
                PT[(h % 2) * 2 + qh][:, :, (qb % 4) * 128:(qb % 4 + 1) * 128],
                Pt[qb % 2].ap().rearrange("p (kc i) -> p kc i", kc=16),
                transpose=True)
        P.op(b, "sync", ptrf, sem=f"tP{qb % 2}", k=16, mark=("ptr", h, qb),
             waits=[("exB", h, qb)] + wt2)

    for h in range(H):
        if h >= 1:
            for qb in range(4):
                emit_ptr(h - 1, qb)
        P.op(b, "sync",
             lambda e, h=h: e.dma_start(kslb[h % 2][:, :],
                                        kspill[h * 128:(h + 1) * 128, :]),
             sem=f"kR{h % 2}", k=16, mark=("ksl", h),
             waits=[("smB", h - 2, 7)] if h >= 2 else [])
        if h >= 1:
            for qb in range(4, 8):
                emit_ptr(h - 1, qb)
        P.op(b, "sync",
             lambda e, h=h: e.dma_start(
                 vslb[h % 2].ap(),
                 vspill[h].rearrange("(tc p) i -> p tc i", p=128)),
             sem=f"vR{h % 2}", k=16, mark=("vsl", h),
             waits=[("avmm", h - 2, 1)] if h >= 2 else [])
    for qb in range(8):
        emit_ptr(H - 1, qb)

    def emit_inv_tr(h):
        wt = [("recip", h)] + ([("invev", h - 1)] if h >= 1 else ["id128"])
        P.op(b, "tensor",
             lambda e: e.transpose(pinvb[0:8, 0:128], Sinv[:, :], id128[:, :]),
             mark=("invtr", h), waits=wt)

    def emit_inv_bc(h):
        def f(e):
            ins = None
            for qb in range(8):
                ins = e.matmul(pinvb[:, qb * 128:(qb + 1) * 128],
                               sel8[:, qb * 128:(qb + 1) * 128], invT[0:8, 208:336],
                               start=True, stop=True)
            return ins
        P.op(b, "tensor", f, mark=("invbc", h), waits=[("invTe", h)])

    def emit_av(hp, qh, kc0, kc1, markkey=None):
        def f(e, hp=hp, qh=qh, kc0=kc0, kc1=kc1):
            ins = None
            for kc in range(kc0, kc1):
                ins = e.matmul(pav[:, qh, :], vslb[hp % 2][:, kc, :],
                               PT[(hp % 2) * 2 + qh][:, kc, :],
                               start=(kc == 0), stop=(kc == 15))
            return ins
        wt = []
        if kc0 == 0:
            wt = [("ptr", hp, qh * 4 + 2), ("ptr", hp, qh * 4 + 3), ("vsl", hp)]
            if hp >= 1:
                wt.append(("avev", hp - 1, qh))
        P.op(b, "tensor", f, mark=markkey, waits=wt)

    for h in range(H):
        for qb in range(8):
            pq = (h, qb - 1) if qb > 0 else (h - 1, 7)

            def smmh(e, h=h, qb=qb, j0=0):
                ins = None
                for j in range(j0, j0 + 2):
                    ins = e.matmul(pscore[:, j * 512:(j + 1) * 512],
                                   Qt[:, h, qb * 128:(qb + 1) * 128],
                                   kslb[h % 2][:, j * 512:(j + 1) * 512],
                                   start=True, stop=True)
                return ins
            wtA = [("ksl", h)]
            wtA.append(("exA",) + pq if (h, qb) != (0, 0) else ("pkev", 159))
            P.op(b, "tensor", lambda e, f=smmh: f(e, j0=0), mark=("smA", h, qb),
                 waits=wtA)
            if h >= 1:
                hp, qh = h - 1, (0 if qb < 4 else 1)
                kc0 = (qb % 4) * 4
                emit_av(hp, qh, kc0, kc0 + 4,
                        markkey=("avmm", hp, qh) if qb % 4 == 3 else None)
            wtB = [("exB",) + pq] if (h, qb) != (0, 0) else []
            P.op(b, "tensor", lambda e, f=smmh: f(e, j0=2), mark=("smB", h, qb),
                 waits=wtB)
            if h >= 1 and qb == 0:
                emit_inv_tr(h - 1)
            if h >= 1 and qb == 1:
                emit_inv_bc(h - 1)
    hl = H - 1
    emit_inv_tr(hl)
    emit_inv_bc(hl)
    for qh in range(2):
        emit_av(hl, qh, 0, 16, markkey=("avmm", hl, qh))

    for h in range(H):
        if h >= 1:
            P.op(b, "vector",
                 lambda e, h=h: e.tensor_copy(invb[(h - 1) % 2][:, :], pinvb[:, :]),
                 mark=("invev", h - 1), waits=[("invbc", h - 1)])
            for qh in range(2):
                P.op(b, "vector",
                     lambda e, h=h, qh=qh: e.tensor_tensor(
                         attn[:, h - 1, qh * 512:(qh + 1) * 512], pav[:, qh, :],
                         invb[(h - 1) % 2][:, qh * 512:(qh + 1) * 512], ALU.mult),
                     mark=("avev", h - 1, qh),
                     waits=[("avmm", h - 1, qh), ("invev", h - 1)])
        P.op(b, "vector",
             lambda e, h=h: e.tensor_tensor(sums[:, h * 8:(h + 1) * 8],
                                            sums[:, h * 8:(h + 1) * 8],
                                            sums2[:, h * 8:(h + 1) * 8], ALU.add),
             mark=("scomb", h), waits=[("exB", h, 7)])
        P.op(b, "vector",
             lambda e, h=h: e.reciprocal(Sinv[:, :], sums[:, h * 8:(h + 1) * 8]),
             mark=("recip", h), waits=[("scomb", h)])

    P.op(b, "vector",
         lambda e: e.tensor_copy(invb[hl % 2][:, :], pinvb[:, :]),
         mark=("invev", hl), waits=[("invbc", hl)])
    for qh in range(2):
        P.op(b, "vector",
             lambda e, qh=qh: e.tensor_tensor(
                 attn[:, hl, qh * 512:(qh + 1) * 512], pav[:, qh, :],
                 invb[hl % 2][:, qh * 512:(qh + 1) * 512], ALU.mult),
             mark=("avev", hl, qh),
             waits=[("avmm", hl, qh), ("invev", hl)])

    for h in range(H):
        for qb in range(8):
            wt = [("smA", h, qb)]
            if h * 8 + qb >= 2:
                pq = (h, qb - 2) if qb >= 2 else (h - 1, 6 + qb)
                wt.append(("ptr",) + pq)
            j2 = h * 8 + qb
            P.op(b, "scalar",
                 lambda e, h=h, qb=qb, j2=j2: e.activation(
                     Pt[qb % 2][:, 0:1024], pscore[:, 0:1024], AF.Exp,
                     bias=a2f[:, NMH + j2:NMH + j2 + 1],
                     accum_out=sums[:, j2:j2 + 1]),
                 mark=("exA", h, qb), waits=wt)
            P.op(b, "scalar",
                 lambda e, h=h, qb=qb, j2=j2: e.activation(
                     Pt[qb % 2][:, 1024:2048], pscore[:, 1024:2048], AF.Exp,
                     bias=a2f[:, NMH + j2:NMH + j2 + 1],
                     accum_out=sums2[:, j2:j2 + 1]),
                 mark=("exB", h, qb), waits=[("smB", h, qb)])
        P.op(b, "scalar", lambda e: e.copy(invT[0:8, 208:336], pinvb[0:8, 0:128]),
             mark=("invTe", h), waits=[("invtr", h)])

    s3.close()

    s4 = ExitStack()
    cob = [sb(f"cob{i}", [128, 16, 128], BF16, s4) for i in range(2)]
    sob = [sb(f"sob{i}", [128, 16, 128], BF16, s4) for i in range(2)]
    wow = [sb(f"wow{i}", [128, 16, 128], BF16, s4) for i in range(2)]
    xqt = [sb(f"xqt{i}", [128, Q], F32, s4) for i in range(2)]
    t1t = [sb(f"t1t{i}", [128, Q], F32, s4) for i in range(2)]
    ybf = [sb(f"ybf{i}", [128, Q], BF16, s4) for i in range(2)]
    yb2 = sb("yb2", [128, Q], BF16, s4)
    poc = psum("poc", [128, 2, 2, 512], s4)
    pst1 = psum("pst1", [128, 512], s4)
    pst2 = psum("pst2", [128, 512], s4)

    b = P.block()
    for doc in range(16):
        P.op(b, "gpsimd",
             lambda e, doc=doc: e.dma_start(
                 cob[doc % 2].ap(),
                 wo[:, doc * 128:(doc + 1) * 128].rearrange("(fc p) f -> p fc f", p=128)),
             sem=f"oW{doc % 2}", k=16, waits=[("wob", doc - 2)] if doc >= 2 else [])
        P.op(b, "gpsimd",
             lambda e, doc=doc: e.dma_start(
                 sob[doc % 2].ap(),
                 so[:, doc * 128:(doc + 1) * 128].rearrange("(fc p) f -> p fc f", p=128)),
             sem=f"oW{doc % 2}", k=16, mark=("wod", doc))
        P.op(b, "sync",
             lambda e, doc=doc: e.dma_start(xqt[doc % 2][:, :],
                                            xq[doc * 128:(doc + 1) * 128, :]),
             sem=f"xQ{doc % 2}", k=16, mark=("xqd", doc),
             waits=[("yadd", doc - 2)] if doc >= 2 else [])
        emit_gdma(64 + 2 * doc, "sync")
        emit_gdma(65 + 2 * doc, "gpsimd")
    def emit_wob(doc):
        P.op(b, "vector",
             lambda e, doc=doc: e.tensor_tensor(
                 wow[doc % 2].ap().rearrange("p a c -> p (a c)"),
                 cob[doc % 2].ap().rearrange("p a c -> p (a c)"),
                 sob[doc % 2].ap().rearrange("p a c -> p (a c)"), ALU.mult),
             mark=("wob", doc),
             waits=[("wod", doc)] + ([("omm", doc - 2)] if doc >= 2 else []))

    emit_wob(0)
    emit_wob(1)
    for doc in range(16):
        def ommf(e, doc=doc):
            ins = None
            for qh in range(2):
                for fc in range(16):
                    ins = e.matmul(poc[:, doc % 2, qh, :], wow[doc % 2][:, fc, :],
                                   attn[:, fc, qh * 512:(qh + 1) * 512],
                                   start=(fc == 0), stop=(fc == 15))
            return ins
        P.op(b, "tensor", ommf, mark=("omm", doc),
             waits=[("wob", doc)] + ([("oev", doc - 2)] if doc >= 2 else []))
        if doc >= 1:
            dp = doc - 1

            def stf(e, dp=dp):
                ins = None
                for qb in range(8):
                    ins = e.matmul(pst1[:, qb:qb + 1],
                                   ybf[dp % 2][:, qb * 128:(qb + 1) * 128],
                                   ones_bf[:, :], start=(dp == 0 and qb == 0), stop=False)
                    ins = e.matmul(pst2[:, qb:qb + 1],
                                   yb2[:, qb * 128:(qb + 1) * 128],
                                   ones_bf[:, :], start=(dp == 0 and qb == 0), stop=False)
                return ins
            P.op(b, "tensor", stf, mark=("stmm", dp), waits=[("yb2", dp)])

    def stfl(e):
        ins = None
        for qb in range(8):
            ins = e.matmul(pst1[:, qb:qb + 1], ybf[15 % 2][:, qb * 128:(qb + 1) * 128],
                           ones_bf[:, :], start=False, stop=(qb == 7))
            ins = e.matmul(pst2[:, qb:qb + 1], yb2[:, qb * 128:(qb + 1) * 128],
                           ones_bf[:, :], start=False, stop=(qb == 7))
        return ins
    P.op(b, "tensor", stfl, mark=("stmm", 15), waits=[("yb2", 15)])

    for doc in range(16):
        P.op(b, "scalar",
             lambda e, doc=doc: e.activation(
                 t1t[doc % 2][:, :],
                 poc.ap()[:, doc % 2, :, :].rearrange("p a c -> p (a c)"),
                 AF.Identity, bias=bo_t[:, doc:doc + 1]),
             mark=("oev", doc),
             waits=[("omm", doc)] + ([("yadd", doc - 2)] if doc >= 2 else []))
        if doc >= 1:
            emit_gred_s(62 + 2 * doc)
            emit_gred_s(63 + 2 * doc)
        if doc == 15:
            emit_gred_s(94)
            emit_gred_s(95)
    for doc in range(16):
        P.op(b, "vector",
             lambda e, doc=doc: e.tensor_tensor(y[:, doc, :], t1t[doc % 2][:, :],
                                                xqt[doc % 2][:, :], ALU.add),
             mark=("yadd", doc), waits=[("oev", doc), ("xqd", doc)])
        P.op(b, "vector",
             lambda e, doc=doc: e.tensor_copy(ybf[doc % 2][:, :], y[:, doc, :]),
             mark=("ybf", doc),
             waits=[("stmm", doc - 2)] if doc >= 2 else [])
        P.op(b, "vector",
             lambda e, doc=doc: e.tensor_tensor(yb2[:, :], ybf[doc % 2][:, :],
                                                ybf[doc % 2][:, :], ALU.mult),
             mark=("yb2", doc),
             waits=[("stmm", doc - 1)] if doc >= 1 else [])
        if doc == 0:
            P.op(b, "vector", lambda e: e.tensor_copy(rmax[:, :], ybf[0][:, :]))
            P.op(b, "vector", lambda e: e.tensor_copy(rmin[:, :], ybf[0][:, :]),
                 mark=("tree", 0))
        else:
            P.op(b, "vector",
                 lambda e, doc=doc: e.tensor_tensor(rmax[:, :], rmax[:, :],
                                                    ybf[doc % 2][:, :], ALU.max))
            P.op(b, "vector",
                 lambda e, doc=doc: e.tensor_tensor(rmin[:, :], rmin[:, :],
                                                    ybf[doc % 2][:, :], ALU.min),
                 mark=("tree", doc))
        if doc + 2 < 16:
            emit_wob(doc + 2)

    P.op(b, "scalar", lambda e: e.copy(stt["sums1"][:, :], pst1[:, 0:8]),
         waits=[("stmm", 15)])
    P.op(b, "scalar", lambda e: e.copy(stt["sumsq1"][:, :], pst2[:, 0:8]),
         mark="ln1ev")

    s4.close()

    s5 = ExitStack()
    txn = [sb(f"txn{i}", [128, Q], F32, s5) for i in range(2)]
    ptr_ = psum("ptr_", [128, 1024], s5)
    pbc = psum("pbc", [128, 1024], s5)

    b = P.block()

    def ln_chain(nfeat, mu_t, R_t, composed, pre, wait0):
        P.op(b, "vector", lambda e, mu_t=mu_t: e.tensor_scalar(
            mu_t[:, :], stt["sums1"][:, :], 1.0 / nfeat, None, ALU.mult),
            waits=[wait0])
        P.op(b, "vector", lambda e: e.tensor_scalar(
            stt["tmp1"][:, :], stt["sumsq1"][:, :], 1.0 / nfeat, None, ALU.mult))
        P.op(b, "vector", lambda e, mu_t=mu_t: e.tensor_tensor(
            stt["tmp2"][:, :], mu_t[:, :], mu_t[:, :], ALU.mult))
        P.op(b, "vector", lambda e: e.tensor_tensor(
            stt["var"][:, :], stt["tmp1"][:, :], stt["tmp2"][:, :], ALU.subtract))
        if composed:
            P.op(b, "vector", lambda e: e.tensor_scalar(
                stt["tmp1"][:, :], stt["var"][:, :], 1.0 + EPS2, EPS1 * EPS2,
                ALU.mult, ALU.add), mark=pre + "a")
        else:
            eps = EPS2 if nfeat == FF else EPS1
            P.op(b, "vector", lambda e, eps=eps: e.tensor_scalar(
                stt["tmp1"][:, :], stt["var"][:, :], eps, None, ALU.add),
                mark=pre + "a")
        P.op(b, "scalar", lambda e: e.activation(stt["tmp2"][:, :], stt["tmp1"][:, :],
                                                 AF.Sqrt),
             mark=pre + "sq", waits=[pre + "a"])
        P.op(b, "vector", lambda e, R_t=R_t: e.reciprocal(R_t[:, :], stt["tmp2"][:, :]),
             mark=pre + "R", waits=[pre + "sq"])

    for doc in range(16):
        emit_gdma(96 + 2 * doc, "sync")
        emit_gdma(97 + 2 * doc, "gpsimd")
    ln_chain(D, stt["mu"], stt["R"], True, "ln1", "ln1ev")
    P.op(b, "vector", lambda e: e.tensor_scalar(stt["tmp3"][:, :], stt["var"][:, :],
                                                EPS1, None, ALU.add),
         mark="ln1add", waits=["ln1R"])
    P.op(b, "scalar", lambda e: e.activation(stt["tmp3"][:, :], stt["tmp3"][:, :],
                                             AF.Sqrt), mark="ln1sA", waits=["ln1add"])
    P.op(b, "vector", lambda e: e.reciprocal(stt["tmp1"][:, :], stt["tmp3"][:, :]),
         waits=["ln1sA"])
    P.op(b, "vector", lambda e: e.tensor_tensor(stt["rho"][:, :], stt["tmp2"][:, :],
                                                stt["tmp1"][:, :], ALU.mult),
         mark="rho")

    def trred(src, dst_t, op, mk, wt, pp):
        ppb = pp.bitcast(BF16)
        def trf(e, src=src, ppb=ppb):
            ins = None
            for j in range(8):
                ins = e.transpose(ppb[:, j * 128:(j + 1) * 128],
                                  src[:, j * 128:(j + 1) * 128], id128b[:, :])
            return ins
        P.op(b, "tensor", trf, mark=mk + "t", waits=wt)

        def rf(e, dst_t=dst_t, op=op, ppb=ppb):
            ins = None
            for j in range(8):
                ins = e.tensor_reduce(dst_t[:, j:j + 1], ppb[:, j * 128:(j + 1) * 128],
                                      AX.X, op)
            return ins
        P.op(b, "vector", rf, mark=mk + "r", waits=[mk + "t"])

    trred(rmax, stt["mx"], ALU.max, "mx1", [("tree", 15)], ptr_)
    trred(rmin, stt["mn"], ALU.min, "mn1", ["mx1r"], ptr_)

    P.op(b, "vector", lambda e: e.tensor_tensor(stt["tmp1"][:, :], stt["mx"][:, :],
                                                stt["mu"][:, :], ALU.subtract),
         waits=["mn1r", "rho"])
    P.op(b, "vector", lambda e: e.tensor_tensor(stt["tmp2"][:, :], stt["mu"][:, :],
                                                stt["mn"][:, :], ALU.subtract))
    P.op(b, "vector", lambda e: e.tensor_tensor(stt["tmp1"][:, :], stt["tmp1"][:, :],
                                                stt["tmp2"][:, :], ALU.max))
    P.op(b, "vector", lambda e: e.tensor_tensor(stt["gx"][:, :], stt["tmp1"][:, :],
                                                stt["R"][:, :], ALU.mult))
    P.op(b, "vector", lambda e: e.tensor_scalar(stt["cx"][:, :], stt["gx"][:, :],
                                                CLIPB, None, ALU.mult))
    P.op(b, "vector", lambda e: e.tensor_scalar(stt["ncx"][:, :], stt["gx"][:, :],
                                                -CLIPB, None, ALU.mult), mark="cx1")

    def bcast(stat, dstB, key, wkey, pp):
        P.op(b, "tensor", lambda e, stat=stat: e.transpose(pp[0:8, 0:128],
                                                           stat[:, :], id128[:, :]),
             mark=key + "t", waits=[wkey])
        P.op(b, "scalar", lambda e: e.copy(trT[:, :], pp[0:8, 0:128]),
             mark=key + "e", waits=[key + "t"])

        def bmmf(e):
            ins = None
            for qb in range(8):
                ins = e.matmul(pp[:, qb * 128:(qb + 1) * 128],
                               sel8[:, qb * 128:(qb + 1) * 128], trT[:, :],
                               start=True, stop=True)
            return ins
        P.op(b, "tensor", bmmf, mark=key + "m", waits=[key + "e"])
        P.op(b, "vector", lambda e, dstB=dstB: e.tensor_copy(dstB[:, :], pp[:, :]),
             mark=key + "b", waits=[key + "m"])

    bcast(stt["mu"], muB, "bmu", "cx1", pbc)
    bcast(stt["R"], RB, "bR", "bmub", pbc)
    bcast(stt["cx"], cB, "bcx", "bRb", pbc)
    bcast(stt["ncx"], ncB, "bncx", "bcxb", pbc)
    bcast(stt["rho"], rhoB, "brho", "bncxb", pbc)

    for doc in range(16):
        P.op(b, "vector",
             lambda e, doc=doc: e.tensor_tensor(txn[doc % 2][:, :], y[:, doc, :],
                                                muB[:, :], ALU.subtract),
             waits=(["brhob"] if doc == 0 else []))
        P.op(b, "vector",
             lambda e, doc=doc: e.tensor_tensor(y[:, doc, :], txn[doc % 2][:, :],
                                                RB[:, :], ALU.mult),
             mark=("xn", doc))
        P.op(b, "vector",
             lambda e, doc=doc: e.tensor_tensor(txn[doc % 2][:, :], y[:, doc, :],
                                                cB[:, :], ALU.min))
        xw = [] if doc < 8 else [("gtile", 126 if doc < 12 else 127)]
        P.op(b, "vector",
             lambda e, doc=doc: e.tensor_tensor(xh[:, doc, :], txn[doc % 2][:, :],
                                                ncB[:, :], ALU.max),
             mark=("xhat", doc), waits=xw)
        P.op(b, "sync",
             lambda e, doc=doc: e.dma_start(xnspill[doc * 128:(doc + 1) * 128, :],
                                            y[:, doc, :]),
             sem="dX", k=16, mark=("xnsp", doc), waits=[("xn", doc)])
    for j in range(96, 128):
        emit_gred_s(j)

    s5.close()

    s6 = ExitStack()
    wfw = [arena1[:, i * 2048:(i + 1) * 2048].rearrange("p (a c) -> p a c", a=16)
           for i in range(4)]
    wps = [arena1[:, 8192 + i * 2048:8192 + (i + 1) * 2048] for i in range(4)]
    pf = sb("pf", [128, 2048], BF16, s6)
    nf = sb("nf", [128, 2048], BF16, s6)
    tf = [sb(f"tf{i}", [128, 16, 128], BF16, s6) for i in range(2)]
    pa = sb("pa", [128, 2048], BF16, s6)
    pb_ = sb("pb_", [128, 2048], BF16, s6)
    tp8 = [sb(f"tp8{i}", [128, 2048], FP8, s6) for i in range(2)]
    hsl = [sb(f"hsl{i}", [128, Q], BF16, s6) for i in range(2)]
    h2t = sb("h2t", [128, Q], BF16, s6)
    ph = psum("ph", [128, 2, 1024], s6)
    psh1 = psum("psh1", [128, 512], s6)
    psh2 = psum("psh2", [128, 512], s6)
    pg6 = psum("pg6", [128, 8], s6)

    b = P.block()
    P.op(b, "vector", lambda e: e.reduce_sum(gacc_f[:, 0:1], gacc_f[:, :], axis=AX.X),
         waits=[("gtile", 127)])
    P.op(b, "vector", lambda e: e.reduce_sum(gacc_p[:, 0:1], gacc_p[:, :], axis=AX.X),
         mark="gred")
    P.op(b, "tensor", lambda e: e.matmul(pg6[0:1, 0:1], gacc_f[:, 0:1], ones_f[:, :],
                                         start=True, stop=False), waits=["gred"])
    P.op(b, "tensor", lambda e: e.matmul(pg6[0:1, 1:2], gacc_p[:, 0:1], ones_f[:, :],
                                         start=False, stop=True), mark="gsum")
    P.op(b, "scalar", lambda e: e.copy(g1[0:1, 0:2], pg6[0:1, 0:2]),
         mark="g1ev", waits=["gsum"])
    P.op(b, "vector", lambda e: e.tensor_scalar(g1[0:1, 2:4], g1[0:1, 0:2],
                                                1.0 / NW, None, ALU.mult),
         waits=["g1ev"])
    P.op(b, "vector", lambda e: e.tensor_scalar(g1[0:1, 4:6], g1[0:1, 2:4],
                                                1e-6, 0.5, ALU.add, ALU.mult))
    P.op(b, "vector", lambda e: e.tensor_scalar(g1[0:1, 6:8], g1[0:1, 4:6],
                                                -1.0, None, ALU.mult))
    P.op(b, "vector", lambda e: e.tensor_scalar(g1[0:1, 8:10], g1[0:1, 2:4],
                                                0.5, None, ALU.mult), mark="gscal")
    gcols = [(4, thf), (5, thp), (6, nthf), (7, nthp), (2, gfb), (9, gpb)]
    for i, (col, dst) in enumerate(gcols):
        P.op(b, "tensor",
             lambda e, i=i, col=col: e.matmul(pg6[:, 2 + i:3 + i], onesr_f[:, :],
                                              g1[0:1, col:col + 1], start=True, stop=True),
             mark=("gbc", i), waits=(["gscal"] if i == 0 else []))
    for i, (col, dst) in enumerate(gcols):
        P.op(b, "scalar", lambda e, i=i, dst=dst: e.copy(dst[:, :], pg6[:, 2 + i:3 + i]),
             mark=("gbcev", i), waits=([("gbc", 5)] if i == 0 else []))

    def emit_wfd(oc):
        P.op(b, "sync",
             lambda e, oc=oc: e.dma_start(
                 wfw[oc % 4],
                 wfc[:, oc * 128:(oc + 1) * 128].rearrange("(dc p) f -> p dc f", p=128)),
             sem=f"fW{oc % 4}", k=16, mark=("wfd", oc),
             waits=([("nfc", oc - 4)] if oc >= 4 else []))

    def emit_trio(oc):
        P.op(b, "vector",
             lambda e, oc=oc: e.tensor_scalar(
                 pf[:, :], wfw[oc % 4].rearrange("p a c -> p (a c)"),
                 thf[:, :], None, ALU.is_gt),
             mark=("pfc", oc),
             waits=[("wfd", oc)] + ([("fmm", oc - 2)] if oc >= 2 else [("gbcev", 5)]))
        P.op(b, "vector",
             lambda e, oc=oc: e.tensor_scalar(
                 nf[:, :], wfw[oc % 4].rearrange("p a c -> p (a c)"),
                 nthf[:, :], None, ALU.is_lt),
             mark=("nfc", oc))
        P.op(b, "vector",
             lambda e, oc=oc: e.tensor_tensor(
                 tf[oc % 2].ap().rearrange("p a c -> p (a c)"),
                 pf[:, :], nf[:, :], ALU.subtract),
             mark=("tfc", oc))

    def emit_slow(j):
        P.op(b, "vector",
             lambda e, j=j: e.tensor_tensor(tp8[j % 2][:, :], pa[:, :], pb_[:, :],
                                            ALU.add),
             mark=("tpc", j),
             waits=[("sgb", j)] + ([("tpw", j - 2)] if j >= 2 else []))
        if j >= 1:
            P.op(b, "vector",
                 lambda e, j=j: e.tensor_tensor(h2t[:, :], hsl[(j - 1) % 2][:, :],
                                                hsl[(j - 1) % 2][:, :], ALU.mult),
                 mark=("h2", j - 1),
                 waits=[("hev", j - 1)] + ([("hst", j - 2)] if j >= 2 else []))
        if j == 0:
            P.op(b, "vector", lambda e: e.tensor_copy(rmax[:, :], hsl[0][:, :]),
                 waits=[("hev", 0)])
            P.op(b, "vector", lambda e: e.tensor_copy(rmin[:, :], hsl[0][:, :]),
                 mark=("htree", 0))
        else:
            P.op(b, "vector",
                 lambda e, j=j: e.tensor_tensor(rmax[:, :], rmax[:, :],
                                                hsl[j % 2][:, :], ALU.max),
                 waits=[("hev", j)])
            P.op(b, "vector",
                 lambda e, j=j: e.tensor_tensor(rmin[:, :], rmin[:, :],
                                                hsl[j % 2][:, :], ALU.min),
                 mark=("htree", j))

    emit_wfd(0)
    emit_wfd(1)
    for oc in range(64):
        if oc + 2 < 64:
            emit_wfd(oc + 2)
        P.op(b, "gpsimd",
             lambda e, oc=oc: e.dma_start(wps[oc % 4][:, :],
                                          wpr[oc * 128:(oc + 1) * 128, :]),
             sem=f"pW{oc % 4}", k=16, mark=("wpd", oc),
             waits=[("sgb", oc - 4)] if oc >= 4 else [])
        emit_trio(oc)
        if oc >= 1:
            emit_slow(oc - 1)
        P.op(b, "scalar",
             lambda e, oc=oc: e.activation(pa[:, :], wps[oc % 4][:, :], AF.Sign,
                                           bias=nthp[:, :]),
             waits=[("wpd", oc)] + ([("tpc", oc - 1)] if oc >= 1 else [("gbcev", 5)]))
        P.op(b, "scalar",
             lambda e, oc=oc: e.activation(pb_[:, :], wps[oc % 4][:, :], AF.Sign,
                                           bias=thp[:, :]),
             mark=("sgb", oc))

        def fmmf(e, oc=oc):
            ins = None
            for qh in range(2):
                for dc in range(16):
                    ins = e.matmul(ph[:, oc % 2, qh * 512:(qh + 1) * 512],
                                   tf[oc % 2][:, dc, :],
                                   xh[:, dc, qh * 512:(qh + 1) * 512],
                                   start=(dc == 0), stop=(dc == 15))
            return ins
        P.op(b, "tensor", fmmf, mark=("fmm", oc),
             waits=[("tfc", oc)] + ([("hev", oc - 2)] if oc >= 2 else []))
        if oc >= 1:
            def hstf(e, op=oc - 1):
                ins = None
                for qb in range(8):
                    ins = e.matmul(psh1[:, qb:qb + 1],
                                   hsl[op % 2][:, qb * 128:(qb + 1) * 128],
                                   ones_bf[:, :], start=(op == 0 and qb == 0), stop=False)
                    ins = e.matmul(psh2[:, qb:qb + 1],
                                   h2t[:, qb * 128:(qb + 1) * 128],
                                   ones_bf[:, :], start=(op == 0 and qb == 0), stop=False)
                return ins
            P.op(b, "tensor", hstf, mark=("hst", oc - 1), waits=[("h2", oc - 1)])
        P.op(b, "scalar",
             lambda e, oc=oc: e.activation(hsl[oc % 2][:, :],
                                           ph.ap()[:, oc % 2, :], AF.Gelu,
                                           scale=gfb[:, :]),
             mark=("hev", oc),
             waits=[("fmm", oc)] + ([("hsp", oc - 2), ("hst", oc - 2),
                                     ("htree", oc - 2)] if oc >= 2 else []))
        P.op(b, "sync",
             lambda e, oc=oc: e.dma_start(hspill[oc * 128:(oc + 1) * 128, :],
                                          hsl[oc % 2][:, :]),
             sem=f"hW{oc % 2}", k=16, mark=("hsp", oc), waits=[("hev", oc)])
        if oc >= 1:
            P.op(b, "sync",
                 lambda e, oc=oc: e.dma_start(tproj[(oc - 1) * 128:oc * 128, :],
                                              tp8[(oc - 1) % 2][:, :]),
                 sem=f"tW{(oc - 1) % 2}", k=16, mark=("tpw", oc - 1),
                 waits=[("tpc", oc - 1)])
    emit_slow(63)
    P.op(b, "sync",
         lambda e: e.dma_start(tproj[63 * 128:64 * 128, :], tp8[63 % 2][:, :]),
         sem=f"tW{63 % 2}", k=16, mark=("tpw", 63), waits=[("tpc", 63)])
    P.op(b, "vector",
         lambda e: e.tensor_tensor(h2t[:, :], hsl[63 % 2][:, :], hsl[63 % 2][:, :],
                                   ALU.mult),
         mark=("h2", 63), waits=[("hev", 63), ("hst", 62)])

    def hstl(e):
        ins = None
        for qb in range(8):
            ins = e.matmul(psh1[:, qb:qb + 1], hsl[63 % 2][:, qb * 128:(qb + 1) * 128],
                           ones_bf[:, :], start=False, stop=(qb == 7))
            ins = e.matmul(psh2[:, qb:qb + 1], h2t[:, qb * 128:(qb + 1) * 128],
                           ones_bf[:, :], start=False, stop=(qb == 7))
        return ins
    P.op(b, "tensor", hstl, mark=("hst", 63), waits=[("h2", 63)])
    P.op(b, "scalar", lambda e: e.copy(stt["sums1"][:, :], psh1[:, 0:8]),
         waits=[("hst", 63)])
    P.op(b, "scalar", lambda e: e.copy(stt["sumsq1"][:, :], psh2[:, 0:8]),
         mark="ln2ev")

    s6.close()

    s7 = ExitStack()
    hrb = [sb(f"hrb{i}", [128, 2, Q], BF16, s7) for i in range(2)]
    th1 = [sb(f"th1{i}", [128, 2, Q], F32, s7) for i in range(1)]
    tpb = [sb(f"tpb{i}", [128, 8, 256], BF16, s7) for i in range(2)]
    xnr = sb("xnr", [128, Q], F32, s7)
    t2t = sb("t2t", [128, Q], F32, s7)
    y2t = [sb(f"y2t{i}", [128, Q], F32, s7) for i in range(2)]
    y2b = sb("y2b", [128, Q], BF16, s7)
    y2b2 = sb("y2b2", [128, Q], BF16, s7)
    py = psum("py", [128, 2, 2, 512], s7)
    psy1 = psum("psy1", [128, 512], s7)
    psy2 = psum("psy2", [128, 512], s7)
    pbc7 = psum("pbc7", [128, 1024], s7)

    b = P.block()
    ln_chain(FF, stt["mu2"], stt["R2"], False, "ln2", "ln2ev")
    trred(rmax, stt["mx"], ALU.max, "mx2", [("htree", 63)], pbc7)
    trred(rmin, stt["mn"], ALU.min, "mn2", ["mx2r"], pbc7)
    P.op(b, "vector", lambda e: e.tensor_tensor(stt["tmp1"][:, :], stt["mx"][:, :],
                                                stt["mu2"][:, :], ALU.subtract),
         waits=["mn2r", "ln2R"])
    P.op(b, "vector", lambda e: e.tensor_tensor(stt["tmp2"][:, :], stt["mu2"][:, :],
                                                stt["mn"][:, :], ALU.subtract))
    P.op(b, "vector", lambda e: e.tensor_tensor(stt["tmp1"][:, :], stt["tmp1"][:, :],
                                                stt["tmp2"][:, :], ALU.max))
    P.op(b, "vector", lambda e: e.tensor_tensor(stt["gx"][:, :], stt["tmp1"][:, :],
                                                stt["R2"][:, :], ALU.mult))
    P.op(b, "vector", lambda e: e.tensor_scalar(stt["c2"][:, :], stt["gx"][:, :],
                                                CLIPB, None, ALU.mult))
    P.op(b, "vector", lambda e: e.tensor_scalar(stt["nc2"][:, :], stt["gx"][:, :],
                                                -CLIPB, None, ALU.mult), mark="cx2")
    bcast(stt["mu2"], muB, "cmu", "cx2", pbc7)
    bcast(stt["R2"], RB, "cR", "cmub", pbc7)
    bcast(stt["c2"], cB, "ccx", "cRb", pbc7)
    bcast(stt["nc2"], ncB, "cncx", "ccxb", pbc7)

    def emit_tpr(dw, ocg):
        if ocg >= 2:
            wt = [("pmm", dw, ocg - 2)]
        elif dw >= 1:
            wt = [("pmm", dw - 1, 6 + ocg)]
        else:
            wt = [("tpw", 63)]
        P.op(b, "gpsimd",
             lambda e, dw=dw, ocg=ocg: e.dma_start(
                 tpb[ocg % 2].ap(),
                 tproj[ocg * 1024:(ocg + 1) * 1024, dw * 256:(dw + 1) * 256]
                 .rearrange("(g p) d -> p g d", p=128)),
             sem=f"tR{ocg % 2}", k=16, mark=("tpr", dw, ocg), waits=wt)

    BC = lambda t: t.ap().unsqueeze(1).broadcast_to([128, 2, Q])
    for tl in range(32):
        oc0 = tl * 2
        P.op(b, "sync",
             lambda e, tl=tl, oc0=oc0: e.dma_start(
                 hrb[tl % 2].ap(),
                 hspill[oc0 * 128:(oc0 + 2) * 128, :]
                 .rearrange("(a p) c -> p a c", p=128)),
             sem=f"hR{tl % 2}", k=16, mark=("hrd", tl),
             waits=([("x2a", tl - 2)] if tl >= 2 else []))
        buf = th1[0]
        P.op(b, "vector",
             lambda e, tl=tl, buf=buf: e.tensor_tensor(
                 buf[:, :, :], hrb[tl % 2][:, :, :], BC(muB), ALU.subtract),
             waits=[("hrd", tl), "cncxb"])
        P.op(b, "vector",
             lambda e, buf=buf: e.tensor_tensor(buf[:, :, :], buf[:, :, :],
                                                BC(RB), ALU.mult))
        P.op(b, "vector",
             lambda e, buf=buf: e.tensor_tensor(buf[:, :, :], buf[:, :, :],
                                                BC(cB), ALU.min))
        P.op(b, "vector",
             lambda e, tl=tl, oc0=oc0, buf=buf: e.tensor_tensor(
                 x2w(oc0), buf[:, :, :], BC(ncB), ALU.max),
             mark=("x2a", tl))
        if tl % 4 == 3:
            emit_tpr(0, tl // 4)
    for dw in range(8):
        for ocg in range(8):
            if dw >= 1:
                emit_tpr(dw, ocg)

            def pmmf(e, dw=dw, ocg=ocg):
                ins = None
                for gi in range(8):
                    oc = ocg * 8 + gi
                    for dj in range(2):
                        for qh in range(2):
                            ins = e.matmul(py[:, dj, qh, :],
                                           tpb[ocg % 2][:, gi, dj * 128:(dj + 1) * 128],
                                           x2sl(oc, qh * 512, (qh + 1) * 512),
                                           start=(oc == 0), stop=(oc == 63))
                return ins
            wt2 = [("tpr", dw, ocg), ("x2a", ocg * 4 + 3)]
            if ocg == 0 and dw >= 1:
                wt2.append(("y2e", dw - 1, 1))
            P.op(b, "tensor", pmmf, mark=("pmm", dw, ocg), waits=wt2)
        for dj in range(2):
            doc = dw * 2 + dj
            P.op(b, "scalar",
                 lambda e, dj=dj: e.activation(
                     t2t[:, :], py.ap()[:, dj, :, :].rearrange("p a c -> p (a c)"),
                     AF.Identity, scale=gpb[:, :]),
                 mark=("y2e", dw, dj),
                 waits=[("pmm", dw, 7)] + ([("y2add", doc - 1)] if doc >= 1 else []))
            P.op(b, "sync",
                 lambda e, doc=doc: e.dma_start(xnr[:, :],
                                                xnspill[doc * 128:(doc + 1) * 128, :]),
                 sem="xR", k=16, mark=("xnr", doc),
                 waits=[("y2add", doc - 1)] if doc >= 1 else [])
            P.op(b, "vector",
                 lambda e, doc=doc: e.tensor_tensor(y2t[doc % 2][:, :], xnr[:, :],
                                                    rhoB[:, :], ALU.mult),
                 waits=[("xnr", doc), ("y2e", dw, dj)] +
                       ([("y2sp", doc - 2)] if doc >= 2 else []))
            P.op(b, "vector",
                 lambda e, doc=doc: e.tensor_tensor(y2t[doc % 2][:, :],
                                                    y2t[doc % 2][:, :], t2t[:, :],
                                                    ALU.add),
                 mark=("y2add", doc))
            P.op(b, "vector",
                 lambda e, doc=doc: e.tensor_copy(y2b[:, :], y2t[doc % 2][:, :]),
                 waits=[("y2st", doc - 1)] if doc >= 1 else [])
            P.op(b, "vector",
                 lambda e: e.tensor_tensor(y2b2[:, :], y2b[:, :], y2b[:, :], ALU.mult),
                 mark=("y2b2", doc))
            if doc == 0:
                P.op(b, "vector", lambda e: e.tensor_copy(rmax[:, :], y2b[:, :]),
                     waits=["mn2t"])
                P.op(b, "vector", lambda e: e.tensor_copy(rmin[:, :], y2b[:, :]),
                     mark=("y2tree", 0))
            else:
                P.op(b, "vector",
                     lambda e: e.tensor_tensor(rmax[:, :], rmax[:, :], y2b[:, :],
                                               ALU.max))
                P.op(b, "vector",
                     lambda e: e.tensor_tensor(rmin[:, :], rmin[:, :], y2b[:, :],
                                               ALU.min),
                     mark=("y2tree", doc))

            def y2stf(e, doc=doc):
                ins = None
                for qb in range(8):
                    ins = e.matmul(psy1[:, qb:qb + 1], y2b[:, qb * 128:(qb + 1) * 128],
                                   ones_bf[:, :], start=(doc == 0 and qb == 0),
                                   stop=(doc == 15 and qb == 7))
                    ins = e.matmul(psy2[:, qb:qb + 1], y2b2[:, qb * 128:(qb + 1) * 128],
                                   ones_bf[:, :], start=(doc == 0 and qb == 0),
                                   stop=(doc == 15 and qb == 7))
                return ins
            P.op(b, "tensor", y2stf, mark=("y2st", doc), waits=[("y2b2", doc)])
            P.op(b, "sync",
                 lambda e, doc=doc: e.dma_start(y2spill[doc * 128:(doc + 1) * 128, :],
                                                y2t[doc % 2][:, :]),
                 sem=f"yW{doc % 2}", k=16, mark=("y2sp", doc), waits=[("y2add", doc)])

    P.op(b, "scalar", lambda e: e.copy(stt["sums1"][:, :], psy1[:, 0:8]),
         waits=[("y2st", 15)])
    P.op(b, "scalar", lambda e: e.copy(stt["sumsq1"][:, :], psy2[:, 0:8]),
         mark="ln3ev")

    s7.close()

    s9 = ExitStack()
    y2r = [sb(f"y2r{i}", [128, Q], F32, s9) for i in range(2)]
    to1 = [sb(f"to1{i}", [128, Q], F32, s9) for i in range(2)]
    pbc9 = psum("pbc9", [128, 1024], s9)

    b = P.block()
    ln_chain(D, stt["mu3"], stt["R3"], False, "ln3", "ln3ev")
    bcast(stt["mu3"], muB, "dmu", "ln3R", pbc9)
    bcast(stt["R3"], RB, "dR", "dmub", pbc9)

    for doc in range(16):
        P.op(b, "sync",
             lambda e, doc=doc: e.dma_start(y2r[doc % 2][:, :],
                                            y2spill[doc * 128:(doc + 1) * 128, :]),
             sem=f"yR{doc % 2}", k=16, mark=("y2r", doc),
             waits=[("outv", doc - 2)] if doc >= 2 else [])
        P.op(b, "vector",
             lambda e, doc=doc: e.tensor_tensor(to1[doc % 2][:, :], y2r[doc % 2][:, :],
                                                muB[:, :], ALU.subtract),
             waits=[("y2r", doc), "dRb"] + ([("outd", doc - 2)] if doc >= 2 else []))
        P.op(b, "vector",
             lambda e, doc=doc: e.tensor_tensor(to1[doc % 2][:, :], to1[doc % 2][:, :],
                                                RB[:, :], ALU.mult),
             mark=("outv", doc))
        P.op(b, "sync",
             lambda e, doc=doc: e.dma_start(out[doc * 128:(doc + 1) * 128, :],
                                            to1[doc % 2][:, :]),
             sem=f"oD{doc % 2}", k=16, mark=("outd", doc), waits=[("outv", doc)])
    P.op(b, "sync", lambda e: e.nop(), sem="dX", waits=[("outd", 14), ("outd", 15)])

    s9.close()
    P.resolve()
    P.emit(nc, sems)
    gs.close()
    return nc


_NC_CACHE = {}


def _get_nc():
    if "nc" not in _NC_CACHE:
        _NC_CACHE["nc"] = build_program()
    return _NC_CACHE["nc"]


def kernel(L0_attn_x, L0_attn_wq_codes, L0_attn_wq_scales, L0_attn_bq,
           L0_attn_wk_codes, L0_attn_wk_scales, L0_attn_bk,
           L0_attn_wv_codes, L0_attn_wv_scales, L0_attn_bv,
           L0_attn_wo_codes, L0_attn_wo_scales, L0_attn_bo,
           L0_ffn_w_fc, L0_ffn_w_proj):
    x = np.asarray(L0_attn_x, dtype=np.float32)
    B = x.shape[0]
    rep = lambda s: np.ascontiguousarray(np.repeat(np.asarray(s, np.float32), 32, axis=0))
    bias_t = lambda v: np.ascontiguousarray(np.asarray(v, np.float32).reshape(16, 128).T)
    common = {
        "wq": np.ascontiguousarray(np.asarray(L0_attn_wq_codes, np.float32)),
        "wk": np.ascontiguousarray(np.asarray(L0_attn_wk_codes, np.float32)),
        "wv": np.ascontiguousarray(np.asarray(L0_attn_wv_codes, np.float32)),
        "wo": np.ascontiguousarray(np.asarray(L0_attn_wo_codes, np.float32)),
        "sq": rep(L0_attn_wq_scales), "sk": rep(L0_attn_wk_scales),
        "sv": rep(L0_attn_wv_scales), "so": rep(L0_attn_wo_scales),
        "bq": bias_t(L0_attn_bq), "bk": bias_t(L0_attn_bk),
        "bv": bias_t(L0_attn_bv), "bo": bias_t(L0_attn_bo),
        "wfc": np.ascontiguousarray(np.asarray(L0_ffn_w_fc, np.float32).T),
        "wpr": np.ascontiguousarray(np.asarray(L0_ffn_w_proj, np.float32).T),
    }
    in_maps = []
    for c in range(8):
        bb, half = c // 2, c % 2
        xT = x[bb].T
        xr = np.ascontiguousarray(np.concatenate(
            [xT[:, half * Q:], xT[:, :half * Q]], axis=1))
        m = dict(common)
        m["xt"] = xr
        m["xq"] = np.ascontiguousarray(xr[:, :Q])
        in_maps.append(m)
    nc = _get_nc()
    res = run_bass_kernel_spmd(nc, in_maps, core_ids=list(range(8)))
    y_out = np.empty((B, S, D), dtype=np.float32)
    for c in range(8):
        bb, half = c // 2, c % 2
        y_out[bb, half * Q:(half + 1) * Q, :] = np.asarray(res.results[c]["out"]).T
    return y_out


# version 13
# speedup vs baseline: 1.0875x; 1.0062x over previous
import numpy as np
import ml_dtypes
from contextlib import ExitStack
import concourse.bass as bass
import concourse.mybir as mybir
from concourse.bass_utils import run_bass_kernel_spmd

F32 = mybir.dt.float32
BF16 = mybir.dt.bfloat16
FP8 = mybir.dt.float8e4
AF = mybir.ActivationFunctionType
ALU = mybir.AluOpType
AX = mybir.AxisListType

D = 2048
S = 2048
Q = 1024
H = 16
FF = 8192
NW = float(D * FF)
ISQ = float(1.0 / np.sqrt(128.0))
EPS1 = 1e-5
EPS2 = 1e-6
CLIPB = float((128.0 - 1e-6) / 128.0)

SB_TOP = 229344
QT_R = (SB_TOP - 32768, SB_TOP)
XH_R = (SB_TOP - 32768, SB_TOP)
AT_R = (SB_TOP - 65536, SB_TOP - 32768)
PT_R = [(SB_TOP - 131072 + i * 16384, SB_TOP - 131072 + (i + 1) * 16384)
        for i in range(4)]
Y_R = (SB_TOP - 131072, SB_TOP - 65536)
X2_R = (SB_TOP - 131072, SB_TOP)

ENGS = ["sync", "tensor", "vector", "scalar", "gpsimd"]
SEM_OF = {"sync": "dS", "tensor": "tsem", "vector": "vsem", "scalar": "ssem",
          "gpsimd": "gsem"}


class Prog:
    def __init__(self):
        self.blocks = []
        self.ev = {}

    def block(self):
        b = {e: [] for e in ENGS}
        self.blocks.append(b)
        return b

    def op(self, b, eng, fn, *, mark=None, sem=None, k=1, waits=()):
        b[eng].append((fn, mark, sem or SEM_OF[eng], k, tuple(waits)))

    def resolve(self):
        self.ev.clear()
        self.drains = {}
        self.selfw = {}
        self.blockstart = {}
        cnt = {}
        for bi, b in enumerate(self.blocks):
            self.blockstart[bi] = [(s, cnt.get(s, 0))
                                   for s in ("tsem", "vsem", "ssem", "gsem")
                                   if cnt.get(s, 0)]
            for eng in ENGS:
                seen = {}
                own = SEM_OF[eng]
                for oi, (fn, mark, sem, k, waits) in enumerate(b[eng]):
                    if eng in ("vector", "scalar", "gpsimd"):
                        self.selfw[(bi, eng, oi)] = cnt.get(own, 0)
                    cnt[sem] = cnt.get(sem, 0) + k
                    if mark is not None:
                        assert mark not in self.ev, f"duplicate mark {mark}"
                        self.ev[mark] = (sem, cnt[sem])
                    if sem not in ("tsem", "vsem", "ssem", "gsem"):
                        seen[sem] = cnt[sem]
                if seen:
                    self.drains[(bi, eng)] = sorted(seen.items())

    def emit(self, nc, sems):
        for bi, b in enumerate(self.blocks):
            with nc.Block() as blk:
                for eng in ENGS:
                    ops = b[eng]
                    if not ops:
                        continue
                    drains = self.drains.get((bi, eng), [])

                    def body(e, ops=ops, drains=drains, bi=bi, eng=eng):
                        own = sems.get(SEM_OF[eng])
                        for s_, v_ in self.blockstart.get(bi, []):
                            e.wait_ge(sems[s_], v_)
                        for oi, (fn, mark, sem, k, waits) in enumerate(ops):
                            sv = self.selfw.get((bi, eng, oi))
                            if sv:
                                e.wait_ge(own, sv)
                            for w in waits:
                                if w in self.ev:
                                    ws, wc = self.ev[w]
                                    e.wait_ge(sems[ws], wc)
                            ins = fn(e)
                            assert ins is not None
                            ins.then_inc(sems[sem], k)
                        for s_, v_ in drains:
                            e.wait_ge(sems[s_], v_)
                    getattr(blk, eng)(body)


def build_program(debug=False):
    nc = bass.Bass("TRN2", target_bir_lowering=False)
    gs = ExitStack()
    P = Prog()

    xt = nc.dram_tensor("xt", [D, S], F32, kind="ExternalInput")
    xq = nc.dram_tensor("xq", [D, Q], F32, kind="ExternalInput")
    wq = nc.dram_tensor("wq", [D, D], F32, kind="ExternalInput")
    wk = nc.dram_tensor("wk", [D, D], F32, kind="ExternalInput")
    wv = nc.dram_tensor("wv", [D, D], F32, kind="ExternalInput")
    wo = nc.dram_tensor("wo", [D, D], F32, kind="ExternalInput")
    sq = nc.dram_tensor("sq", [D, D], F32, kind="ExternalInput")
    sk = nc.dram_tensor("sk", [D, D], F32, kind="ExternalInput")
    sv = nc.dram_tensor("sv", [D, D], F32, kind="ExternalInput")
    so = nc.dram_tensor("so", [D, D], F32, kind="ExternalInput")
    bqd = nc.dram_tensor("bq", [128, 16], F32, kind="ExternalInput")
    bkd = nc.dram_tensor("bk", [128, 16], F32, kind="ExternalInput")
    bvd = nc.dram_tensor("bv", [128, 16], F32, kind="ExternalInput")
    bod = nc.dram_tensor("bo", [128, 16], F32, kind="ExternalInput")
    wfc = nc.dram_tensor("wfc", [D, FF], F32, kind="ExternalInput")
    wpr = nc.dram_tensor("wpr", [FF, D], F32, kind="ExternalInput")
    out = nc.dram_tensor("out", [D, Q], F32, kind="ExternalOutput")

    kd = {"kind": "ExternalOutput"} if debug else {}
    kspill = nc.dram_tensor("kspill", [D, S], BF16, **kd)
    vspill = nc.dram_tensor("vspill", [H, S, 128], BF16, **kd)
    tproj = nc.dram_tensor("tproj", [FF, D], FP8, **kd)
    hspill = nc.dram_tensor("hspill", [FF, Q], BF16, **kd)
    xnspill = nc.dram_tensor("xnspill", [D, Q], F32, **kd)
    y2spill = nc.dram_tensor("y2spill", [D, Q], F32, **kd)

    semnames = (["tsem", "vsem", "ssem", "gsem", "ost", "dX", "gX", "kSp", "xR"]
                + [f"{p}{i}" for p in ["gW", "vT", "vP", "kR", "vR", "tP",
                                       "xQ", "oW", "hW", "tW", "hR",
                                       "tR", "yW", "yR", "oD"] for i in range(2)]
                + [f"{p}{i}" for p in ["fW", "pW", "gG", "gS"] for i in range(4)])
    sems = {n: gs.enter_context(nc.semaphore(n)) for n in semnames}

    def sb(name, shape, dt, scope, **kw):
        return scope.enter_context(nc.sbuf_tensor(name, shape, dt, **kw))

    def psum(name, shape, scope):
        return scope.enter_context(nc.psum_tensor(name, shape, F32))

    ones_bf = sb("ones_bf", [128, 1], BF16, gs)
    ones_f = sb("ones_f", [128, 1], F32, gs)
    onesr_f = sb("onesr_f", [1, 128], F32, gs)
    id128 = sb("id128", [128, 128], F32, gs)
    id128b = sb("id128b", [128, 128], BF16, gs)
    sel8 = sb("sel8", [8, 1024], F32, gs)
    bq_s = sb("bq_s", [128, 16], F32, gs)
    bk_t = sb("bk_t", [128, 16], F32, gs)
    bv_t = sb("bv_t", [128, 16], F32, gs)
    bo_t = sb("bo_t", [128, 16], F32, gs)
    thf = sb("thf", [128, 1], F32, gs)
    nthf = sb("nthf", [128, 1], F32, gs)
    thp = sb("thp", [128, 1], F32, gs)
    nthp = sb("nthp", [128, 1], F32, gs)
    gfb = sb("gfb", [128, 1], F32, gs)
    gpb = sb("gpb", [128, 1], F32, gs)
    gacc_f = sb("gacc_f", [128, 64], F32, gs)
    gacc_p = sb("gacc_p", [128, 64], F32, gs)
    g1 = sb("g1", [1, 16], F32, gs)
    sums = None
    stn = ["sums1", "sumsq1", "mu", "var", "R", "rho", "gx", "cx", "ncx",
           "mu2", "R2", "c2", "nc2", "mu3", "R3", "tmp1", "tmp2", "tmp3",
           "mx", "mn"]
    stt = {n: sb("st_" + n, [128, 8], F32, gs) for n in stn}
    muB = sb("muB", [128, Q], F32, gs)
    RB = sb("RB", [128, Q], F32, gs)
    cB = sb("cB", [128, Q], BF16, gs)
    ncB = sb("ncB", [128, Q], BF16, gs)
    rhoB = sb("rhoB", [128, Q], F32, gs)
    rmax = sb("rmax", [128, Q], BF16, gs)
    rmin = sb("rmin", [128, Q], BF16, gs)
    rminf = rmin.bitcast(F32)
    sums = rminf[:, 336:464]
    trT = sb("trT", [8, 128], F32, gs)
    arena1 = sb("arena1", [128, 16384], F32, gs)
    arena2 = sb("arena2", [128, 16384], BF16, gs)
    arena3 = sb("arena3", [128, 16384], BF16, gs)
    a1b = arena1.bitcast(BF16)
    a2f = arena2.bitcast(F32)
    Qt = arena3.ap().rearrange("p (a c) -> p a c", a=16)
    attn = arena2.ap().rearrange("p (a c) -> p a c", a=16)
    PT = [a1b[:, i * 8192:(i + 1) * 8192].rearrange("p (kc i) -> p kc i", kc=16)
          for i in range(4)]
    y = arena1.ap().rearrange("p (a c) -> p a c", a=16)
    xh = arena3.ap().rearrange("p (a c) -> p a c", a=16)
    x2parts = [a1b.rearrange("p (a c) -> p a c", a=32),
               arena2.ap().rearrange("p (a c) -> p a c", a=16),
               arena3.ap().rearrange("p (a c) -> p a c", a=16)]

    def x2w(oc0):
        if oc0 < 32:
            return x2parts[0][:, oc0:oc0 + 2, :]
        if oc0 < 48:
            return x2parts[1][:, oc0 - 32:oc0 - 30, :]
        return x2parts[2][:, oc0 - 48:oc0 - 46, :]

    def x2sl(oc, c0, c1):
        if oc < 32:
            return x2parts[0][:, oc, c0:c1]
        if oc < 48:
            return x2parts[1][:, oc - 32, c0:c1]
        return x2parts[2][:, oc - 48, c0:c1]

    b = P.block()
    for dst, src in [(bq_s, bqd), (bk_t, bkd), (bv_t, bvd), (bo_t, bod)]:
        P.op(b, "sync", lambda e, d=dst, s=src: e.dma_start(d[:, :], s[:, :]),
             sem="dX", k=16, mark=("bld", dst.name))
    for t_ in [ones_f, onesr_f, ones_bf, id128]:
        P.op(b, "vector", lambda e, t_=t_: e.memset(t_[:, :], 1.0))
    P.op(b, "vector", lambda e: e.memset(sel8[:, :], 1.0), mark="consts")
    P.op(b, "vector",
         lambda e: e.tensor_scalar(bq_s[:, :], bq_s[:, :], ISQ, None, ALU.mult),
         waits=[("bld", "bo_t")], mark="bqs")
    P.op(b, "vector", lambda e: e.memset(id128b[:, :], 1.0), mark="constsb")
    P.op(b, "gpsimd",
         lambda e: e.affine_select(id128[:, :], id128[:, :], [[1, 128]],
                                   ALU.is_equal, 0.0, base=0, channel_multiplier=-1),
         waits=["consts", "constsb"], mark="id128")
    P.op(b, "gpsimd",
         lambda e: e.affine_select(id128b[:, :], id128b[:, :], [[1, 128]],
                                   ALU.is_equal, 0.0, base=0, channel_multiplier=-1),
         mark="id128b")
    P.op(b, "gpsimd",
         lambda e: e.affine_select(sel8.ap().rearrange("p (b c) -> p b c", b=8),
                                   sel8.ap().rearrange("p (b c) -> p b c", b=8),
                                   [[1, 8], [0, 128]], ALU.is_equal, 0.0,
                                   base=0, channel_multiplier=-1),
         mark="sel8")

    s1 = ExitStack()
    x_fm = a1b.rearrange("p (a c) -> p a c", a=16)
    kev = sb("kev", [128, 4, 512], BF16, s1)
    ones8c = sb("ones8c", [128, 8], BF16, s1)
    ones8r = sb("ones8r", [8, 128], BF16, s1)
    mk2b8 = sb("mk2b8", [8, 16], BF16, s1)
    k2c = sb("k2c", [128, 2, 512], BF16, s1)
    kmx8 = rminf
    q2c = sb("q2c", [128, 8, 128], BF16, s1)
    ckb = [sb(f"ckb{i}", [128, 16, 128], BF16, s1) for i in range(2)]
    skb = [sb(f"skb{i}", [128, 16, 128], BF16, s1) for i in range(2)]
    wkb = [sb(f"wkb{i}", [128, 16, 128], BF16, s1) for i in range(2)]
    vrow = [sb(f"vrow{i}", [128, 4, 512], BF16, s1) for i in range(2)]
    vtrb = [sb(f"vtrb{i}", [128, 16, 128], BF16, s1, is_transpose_dest=True)
            for i in range(2)]
    pk = psum("pk", [128, 4, 512], s1)
    pkn = psum("pkn", [8, 2, 512], s1)

    def wcon(mi):
        if mi < 16:
            return ("pkg", mi * 4 + 3)
        if mi < 32:
            return ("pkg", 64 + (mi - 16) * 4 + 3)
        return ("pkg", 128 + (mi - 32) * 2 + 1)

    b = P.block()
    P.op(b, "gpsimd",
         lambda e: e.dma_start(x_fm,
                               xt.ap().rearrange("(dc p) t -> p dc t", p=128)),
         sem="gX", k=16, mark="x_fm")
    mats = [(wk, sk), (wv, sv), (wq, sq)]
    wdst = {"i": 0}

    def emit_wd(upto):
        while wdst["i"] <= min(upto, 47):
            mi = wdst["i"]
            m, fc = divmod(mi, 16)
            cod, scl = mats[m]
            wt = [wcon(mi - 2)] if mi >= 2 else []
            P.op(b, "gpsimd",
                 lambda e, cod=cod, fc=fc, mi=mi: e.dma_start(
                     ckb[mi % 2].ap(),
                     cod[:, fc * 128:(fc + 1) * 128]
                     .rearrange("(dc p) f -> p dc f", p=128)),
                 sem=f"gW{mi % 2}", k=16, waits=wt)
            P.op(b, "gpsimd",
                 lambda e, scl=scl, fc=fc, mi=mi: e.dma_start(
                     skb[mi % 2].ap(),
                     scl[:, fc * 128:(fc + 1) * 128]
                     .rearrange("(dc p) f -> p dc f", p=128)),
                 sem=f"gW{mi % 2}", k=16, mark=("wdma", mi))
            wdst["i"] += 1
    wbst = {"i": 0}

    def emit_wb(upto):
        while wbst["i"] <= min(upto, 47):
            mi = wbst["i"]
            wt = [("wdma", mi)] + ([wcon(mi - 2)] if mi >= 2 else [])
            P.op(b, "vector",
                 lambda e, mi=mi: e.tensor_tensor(
                     wkb[mi % 2].ap().rearrange("p a c -> p (a c)"),
                     ckb[mi % 2].ap().rearrange("p a c -> p (a c)"),
                     skb[mi % 2].ap().rearrange("p a c -> p (a c)"), ALU.mult),
                 mark=("wbuild", mi), waits=wt)
            wbst["i"] += 1
    P.op(b, "vector", lambda e: e.memset(ones8c[:, :], 1.0))
    P.op(b, "vector", lambda e: e.memset(ones8r[:, :], 1.0), mark="ones8")
    emit_wd(2)
    emit_wb(1)

    groups = []
    for fc in range(16):
        for th in range(4):
            groups.append((len(groups), fc, th * 512))
    for fc in range(16):
        for th in range(4):
            groups.append((len(groups), 16 + fc, th * 512))
    for fc in range(16):
        for th in range(2):
            groups.append((len(groups), 32 + fc, th * 512))

    for n, mi, tok0 in groups:
        def mmg(e, n=n, mi=mi, tok0=tok0):
            ins = None
            for dc in range(16):
                ins = e.matmul(pk[:, n % 4, :], wkb[mi % 2][:, dc, :],
                               x_fm[:, dc, tok0:tok0 + 512],
                               start=(dc == 0), stop=(dc == 15))
            return ins
        wt = ["x_fm", ("wbuild", mi)]
        if n >= 4:
            wt.append(("pkev", n - 4))
        P.op(b, "tensor", mmg, mark=("pkg", n), waits=wt)
        j = n - 1
        if 0 <= j < 64:
            wtk = [("k2q", j), "ones8"] + ([("kmxr", j - 2)] if j >= 2 else [])
            P.op(b, "tensor",
                 lambda e, j=j: e.matmul(pkn[:, j % 2, :], ones8c[:, :],
                                         k2c[:, j % 2, :], start=True, stop=True),
                 mark=("pknm", j), waits=wtk)

    gplan = [("f", i) for i in range(64)] + [("p", i) for i in range(64)]

    a3f = arena3.bitcast(F32)

    def g_src(j):
        which, i = gplan[j]
        if which == "f":
            nn, mm_ = divmod(i, 4)
            return wfc.ap().rearrange("(n p) (m f) -> n m p f", p=128, f=2048)[nn, mm_]
        return wpr.ap().rearrange("(n p) f -> n p f", p=128)[i]

    def g_buf(j):
        if j < 40:
            t = j % 4
            return a2f[:, t * 2048:(t + 1) * 2048]
        if j < 64:
            t = (j - 40) % 3
            return a2f[:, t * 2048:(t + 1) * 2048]
        if j < 96:
            t = j % 4
            return a3f[:, t * 2048:(t + 1) * 2048]
        return a3f[:, (2 + j % 2) * 2048:(3 + j % 2) * 2048]

    def g_prev(j):
        if j >= 96:
            return j - 2
        if j >= 43:
            return j - 3 if j < 64 else j - 4
        return j - 4

    def emit_gdma(j, eng):
        wt = [("gtile", g_prev(j))] if g_prev(j) >= 0 else []
        fam = "gG" if eng == "gpsimd" else "gS"
        P.op(b, eng, lambda e, j=j: e.dma_start(g_buf(j), g_src(j)),
             sem=f"{fam}{j % 4}", k=16, mark=("gdma", j), waits=wt)

    def emit_gred_v(j):
        which, i = gplan[j]
        acc = gacc_f if which == "f" else gacc_p
        P.op(b, "vector",
             lambda e, j=j, acc=acc, i=i: e.reduce_sum(
                 acc[:, i:i + 1], g_buf(j), axis=AX.X,
                 apply_absolute_value=True),
             mark=("gtile", j), waits=[("gdma", j)])

    def emit_gred_s(j):
        which, i = gplan[j]
        acc = gacc_f if which == "f" else gacc_p
        P.op(b, "scalar",
             lambda e, j=j, acc=acc, i=i: e.activation(
                 g_buf(j), g_buf(j), AF.Abs, accum_out=acc[:, i:i + 1]),
             mark=("gtile", j), waits=[("gdma", j)])

    gfst = {"d": 0, "r": 0}

    def g_dma_upto(upto):
        while gfst["d"] < min(upto, 40):
            emit_gdma(gfst["d"], "sync")
            gfst["d"] += 1

    def g_red_upto(upto):
        while gfst["r"] < min(upto, 40):
            emit_gred_v(gfst["r"])
            gfst["r"] += 1
    KMX = 7424
    MKH = 7488
    MKB = 7504
    MH2 = 7168
    MHP = 7296
    NMH = 8064
    for n in range(64):
        fc, th = divmod(n, 4)
        wt = [("pkg", n)]
        if th == 0 and fc >= 1:
            wt.append(("kspill", fc - 1))
        wtq = [("pkg", n)] + ([("pknm", n - 2)] if n >= 2 else [])
        P.op(b, "scalar",
             lambda e, fc=fc, th=th, n=n: e.activation(
                 k2c[:, n % 2, :], pk[:, n % 4, :], AF.Square,
                 bias=bk_t[:, fc:fc + 1]),
             mark=("k2q", n), waits=wtq)
        P.op(b, "scalar",
             lambda e, fc=fc, th=th, n=n: e.activation(
                 kev[:, th, :], pk[:, n % 4, :], AF.Identity, bias=bk_t[:, fc:fc + 1]),
             mark=("pkev", n), waits=wt)
        emit_wb(fc + 1)
        P.op(b, "vector",
             lambda e, n=n: e.reduce_max(kmx8[0:8, 128 + n:129 + n],
                                         pkn[0:8, n % 2, :], axis=AX.X),
             mark=("kmxr", n), waits=[("pknm", n)])
        if th == 3:
            P.op(b, "vector",
                 lambda e, fc=fc: e.reduce_max(kmx8[0:8, 192 + fc:193 + fc],
                                               kmx8[0:8, 128 + 4 * fc:132 + 4 * fc],
                                               axis=AX.X),
                 mark=("mkh", fc))
        if th == 3:
            P.op(b, "sync",
                 lambda e, fc=fc: e.dma_start(
                     kspill[fc * 128:(fc + 1) * 128, :],
                     kev.rearrange("p a c -> p (a c)")),
                 sem="kSp", k=16, mark=("kspill", fc), waits=[("pkev", fc * 4 + 3)])
        emit_wd(n // 4 + 3)
        g_dma_upto(n // 2 + 1)
        g_red_upto(n // 2 - 2)
    for n in range(64, 128):
        fc, th = divmod(n - 64, 4)
        wt = [("pkg", n)]
        if th == 0 and fc >= 2:
            wt.append(("vtr", fc - 2))
        P.op(b, "scalar",
             lambda e, fc=fc, th=th, n=n: e.activation(
                 vrow[fc % 2][:, th, :], pk[:, n % 4, :], AF.Identity,
                 bias=bv_t[:, fc:fc + 1]),
             mark=("pkev", n), waits=wt)
        emit_wd((n - 64) // 4 + 19)
        emit_wb((n - 64) // 4 + 17)
        g_dma_upto(32 + (n - 64) // 2)
        g_red_upto(30 + (n - 64) // 2)
        if th == 3:
            wt2 = [("pkev", n)] + ([("vsp", fc - 2)] if fc >= 2 else [])
            P.op(b, "sync",
                 lambda e, fc=fc: e.dma_start(
                     vtrb[fc % 2].ap(),
                     vrow[fc % 2].ap().rearrange("p a c -> p (a c)")
                     .rearrange("p (tc i) -> p tc i", tc=16),
                     transpose=True),
                 sem=f"vT{fc % 2}", k=16, mark=("vtr", fc), waits=wt2)
            P.op(b, "sync",
                 lambda e, fc=fc: e.dma_start(
                     vspill[fc].rearrange("(tc p) i -> p tc i", p=128),
                     vtrb[fc % 2].ap()),
                 sem=f"vP{fc % 2}", k=16, mark=("vsp", fc), waits=[("vtr", fc)])
    for n in range(128, 160):
        fc, th = divmod(n - 128, 2)
        P.op(b, "scalar",
             lambda e, fc=fc, th=th, n=n: e.activation(
                 Qt[:, fc, th * 512:(th + 1) * 512], pk[:, n % 4, :],
                 AF.Identity, scale=ISQ, bias=bq_s[:, fc:fc + 1]),
             mark=("pkev", n), waits=[("pkg", n)])
        emit_wd((n - 128) // 2 + 35)
        emit_wb((n - 128) // 2 + 33)
        g_red_upto(34 + (n - 128) // 2)
    emit_wd(47)
    emit_wb(47)
    g_red_upto(40)


    s2 = ExitStack()
    pB2 = psum("pB2", [128, 512], s2)
    pqn = psum("pqn", [128, 128], s2)
    b = P.block()
    P.op(b, "vector", lambda e: e.tensor_copy(mk2b8[:, :], kmx8[0:8, 192:208]),
         mark="mk2b")
    P.op(b, "tensor",
         lambda e: e.matmul(pB2[:, 256:272], ones8r[:, :], mk2b8[:, :],
                            start=True, stop=True),
         mark="mkbc", waits=["mk2b"])
    P.op(b, "scalar", lambda e: e.copy(a2f[:, MKB:MKB + 16], pB2[:, 256:272]),
         mark="mkev", waits=["mkbc"])
    for j2 in range(128):
        h_, qb_ = divmod(j2, 8)
        P.op(b, "vector",
             lambda e, j2=j2, h_=h_, qb_=qb_: e.tensor_tensor(
                 q2c[:, j2 % 8, :], Qt[:, h_, qb_ * 128:(qb_ + 1) * 128],
                 Qt[:, h_, qb_ * 128:(qb_ + 1) * 128], ALU.mult),
             mark=("q2c", j2), waits=[("qnm", j2 - 8)] if j2 >= 8 else [])
        P.op(b, "tensor",
             lambda e, j2=j2: e.matmul(pqn[:, j2:j2 + 1], q2c[:, j2 % 8, :],
                                       ones_bf[:, :], start=True, stop=True),
             mark=("qnm", j2), waits=[("q2c", j2)])
        if j2 % 5 == 0 and j2 // 5 < 24:
            emit_gdma(40 + j2 // 5, "gpsimd")
            if j2 // 5 >= 2:
                emit_gred_v(38 + j2 // 5)
    for h_ in range(16):
        P.op(b, "vector",
             lambda e, h_=h_: e.tensor_scalar(
                 a2f[:, MH2 + h_ * 8:MH2 + h_ * 8 + 8], pqn[:, h_ * 8:h_ * 8 + 8],
                 a2f[:, MKB + h_:MKB + h_ + 1], None, ALU.mult),
             mark=("mh2", h_), waits=[("qnm", h_ * 8 + 7)] + (["mkev"] if h_ == 0 else []))
    P.op(b, "scalar",
         lambda e: e.activation(a2f[:, MHP:MHP + 128], a2f[:, MH2:MH2 + 128], AF.Sqrt),
         mark="mhs", waits=[("mh2", 15)])
    P.op(b, "vector",
         lambda e: e.tensor_scalar(a2f[:, NMH:NMH + 128], a2f[:, MHP:MHP + 128],
                                   -1.0, None, ALU.mult),
         mark="nmh", waits=["mhs"])
    for j in range(62, 64):
        emit_gred_v(j)

    s2.close()
    s1.close()

    s3 = ExitStack()
    kslb = [sb(f"kslb{i}", [128, S], BF16, s3) for i in range(2)]
    vslb = [sb(f"vslb{i}", [128, 16, 128], BF16, s3) for i in range(2)]
    Pt = [sb(f"Pt{i}", [128, S], BF16, s3) for i in range(2)]
    sums2 = rminf
    invT = rminf
    Sinv = sb("Sinv", [128, 8], F32, s3)
    invb = [sb(f"invb{i}", [128, Q], F32, s3) for i in range(2)]
    pscore = psum("pscore", [128, 2048], s3)
    pav = psum("pav", [128, 2, 512], s3)
    pinvb = psum("pinvb", [128, Q], s3)

    b = P.block()

    def emit_ptr(h, qb):
        qh = 0 if qb < 4 else 1
        wt2 = []
        if h >= 2 and qb % 4 == 0:
            wt2.append(("avmm", h - 2, qh))

        def ptrf(e, h=h, qb=qb, qh=qh):
            return e.dma_start(
                PT[(h % 2) * 2 + qh][:, :, (qb % 4) * 128:(qb % 4 + 1) * 128],
                Pt[qb % 2].ap().rearrange("p (kc i) -> p kc i", kc=16),
                transpose=True)
        P.op(b, "sync", ptrf, sem=f"tP{qb % 2}", k=16, mark=("ptr", h, qb),
             waits=[("exB", h, qb)] + wt2)

    for h in range(H):
        if h >= 1:
            for qb in range(4):
                emit_ptr(h - 1, qb)
        P.op(b, "sync",
             lambda e, h=h: e.dma_start(kslb[h % 2][:, :],
                                        kspill[h * 128:(h + 1) * 128, :]),
             sem=f"kR{h % 2}", k=16, mark=("ksl", h),
             waits=[("smB", h - 2, 7)] if h >= 2 else [])
        if h >= 1:
            for qb in range(4, 8):
                emit_ptr(h - 1, qb)
        P.op(b, "sync",
             lambda e, h=h: e.dma_start(
                 vslb[h % 2].ap(),
                 vspill[h].rearrange("(tc p) i -> p tc i", p=128)),
             sem=f"vR{h % 2}", k=16, mark=("vsl", h),
             waits=[("avmm", h - 2, 1)] if h >= 2 else [])
    for qb in range(8):
        emit_ptr(H - 1, qb)

    def emit_inv_tr(h):
        wt = [("recip", h)] + ([("invev", h - 1)] if h >= 1 else ["id128"])
        P.op(b, "tensor",
             lambda e: e.transpose(pinvb[0:8, 0:128], Sinv[:, :], id128[:, :]),
             mark=("invtr", h), waits=wt)

    def emit_inv_bc(h):
        def f(e):
            ins = None
            for qb in range(8):
                ins = e.matmul(pinvb[:, qb * 128:(qb + 1) * 128],
                               sel8[:, qb * 128:(qb + 1) * 128], invT[0:8, 208:336],
                               start=True, stop=True)
            return ins
        P.op(b, "tensor", f, mark=("invbc", h), waits=[("invTe", h)])

    def emit_av(hp, qh, kc0, kc1, markkey=None):
        def f(e, hp=hp, qh=qh, kc0=kc0, kc1=kc1):
            ins = None
            for kc in range(kc0, kc1):
                ins = e.matmul(pav[:, qh, :], vslb[hp % 2][:, kc, :],
                               PT[(hp % 2) * 2 + qh][:, kc, :],
                               start=(kc == 0), stop=(kc == 15))
            return ins
        wt = []
        if kc0 == 0:
            wt = [("ptr", hp, qh * 4 + 2), ("ptr", hp, qh * 4 + 3), ("vsl", hp)]
            if hp >= 1:
                wt.append(("avev", hp - 1, qh))
        P.op(b, "tensor", f, mark=markkey, waits=wt)

    for h in range(H):
        for qb in range(8):
            pq = (h, qb - 1) if qb > 0 else (h - 1, 7)

            def smmh(e, h=h, qb=qb, j0=0):
                ins = None
                for j in range(j0, j0 + 2):
                    ins = e.matmul(pscore[:, j * 512:(j + 1) * 512],
                                   Qt[:, h, qb * 128:(qb + 1) * 128],
                                   kslb[h % 2][:, j * 512:(j + 1) * 512],
                                   start=True, stop=True)
                return ins
            wtA = [("ksl", h)]
            wtA.append(("exA",) + pq if (h, qb) != (0, 0) else ("pkev", 159))
            P.op(b, "tensor", lambda e, f=smmh: f(e, j0=0), mark=("smA", h, qb),
                 waits=wtA)
            if h >= 1:
                hp, qh = h - 1, (0 if qb < 4 else 1)
                kc0 = (qb % 4) * 4
                emit_av(hp, qh, kc0, kc0 + 4,
                        markkey=("avmm", hp, qh) if qb % 4 == 3 else None)
            wtB = [("exB",) + pq] if (h, qb) != (0, 0) else []
            P.op(b, "tensor", lambda e, f=smmh: f(e, j0=2), mark=("smB", h, qb),
                 waits=wtB)
            if h >= 1 and qb == 0:
                emit_inv_tr(h - 1)
            if h >= 1 and qb == 1:
                emit_inv_bc(h - 1)
    hl = H - 1
    emit_inv_tr(hl)
    emit_inv_bc(hl)
    for qh in range(2):
        emit_av(hl, qh, 0, 16, markkey=("avmm", hl, qh))

    for h in range(H):
        if h >= 1:
            P.op(b, "vector",
                 lambda e, h=h: e.tensor_copy(invb[(h - 1) % 2][:, :], pinvb[:, :]),
                 mark=("invev", h - 1), waits=[("invbc", h - 1)])
            for qh in range(2):
                P.op(b, "vector",
                     lambda e, h=h, qh=qh: e.tensor_tensor(
                         attn[:, h - 1, qh * 512:(qh + 1) * 512], pav[:, qh, :],
                         invb[(h - 1) % 2][:, qh * 512:(qh + 1) * 512], ALU.mult),
                     mark=("avev", h - 1, qh),
                     waits=[("avmm", h - 1, qh), ("invev", h - 1)])
        P.op(b, "vector",
             lambda e, h=h: e.tensor_tensor(sums[:, h * 8:(h + 1) * 8],
                                            sums[:, h * 8:(h + 1) * 8],
                                            sums2[:, h * 8:(h + 1) * 8], ALU.add),
             mark=("scomb", h), waits=[("exB", h, 7)])
        P.op(b, "vector",
             lambda e, h=h: e.reciprocal(Sinv[:, :], sums[:, h * 8:(h + 1) * 8]),
             mark=("recip", h), waits=[("scomb", h)])

    P.op(b, "vector",
         lambda e: e.tensor_copy(invb[hl % 2][:, :], pinvb[:, :]),
         mark=("invev", hl), waits=[("invbc", hl)])
    for qh in range(2):
        P.op(b, "vector",
             lambda e, qh=qh: e.tensor_tensor(
                 attn[:, hl, qh * 512:(qh + 1) * 512], pav[:, qh, :],
                 invb[hl % 2][:, qh * 512:(qh + 1) * 512], ALU.mult),
             mark=("avev", hl, qh),
             waits=[("avmm", hl, qh), ("invev", hl)])

    for h in range(H):
        for qb in range(8):
            wt = [("smA", h, qb)]
            if h * 8 + qb >= 2:
                pq = (h, qb - 2) if qb >= 2 else (h - 1, 6 + qb)
                wt.append(("ptr",) + pq)
            j2 = h * 8 + qb
            P.op(b, "scalar",
                 lambda e, h=h, qb=qb, j2=j2: e.activation(
                     Pt[qb % 2][:, 0:1024], pscore[:, 0:1024], AF.Exp,
                     bias=a2f[:, NMH + j2:NMH + j2 + 1],
                     accum_out=sums[:, j2:j2 + 1]),
                 mark=("exA", h, qb), waits=wt)
            P.op(b, "scalar",
                 lambda e, h=h, qb=qb, j2=j2: e.activation(
                     Pt[qb % 2][:, 1024:2048], pscore[:, 1024:2048], AF.Exp,
                     bias=a2f[:, NMH + j2:NMH + j2 + 1],
                     accum_out=sums2[:, j2:j2 + 1]),
                 mark=("exB", h, qb), waits=[("smB", h, qb)])
        P.op(b, "scalar", lambda e: e.copy(invT[0:8, 208:336], pinvb[0:8, 0:128]),
             mark=("invTe", h), waits=[("invtr", h)])

    s3.close()

    s4 = ExitStack()
    cob = [sb(f"cob{i}", [128, 16, 128], BF16, s4) for i in range(2)]
    sob = [sb(f"sob{i}", [128, 16, 128], BF16, s4) for i in range(2)]
    wow = [sb(f"wow{i}", [128, 16, 128], BF16, s4) for i in range(2)]
    xqt = [sb(f"xqt{i}", [128, Q], F32, s4) for i in range(2)]
    t1t = [sb(f"t1t{i}", [128, Q], F32, s4) for i in range(2)]
    ybf = [sb(f"ybf{i}", [128, Q], BF16, s4) for i in range(2)]
    yb2 = sb("yb2", [128, Q], BF16, s4)
    poc = psum("poc", [128, 2, 2, 512], s4)
    pst1 = psum("pst1", [128, 512], s4)
    pst2 = psum("pst2", [128, 512], s4)

    b = P.block()
    for doc in range(16):
        P.op(b, "gpsimd",
             lambda e, doc=doc: e.dma_start(
                 cob[doc % 2].ap(),
                 wo[:, doc * 128:(doc + 1) * 128].rearrange("(fc p) f -> p fc f", p=128)),
             sem=f"oW{doc % 2}", k=16, waits=[("wob", doc - 2)] if doc >= 2 else [])
        P.op(b, "gpsimd",
             lambda e, doc=doc: e.dma_start(
                 sob[doc % 2].ap(),
                 so[:, doc * 128:(doc + 1) * 128].rearrange("(fc p) f -> p fc f", p=128)),
             sem=f"oW{doc % 2}", k=16, mark=("wod", doc))
        P.op(b, "sync",
             lambda e, doc=doc: e.dma_start(xqt[doc % 2][:, :],
                                            xq[doc * 128:(doc + 1) * 128, :]),
             sem=f"xQ{doc % 2}", k=16, mark=("xqd", doc),
             waits=[("yadd", doc - 2)] if doc >= 2 else [])
        emit_gdma(64 + 2 * doc, "sync")
        emit_gdma(65 + 2 * doc, "gpsimd")
    def emit_wob(doc):
        P.op(b, "vector",
             lambda e, doc=doc: e.tensor_tensor(
                 wow[doc % 2].ap().rearrange("p a c -> p (a c)"),
                 cob[doc % 2].ap().rearrange("p a c -> p (a c)"),
                 sob[doc % 2].ap().rearrange("p a c -> p (a c)"), ALU.mult),
             mark=("wob", doc),
             waits=[("wod", doc)] + ([("omm", doc - 2)] if doc >= 2 else []))

    emit_wob(0)
    emit_wob(1)
    for doc in range(16):
        def ommf(e, doc=doc):
            ins = None
            for qh in range(2):
                for fc in range(16):
                    ins = e.matmul(poc[:, doc % 2, qh, :], wow[doc % 2][:, fc, :],
                                   attn[:, fc, qh * 512:(qh + 1) * 512],
                                   start=(fc == 0), stop=(fc == 15))
            return ins
        P.op(b, "tensor", ommf, mark=("omm", doc),
             waits=[("wob", doc)] + ([("oev", doc - 2)] if doc >= 2 else []))
        if doc >= 1:
            dp = doc - 1

            def stf(e, dp=dp):
                ins = None
                for qb in range(8):
                    ins = e.matmul(pst1[:, qb:qb + 1],
                                   ybf[dp % 2][:, qb * 128:(qb + 1) * 128],
                                   ones_bf[:, :], start=(dp == 0 and qb == 0), stop=False)
                    ins = e.matmul(pst2[:, qb:qb + 1],
                                   yb2[:, qb * 128:(qb + 1) * 128],
                                   ones_bf[:, :], start=(dp == 0 and qb == 0), stop=False)
                return ins
            P.op(b, "tensor", stf, mark=("stmm", dp), waits=[("yb2", dp)])

    def stfl(e):
        ins = None
        for qb in range(8):
            ins = e.matmul(pst1[:, qb:qb + 1], ybf[15 % 2][:, qb * 128:(qb + 1) * 128],
                           ones_bf[:, :], start=False, stop=(qb == 7))
            ins = e.matmul(pst2[:, qb:qb + 1], yb2[:, qb * 128:(qb + 1) * 128],
                           ones_bf[:, :], start=False, stop=(qb == 7))
        return ins
    P.op(b, "tensor", stfl, mark=("stmm", 15), waits=[("yb2", 15)])

    for doc in range(16):
        P.op(b, "scalar",
             lambda e, doc=doc: e.activation(
                 t1t[doc % 2][:, :],
                 poc.ap()[:, doc % 2, :, :].rearrange("p a c -> p (a c)"),
                 AF.Identity, bias=bo_t[:, doc:doc + 1]),
             mark=("oev", doc),
             waits=[("omm", doc)] + ([("yadd", doc - 2)] if doc >= 2 else []))
        if doc >= 1:
            emit_gred_s(62 + 2 * doc)
            emit_gred_s(63 + 2 * doc)
        if doc == 15:
            emit_gred_s(94)
            emit_gred_s(95)
    for doc in range(16):
        P.op(b, "vector",
             lambda e, doc=doc: e.tensor_tensor(y[:, doc, :], t1t[doc % 2][:, :],
                                                xqt[doc % 2][:, :], ALU.add),
             mark=("yadd", doc), waits=[("oev", doc), ("xqd", doc)])
        P.op(b, "vector",
             lambda e, doc=doc: e.tensor_copy(ybf[doc % 2][:, :], y[:, doc, :]),
             mark=("ybf", doc),
             waits=[("stmm", doc - 2)] if doc >= 2 else [])
        P.op(b, "vector",
             lambda e, doc=doc: e.tensor_tensor(yb2[:, :], ybf[doc % 2][:, :],
                                                ybf[doc % 2][:, :], ALU.mult),
             mark=("yb2", doc),
             waits=[("stmm", doc - 1)] if doc >= 1 else [])
        if doc == 0:
            P.op(b, "vector", lambda e: e.tensor_copy(rmax[:, :], ybf[0][:, :]))
            P.op(b, "vector", lambda e: e.tensor_copy(rmin[:, :], ybf[0][:, :]),
                 mark=("tree", 0))
        else:
            P.op(b, "vector",
                 lambda e, doc=doc: e.tensor_tensor(rmax[:, :], rmax[:, :],
                                                    ybf[doc % 2][:, :], ALU.max))
            P.op(b, "vector",
                 lambda e, doc=doc: e.tensor_tensor(rmin[:, :], rmin[:, :],
                                                    ybf[doc % 2][:, :], ALU.min),
                 mark=("tree", doc))
        if doc + 2 < 16:
            emit_wob(doc + 2)

    P.op(b, "scalar", lambda e: e.copy(stt["sums1"][:, :], pst1[:, 0:8]),
         waits=[("stmm", 15)])
    P.op(b, "scalar", lambda e: e.copy(stt["sumsq1"][:, :], pst2[:, 0:8]),
         mark="ln1ev")

    s4.close()

    s5 = ExitStack()
    txn = [sb(f"txn{i}", [128, Q], F32, s5) for i in range(2)]
    ptr_ = psum("ptr_", [128, 1024], s5)
    pbc = psum("pbc", [128, 1024], s5)

    b = P.block()

    def ln_chain(nfeat, mu_t, R_t, composed, pre, wait0):
        P.op(b, "vector", lambda e, mu_t=mu_t: e.tensor_scalar(
            mu_t[:, :], stt["sums1"][:, :], 1.0 / nfeat, None, ALU.mult),
            waits=[wait0])
        P.op(b, "vector", lambda e: e.tensor_scalar(
            stt["tmp1"][:, :], stt["sumsq1"][:, :], 1.0 / nfeat, None, ALU.mult))
        P.op(b, "vector", lambda e, mu_t=mu_t: e.tensor_tensor(
            stt["tmp2"][:, :], mu_t[:, :], mu_t[:, :], ALU.mult))
        P.op(b, "vector", lambda e: e.tensor_tensor(
            stt["var"][:, :], stt["tmp1"][:, :], stt["tmp2"][:, :], ALU.subtract))
        if composed:
            P.op(b, "vector", lambda e: e.tensor_scalar(
                stt["tmp1"][:, :], stt["var"][:, :], 1.0 + EPS2, EPS1 * EPS2,
                ALU.mult, ALU.add), mark=pre + "a")
        else:
            eps = EPS2 if nfeat == FF else EPS1
            P.op(b, "vector", lambda e, eps=eps: e.tensor_scalar(
                stt["tmp1"][:, :], stt["var"][:, :], eps, None, ALU.add),
                mark=pre + "a")
        P.op(b, "scalar", lambda e: e.activation(stt["tmp2"][:, :], stt["tmp1"][:, :],
                                                 AF.Sqrt),
             mark=pre + "sq", waits=[pre + "a"])
        P.op(b, "vector", lambda e, R_t=R_t: e.reciprocal(R_t[:, :], stt["tmp2"][:, :]),
             mark=pre + "R", waits=[pre + "sq"])

    for doc in range(16):
        emit_gdma(96 + 2 * doc, "sync")
        emit_gdma(97 + 2 * doc, "gpsimd")
    ln_chain(D, stt["mu"], stt["R"], True, "ln1", "ln1ev")
    P.op(b, "vector", lambda e: e.tensor_scalar(stt["tmp3"][:, :], stt["var"][:, :],
                                                EPS1, None, ALU.add),
         mark="ln1add", waits=["ln1R"])
    P.op(b, "scalar", lambda e: e.activation(stt["tmp3"][:, :], stt["tmp3"][:, :],
                                             AF.Sqrt), mark="ln1sA", waits=["ln1add"])
    P.op(b, "vector", lambda e: e.reciprocal(stt["tmp1"][:, :], stt["tmp3"][:, :]),
         waits=["ln1sA"])
    P.op(b, "vector", lambda e: e.tensor_tensor(stt["rho"][:, :], stt["tmp2"][:, :],
                                                stt["tmp1"][:, :], ALU.mult),
         mark="rho")

    def trred(src, dst_t, op, mk, wt, pp):
        ppb = pp.bitcast(BF16)
        def trf(e, src=src, ppb=ppb):
            ins = None
            for j in range(8):
                ins = e.transpose(ppb[:, j * 128:(j + 1) * 128],
                                  src[:, j * 128:(j + 1) * 128], id128b[:, :])
            return ins
        P.op(b, "tensor", trf, mark=mk + "t", waits=wt)

        def rf(e, dst_t=dst_t, op=op, ppb=ppb):
            ins = None
            for j in range(8):
                ins = e.tensor_reduce(dst_t[:, j:j + 1], ppb[:, j * 128:(j + 1) * 128],
                                      AX.X, op)
            return ins
        P.op(b, "vector", rf, mark=mk + "r", waits=[mk + "t"])

    trred(rmax, stt["mx"], ALU.max, "mx1", [("tree", 15)], ptr_)
    trred(rmin, stt["mn"], ALU.min, "mn1", ["mx1r"], ptr_)

    P.op(b, "vector", lambda e: e.tensor_tensor(stt["tmp1"][:, :], stt["mx"][:, :],
                                                stt["mu"][:, :], ALU.subtract),
         waits=["mn1r", "rho"])
    P.op(b, "vector", lambda e: e.tensor_tensor(stt["tmp2"][:, :], stt["mu"][:, :],
                                                stt["mn"][:, :], ALU.subtract))
    P.op(b, "vector", lambda e: e.tensor_tensor(stt["tmp1"][:, :], stt["tmp1"][:, :],
                                                stt["tmp2"][:, :], ALU.max))
    P.op(b, "vector", lambda e: e.tensor_tensor(stt["gx"][:, :], stt["tmp1"][:, :],
                                                stt["R"][:, :], ALU.mult))
    P.op(b, "vector", lambda e: e.tensor_scalar(stt["cx"][:, :], stt["gx"][:, :],
                                                CLIPB, None, ALU.mult))
    P.op(b, "vector", lambda e: e.tensor_scalar(stt["ncx"][:, :], stt["gx"][:, :],
                                                -CLIPB, None, ALU.mult), mark="cx1")

    def bcast(stat, dstB, key, wkey, pp):
        P.op(b, "tensor", lambda e, stat=stat: e.transpose(pp[0:8, 0:128],
                                                           stat[:, :], id128[:, :]),
             mark=key + "t", waits=[wkey])
        P.op(b, "scalar", lambda e: e.copy(trT[:, :], pp[0:8, 0:128]),
             mark=key + "e", waits=[key + "t"])

        def bmmf(e):
            ins = None
            for qb in range(8):
                ins = e.matmul(pp[:, qb * 128:(qb + 1) * 128],
                               sel8[:, qb * 128:(qb + 1) * 128], trT[:, :],
                               start=True, stop=True)
            return ins
        P.op(b, "tensor", bmmf, mark=key + "m", waits=[key + "e"])
        P.op(b, "vector", lambda e, dstB=dstB: e.tensor_copy(dstB[:, :], pp[:, :]),
             mark=key + "b", waits=[key + "m"])

    bcast(stt["mu"], muB, "bmu", "cx1", pbc)
    bcast(stt["R"], RB, "bR", "bmub", pbc)
    bcast(stt["cx"], cB, "bcx", "bRb", pbc)
    bcast(stt["ncx"], ncB, "bncx", "bcxb", pbc)
    bcast(stt["rho"], rhoB, "brho", "bncxb", pbc)

    for doc in range(16):
        P.op(b, "vector",
             lambda e, doc=doc: e.tensor_tensor(txn[doc % 2][:, :], y[:, doc, :],
                                                muB[:, :], ALU.subtract),
             waits=(["brhob"] if doc == 0 else []))
        P.op(b, "vector",
             lambda e, doc=doc: e.tensor_tensor(y[:, doc, :], txn[doc % 2][:, :],
                                                RB[:, :], ALU.mult),
             mark=("xn", doc))
        P.op(b, "vector",
             lambda e, doc=doc: e.tensor_tensor(txn[doc % 2][:, :], y[:, doc, :],
                                                cB[:, :], ALU.min))
        xw = [] if doc < 8 else [("gtile", 126 if doc < 12 else 127)]
        P.op(b, "vector",
             lambda e, doc=doc: e.tensor_tensor(xh[:, doc, :], txn[doc % 2][:, :],
                                                ncB[:, :], ALU.max),
             mark=("xhat", doc), waits=xw)
        P.op(b, "sync",
             lambda e, doc=doc: e.dma_start(xnspill[doc * 128:(doc + 1) * 128, :],
                                            y[:, doc, :]),
             sem="dX", k=16, mark=("xnsp", doc), waits=[("xn", doc)])
    for j in range(96, 128):
        emit_gred_s(j)

    s5.close()

    s6 = ExitStack()
    wfw = [arena1[:, i * 2048:(i + 1) * 2048].rearrange("p (a c) -> p a c", a=16)
           for i in range(4)]
    wps = [arena1[:, 8192 + i * 2048:8192 + (i + 1) * 2048] for i in range(4)]
    pf = sb("pf", [128, 2048], BF16, s6)
    nf = sb("nf", [128, 2048], BF16, s6)
    tf = [sb(f"tf{i}", [128, 16, 128], BF16, s6) for i in range(2)]
    pa = sb("pa", [128, 2048], BF16, s6)
    pb_ = sb("pb_", [128, 2048], BF16, s6)
    tp8 = [sb(f"tp8{i}", [128, 2048], FP8, s6) for i in range(2)]
    hsl = [sb(f"hsl{i}", [128, Q], BF16, s6) for i in range(2)]
    h2t = sb("h2t", [128, Q], BF16, s6)
    ph = psum("ph", [128, 2, 1024], s6)
    psh1 = psum("psh1", [128, 512], s6)
    psh2 = psum("psh2", [128, 512], s6)
    pg6 = psum("pg6", [128, 8], s6)

    b = P.block()
    P.op(b, "vector", lambda e: e.reduce_sum(gacc_f[:, 0:1], gacc_f[:, :], axis=AX.X),
         waits=[("gtile", 127)])
    P.op(b, "vector", lambda e: e.reduce_sum(gacc_p[:, 0:1], gacc_p[:, :], axis=AX.X),
         mark="gred")
    P.op(b, "tensor", lambda e: e.matmul(pg6[0:1, 0:1], gacc_f[:, 0:1], ones_f[:, :],
                                         start=True, stop=False), waits=["gred"])
    P.op(b, "tensor", lambda e: e.matmul(pg6[0:1, 1:2], gacc_p[:, 0:1], ones_f[:, :],
                                         start=False, stop=True), mark="gsum")
    P.op(b, "scalar", lambda e: e.copy(g1[0:1, 0:2], pg6[0:1, 0:2]),
         mark="g1ev", waits=["gsum"])
    P.op(b, "vector", lambda e: e.tensor_scalar(g1[0:1, 2:4], g1[0:1, 0:2],
                                                1.0 / NW, None, ALU.mult),
         waits=["g1ev"])
    P.op(b, "vector", lambda e: e.tensor_scalar(g1[0:1, 4:6], g1[0:1, 2:4],
                                                1e-6, 0.5, ALU.add, ALU.mult))
    P.op(b, "vector", lambda e: e.tensor_scalar(g1[0:1, 6:8], g1[0:1, 4:6],
                                                -1.0, None, ALU.mult))
    P.op(b, "vector", lambda e: e.tensor_scalar(g1[0:1, 8:10], g1[0:1, 2:4],
                                                0.5, None, ALU.mult), mark="gscal")
    gcols = [(4, thf), (5, thp), (6, nthf), (7, nthp), (2, gfb), (9, gpb)]
    for i, (col, dst) in enumerate(gcols):
        P.op(b, "tensor",
             lambda e, i=i, col=col: e.matmul(pg6[:, 2 + i:3 + i], onesr_f[:, :],
                                              g1[0:1, col:col + 1], start=True, stop=True),
             mark=("gbc", i), waits=(["gscal"] if i == 0 else []))
    for i, (col, dst) in enumerate(gcols):
        P.op(b, "scalar", lambda e, i=i, dst=dst: e.copy(dst[:, :], pg6[:, 2 + i:3 + i]),
             mark=("gbcev", i), waits=([("gbc", 5)] if i == 0 else []))

    def emit_wfd(oc):
        P.op(b, "sync",
             lambda e, oc=oc: e.dma_start(
                 wfw[oc % 4],
                 wfc[:, oc * 128:(oc + 1) * 128].rearrange("(dc p) f -> p dc f", p=128)),
             sem=f"fW{oc % 4}", k=16, mark=("wfd", oc),
             waits=([("nfc", oc - 4)] if oc >= 4 else []))

    def emit_trio(oc):
        P.op(b, "vector",
             lambda e, oc=oc: e.tensor_scalar(
                 pf[:, :], wfw[oc % 4].rearrange("p a c -> p (a c)"),
                 thf[:, :], None, ALU.is_gt),
             mark=("pfc", oc),
             waits=[("wfd", oc)] + ([("fmm", oc - 2)] if oc >= 2 else [("gbcev", 5)]))
        P.op(b, "vector",
             lambda e, oc=oc: e.tensor_scalar(
                 nf[:, :], wfw[oc % 4].rearrange("p a c -> p (a c)"),
                 nthf[:, :], None, ALU.is_lt),
             mark=("nfc", oc))
        P.op(b, "vector",
             lambda e, oc=oc: e.tensor_tensor(
                 tf[oc % 2].ap().rearrange("p a c -> p (a c)"),
                 pf[:, :], nf[:, :], ALU.subtract),
             mark=("tfc", oc))

    def emit_slow(j):
        P.op(b, "vector",
             lambda e, j=j: e.tensor_tensor(tp8[j % 2][:, :], pa[:, :], pb_[:, :],
                                            ALU.add),
             mark=("tpc", j),
             waits=[("sgb", j)] + ([("tpw", j - 2)] if j >= 2 else []))
        if j >= 1:
            P.op(b, "vector",
                 lambda e, j=j: e.tensor_tensor(h2t[:, :], hsl[(j - 1) % 2][:, :],
                                                hsl[(j - 1) % 2][:, :], ALU.mult),
                 mark=("h2", j - 1),
                 waits=[("hev", j - 1)] + ([("hst", j - 2)] if j >= 2 else []))
        if j == 0:
            P.op(b, "vector", lambda e: e.tensor_copy(rmax[:, :], hsl[0][:, :]),
                 waits=[("hev", 0)])
            P.op(b, "vector", lambda e: e.tensor_copy(rmin[:, :], hsl[0][:, :]),
                 mark=("htree", 0))
        else:
            P.op(b, "vector",
                 lambda e, j=j: e.tensor_tensor(rmax[:, :], rmax[:, :],
                                                hsl[j % 2][:, :], ALU.max),
                 waits=[("hev", j)])
            P.op(b, "vector",
                 lambda e, j=j: e.tensor_tensor(rmin[:, :], rmin[:, :],
                                                hsl[j % 2][:, :], ALU.min),
                 mark=("htree", j))

    emit_wfd(0)
    emit_wfd(1)
    for oc in range(64):
        if oc + 2 < 64:
            emit_wfd(oc + 2)
        P.op(b, "gpsimd",
             lambda e, oc=oc: e.dma_start(wps[oc % 4][:, :],
                                          wpr[oc * 128:(oc + 1) * 128, :]),
             sem=f"pW{oc % 4}", k=16, mark=("wpd", oc),
             waits=[("sgb", oc - 4)] if oc >= 4 else [])
        emit_trio(oc)
        if oc >= 1:
            emit_slow(oc - 1)
        P.op(b, "scalar",
             lambda e, oc=oc: e.activation(pa[:, :], wps[oc % 4][:, :], AF.Sign,
                                           bias=nthp[:, :]),
             waits=[("wpd", oc)] + ([("tpc", oc - 1)] if oc >= 1 else [("gbcev", 5)]))
        P.op(b, "scalar",
             lambda e, oc=oc: e.activation(pb_[:, :], wps[oc % 4][:, :], AF.Sign,
                                           bias=thp[:, :]),
             mark=("sgb", oc))

        def fmmf(e, oc=oc):
            ins = None
            for qh in range(2):
                for dc in range(16):
                    ins = e.matmul(ph[:, oc % 2, qh * 512:(qh + 1) * 512],
                                   tf[oc % 2][:, dc, :],
                                   xh[:, dc, qh * 512:(qh + 1) * 512],
                                   start=(dc == 0), stop=(dc == 15))
            return ins
        P.op(b, "tensor", fmmf, mark=("fmm", oc),
             waits=[("tfc", oc)] + ([("hev", oc - 2)] if oc >= 2 else []))
        if oc >= 1:
            def hstf(e, op=oc - 1):
                ins = None
                for qb in range(8):
                    ins = e.matmul(psh1[:, qb:qb + 1],
                                   hsl[op % 2][:, qb * 128:(qb + 1) * 128],
                                   ones_bf[:, :], start=(op == 0 and qb == 0), stop=False)
                    ins = e.matmul(psh2[:, qb:qb + 1],
                                   h2t[:, qb * 128:(qb + 1) * 128],
                                   ones_bf[:, :], start=(op == 0 and qb == 0), stop=False)
                return ins
            P.op(b, "tensor", hstf, mark=("hst", oc - 1), waits=[("h2", oc - 1)])
        P.op(b, "scalar",
             lambda e, oc=oc: e.activation(hsl[oc % 2][:, :],
                                           ph.ap()[:, oc % 2, :], AF.Gelu,
                                           scale=gfb[:, :]),
             mark=("hev", oc),
             waits=[("fmm", oc)] + ([("hsp", oc - 2), ("hst", oc - 2),
                                     ("htree", oc - 2)] if oc >= 2 else []))
        P.op(b, "sync",
             lambda e, oc=oc: e.dma_start(hspill[oc * 128:(oc + 1) * 128, :],
                                          hsl[oc % 2][:, :]),
             sem=f"hW{oc % 2}", k=16, mark=("hsp", oc), waits=[("hev", oc)])
        if oc >= 1:
            P.op(b, "sync",
                 lambda e, oc=oc: e.dma_start(tproj[(oc - 1) * 128:oc * 128, :],
                                              tp8[(oc - 1) % 2][:, :]),
                 sem=f"tW{(oc - 1) % 2}", k=16, mark=("tpw", oc - 1),
                 waits=[("tpc", oc - 1)])
    emit_slow(63)
    P.op(b, "sync",
         lambda e: e.dma_start(tproj[63 * 128:64 * 128, :], tp8[63 % 2][:, :]),
         sem=f"tW{63 % 2}", k=16, mark=("tpw", 63), waits=[("tpc", 63)])
    P.op(b, "vector",
         lambda e: e.tensor_tensor(h2t[:, :], hsl[63 % 2][:, :], hsl[63 % 2][:, :],
                                   ALU.mult),
         mark=("h2", 63), waits=[("hev", 63), ("hst", 62)])

    def hstl(e):
        ins = None
        for qb in range(8):
            ins = e.matmul(psh1[:, qb:qb + 1], hsl[63 % 2][:, qb * 128:(qb + 1) * 128],
                           ones_bf[:, :], start=False, stop=(qb == 7))
            ins = e.matmul(psh2[:, qb:qb + 1], h2t[:, qb * 128:(qb + 1) * 128],
                           ones_bf[:, :], start=False, stop=(qb == 7))
        return ins
    P.op(b, "tensor", hstl, mark=("hst", 63), waits=[("h2", 63)])
    P.op(b, "scalar", lambda e: e.copy(stt["sums1"][:, :], psh1[:, 0:8]),
         waits=[("hst", 63)])
    P.op(b, "scalar", lambda e: e.copy(stt["sumsq1"][:, :], psh2[:, 0:8]),
         mark="ln2ev")

    s6.close()

    s7 = ExitStack()
    hrb = [sb(f"hrb{i}", [128, 2, Q], BF16, s7) for i in range(2)]
    th1 = [sb(f"th1{i}", [128, 2, Q], F32, s7) for i in range(1)]
    tpb = [sb(f"tpb{i}", [128, 8, 256], BF16, s7) for i in range(2)]
    xnr = sb("xnr", [128, Q], F32, s7)
    t2t = sb("t2t", [128, Q], F32, s7)
    y2t = [sb(f"y2t{i}", [128, Q], F32, s7) for i in range(2)]
    y2b = sb("y2b", [128, Q], BF16, s7)
    y2b2 = sb("y2b2", [128, Q], BF16, s7)
    py = psum("py", [128, 2, 2, 512], s7)
    psy1 = psum("psy1", [128, 512], s7)
    psy2 = psum("psy2", [128, 512], s7)
    pbc7 = psum("pbc7", [128, 1024], s7)

    b = P.block()
    ln_chain(FF, stt["mu2"], stt["R2"], False, "ln2", "ln2ev")
    trred(rmax, stt["mx"], ALU.max, "mx2", [("htree", 63)], pbc7)
    trred(rmin, stt["mn"], ALU.min, "mn2", ["mx2r"], pbc7)
    P.op(b, "vector", lambda e: e.tensor_tensor(stt["tmp1"][:, :], stt["mx"][:, :],
                                                stt["mu2"][:, :], ALU.subtract),
         waits=["mn2r", "ln2R"])
    P.op(b, "vector", lambda e: e.tensor_tensor(stt["tmp2"][:, :], stt["mu2"][:, :],
                                                stt["mn"][:, :], ALU.subtract))
    P.op(b, "vector", lambda e: e.tensor_tensor(stt["tmp1"][:, :], stt["tmp1"][:, :],
                                                stt["tmp2"][:, :], ALU.max))
    P.op(b, "vector", lambda e: e.tensor_tensor(stt["gx"][:, :], stt["tmp1"][:, :],
                                                stt["R2"][:, :], ALU.mult))
    P.op(b, "vector", lambda e: e.tensor_scalar(stt["c2"][:, :], stt["gx"][:, :],
                                                CLIPB, None, ALU.mult))
    P.op(b, "vector", lambda e: e.tensor_scalar(stt["nc2"][:, :], stt["gx"][:, :],
                                                -CLIPB, None, ALU.mult), mark="cx2")
    bcast(stt["mu2"], muB, "cmu", "cx2", pbc7)
    bcast(stt["R2"], RB, "cR", "cmub", pbc7)
    bcast(stt["c2"], cB, "ccx", "cRb", pbc7)
    bcast(stt["nc2"], ncB, "cncx", "ccxb", pbc7)

    def emit_tpr(dw, ocg):
        if ocg >= 2:
            wt = [("pmm", dw, ocg - 2)]
        elif dw >= 1:
            wt = [("pmm", dw - 1, 6 + ocg)]
        else:
            wt = [("tpw", 63)]
        P.op(b, "gpsimd",
             lambda e, dw=dw, ocg=ocg: e.dma_start(
                 tpb[ocg % 2].ap(),
                 tproj[ocg * 1024:(ocg + 1) * 1024, dw * 256:(dw + 1) * 256]
                 .rearrange("(g p) d -> p g d", p=128)),
             sem=f"tR{ocg % 2}", k=16, mark=("tpr", dw, ocg), waits=wt)

    BC = lambda t: t.ap().unsqueeze(1).broadcast_to([128, 2, Q])
    for tl in range(32):
        oc0 = tl * 2
        P.op(b, "sync",
             lambda e, tl=tl, oc0=oc0: e.dma_start(
                 hrb[tl % 2].ap(),
                 hspill[oc0 * 128:(oc0 + 2) * 128, :]
                 .rearrange("(a p) c -> p a c", p=128)),
             sem=f"hR{tl % 2}", k=16, mark=("hrd", tl),
             waits=([("x2a", tl - 2)] if tl >= 2 else []))
        buf = th1[0]
        P.op(b, "vector",
             lambda e, tl=tl, buf=buf: e.tensor_tensor(
                 buf[:, :, :], hrb[tl % 2][:, :, :], BC(muB), ALU.subtract),
             waits=[("hrd", tl), "cncxb"])
        P.op(b, "vector",
             lambda e, tl=tl, buf=buf: e.tensor_tensor(
                 hrb[tl % 2][:, :, :], buf[:, :, :], BC(RB), ALU.mult))
        P.op(b, "vector",
             lambda e, tl=tl: e.tensor_tensor(hrb[tl % 2][:, :, :],
                                              hrb[tl % 2][:, :, :], BC(cB),
                                              ALU.min))
        P.op(b, "vector",
             lambda e, tl=tl, oc0=oc0: e.tensor_tensor(
                 x2w(oc0), hrb[tl % 2][:, :, :], BC(ncB), ALU.max),
             mark=("x2a", tl))
        if tl % 4 == 3:
            emit_tpr(0, tl // 4)
    for dw in range(8):
        for ocg in range(8):
            if dw >= 1:
                emit_tpr(dw, ocg)

            def pmmf(e, dw=dw, ocg=ocg):
                ins = None
                for gi in range(8):
                    oc = ocg * 8 + gi
                    for dj in range(2):
                        for qh in range(2):
                            ins = e.matmul(py[:, dj, qh, :],
                                           tpb[ocg % 2][:, gi, dj * 128:(dj + 1) * 128],
                                           x2sl(oc, qh * 512, (qh + 1) * 512),
                                           start=(oc == 0), stop=(oc == 63))
                return ins
            wt2 = [("tpr", dw, ocg), ("x2a", ocg * 4 + 3)]
            if ocg == 0 and dw >= 1:
                wt2.append(("y2e", dw - 1, 1))
            P.op(b, "tensor", pmmf, mark=("pmm", dw, ocg), waits=wt2)
        for dj in range(2):
            doc = dw * 2 + dj
            P.op(b, "scalar",
                 lambda e, dj=dj: e.activation(
                     t2t[:, :], py.ap()[:, dj, :, :].rearrange("p a c -> p (a c)"),
                     AF.Identity, scale=gpb[:, :]),
                 mark=("y2e", dw, dj),
                 waits=[("pmm", dw, 7)] + ([("y2add", doc - 1)] if doc >= 1 else []))
            P.op(b, "sync",
                 lambda e, doc=doc: e.dma_start(xnr[:, :],
                                                xnspill[doc * 128:(doc + 1) * 128, :]),
                 sem="xR", k=16, mark=("xnr", doc),
                 waits=[("y2add", doc - 1)] if doc >= 1 else [])
            P.op(b, "vector",
                 lambda e, doc=doc: e.tensor_tensor(y2t[doc % 2][:, :], xnr[:, :],
                                                    rhoB[:, :], ALU.mult),
                 waits=[("xnr", doc), ("y2e", dw, dj)] +
                       ([("y2sp", doc - 2)] if doc >= 2 else []))
            P.op(b, "vector",
                 lambda e, doc=doc: e.tensor_tensor(y2t[doc % 2][:, :],
                                                    y2t[doc % 2][:, :], t2t[:, :],
                                                    ALU.add),
                 mark=("y2add", doc))
            P.op(b, "vector",
                 lambda e, doc=doc: e.tensor_copy(y2b[:, :], y2t[doc % 2][:, :]),
                 waits=[("y2st", doc - 1)] if doc >= 1 else [])
            P.op(b, "vector",
                 lambda e: e.tensor_tensor(y2b2[:, :], y2b[:, :], y2b[:, :], ALU.mult),
                 mark=("y2b2", doc))
            if doc == 0:
                P.op(b, "vector", lambda e: e.tensor_copy(rmax[:, :], y2b[:, :]),
                     waits=["mn2t"])
                P.op(b, "vector", lambda e: e.tensor_copy(rmin[:, :], y2b[:, :]),
                     mark=("y2tree", 0))
            else:
                P.op(b, "vector",
                     lambda e: e.tensor_tensor(rmax[:, :], rmax[:, :], y2b[:, :],
                                               ALU.max))
                P.op(b, "vector",
                     lambda e: e.tensor_tensor(rmin[:, :], rmin[:, :], y2b[:, :],
                                               ALU.min),
                     mark=("y2tree", doc))

            def y2stf(e, doc=doc):
                ins = None
                for qb in range(8):
                    ins = e.matmul(psy1[:, qb:qb + 1], y2b[:, qb * 128:(qb + 1) * 128],
                                   ones_bf[:, :], start=(doc == 0 and qb == 0),
                                   stop=(doc == 15 and qb == 7))
                    ins = e.matmul(psy2[:, qb:qb + 1], y2b2[:, qb * 128:(qb + 1) * 128],
                                   ones_bf[:, :], start=(doc == 0 and qb == 0),
                                   stop=(doc == 15 and qb == 7))
                return ins
            P.op(b, "tensor", y2stf, mark=("y2st", doc), waits=[("y2b2", doc)])
            P.op(b, "sync",
                 lambda e, doc=doc: e.dma_start(y2spill[doc * 128:(doc + 1) * 128, :],
                                                y2t[doc % 2][:, :]),
                 sem=f"yW{doc % 2}", k=16, mark=("y2sp", doc), waits=[("y2add", doc)])

    P.op(b, "scalar", lambda e: e.copy(stt["sums1"][:, :], psy1[:, 0:8]),
         waits=[("y2st", 15)])
    P.op(b, "scalar", lambda e: e.copy(stt["sumsq1"][:, :], psy2[:, 0:8]),
         mark="ln3ev")

    s7.close()

    s9 = ExitStack()
    y2r = [sb(f"y2r{i}", [128, Q], F32, s9) for i in range(2)]
    to1 = [sb(f"to1{i}", [128, Q], F32, s9) for i in range(2)]
    pbc9 = psum("pbc9", [128, 1024], s9)

    b = P.block()
    ln_chain(D, stt["mu3"], stt["R3"], False, "ln3", "ln3ev")
    bcast(stt["mu3"], muB, "dmu", "ln3R", pbc9)
    bcast(stt["R3"], RB, "dR", "dmub", pbc9)

    for doc in range(16):
        P.op(b, "sync",
             lambda e, doc=doc: e.dma_start(y2r[doc % 2][:, :],
                                            y2spill[doc * 128:(doc + 1) * 128, :]),
             sem=f"yR{doc % 2}", k=16, mark=("y2r", doc),
             waits=[("outv", doc - 2)] if doc >= 2 else [])
        P.op(b, "vector",
             lambda e, doc=doc: e.tensor_tensor(to1[doc % 2][:, :], y2r[doc % 2][:, :],
                                                muB[:, :], ALU.subtract),
             waits=[("y2r", doc), "dRb"] + ([("outd", doc - 2)] if doc >= 2 else []))
        P.op(b, "vector",
             lambda e, doc=doc: e.tensor_tensor(to1[doc % 2][:, :], to1[doc % 2][:, :],
                                                RB[:, :], ALU.mult),
             mark=("outv", doc))
        P.op(b, "sync",
             lambda e, doc=doc: e.dma_start(out[doc * 128:(doc + 1) * 128, :],
                                            to1[doc % 2][:, :]),
             sem=f"oD{doc % 2}", k=16, mark=("outd", doc), waits=[("outv", doc)])
    P.op(b, "sync", lambda e: e.nop(), sem="dX", waits=[("outd", 14), ("outd", 15)])

    s9.close()
    P.resolve()
    P.emit(nc, sems)
    gs.close()
    return nc


_NC_CACHE = {}


def _get_nc():
    if "nc" not in _NC_CACHE:
        _NC_CACHE["nc"] = build_program()
    return _NC_CACHE["nc"]


def kernel(L0_attn_x, L0_attn_wq_codes, L0_attn_wq_scales, L0_attn_bq,
           L0_attn_wk_codes, L0_attn_wk_scales, L0_attn_bk,
           L0_attn_wv_codes, L0_attn_wv_scales, L0_attn_bv,
           L0_attn_wo_codes, L0_attn_wo_scales, L0_attn_bo,
           L0_ffn_w_fc, L0_ffn_w_proj):
    x = np.asarray(L0_attn_x, dtype=np.float32)
    B = x.shape[0]
    rep = lambda s: np.ascontiguousarray(np.repeat(np.asarray(s, np.float32), 32, axis=0))
    bias_t = lambda v: np.ascontiguousarray(np.asarray(v, np.float32).reshape(16, 128).T)
    common = {
        "wq": np.ascontiguousarray(np.asarray(L0_attn_wq_codes, np.float32)),
        "wk": np.ascontiguousarray(np.asarray(L0_attn_wk_codes, np.float32)),
        "wv": np.ascontiguousarray(np.asarray(L0_attn_wv_codes, np.float32)),
        "wo": np.ascontiguousarray(np.asarray(L0_attn_wo_codes, np.float32)),
        "sq": rep(L0_attn_wq_scales), "sk": rep(L0_attn_wk_scales),
        "sv": rep(L0_attn_wv_scales), "so": rep(L0_attn_wo_scales),
        "bq": bias_t(L0_attn_bq), "bk": bias_t(L0_attn_bk),
        "bv": bias_t(L0_attn_bv), "bo": bias_t(L0_attn_bo),
        "wfc": np.ascontiguousarray(np.asarray(L0_ffn_w_fc, np.float32).T),
        "wpr": np.ascontiguousarray(np.asarray(L0_ffn_w_proj, np.float32).T),
    }
    in_maps = []
    for c in range(8):
        bb, half = c // 2, c % 2
        xT = x[bb].T
        xr = np.ascontiguousarray(np.concatenate(
            [xT[:, half * Q:], xT[:, :half * Q]], axis=1))
        m = dict(common)
        m["xt"] = xr
        m["xq"] = np.ascontiguousarray(xr[:, :Q])
        in_maps.append(m)
    nc = _get_nc()
    res = run_bass_kernel_spmd(nc, in_maps, core_ids=list(range(8)))
    y_out = np.empty((B, S, D), dtype=np.float32)
    for c in range(8):
        bb, half = c // 2, c % 2
        y_out[bb, half * Q:(half + 1) * Q, :] = np.asarray(res.results[c]["out"]).T
    return y_out


# version 14
# speedup vs baseline: 1.1113x; 1.0218x over previous
import numpy as np
import ml_dtypes
from contextlib import ExitStack
import concourse.bass as bass
import concourse.mybir as mybir
from concourse.bass_utils import run_bass_kernel_spmd

F32 = mybir.dt.float32
BF16 = mybir.dt.bfloat16
FP8 = mybir.dt.float8e4
AF = mybir.ActivationFunctionType
ALU = mybir.AluOpType
AX = mybir.AxisListType

D = 2048
S = 2048
Q = 1024
H = 16
FF = 8192
NW = float(D * FF)
ISQ = float(1.0 / np.sqrt(128.0))
EPS1 = 1e-5
EPS2 = 1e-6
CLIPB = float((128.0 - 1e-6) / 128.0)

SB_TOP = 229344
QT_R = (SB_TOP - 32768, SB_TOP)
XH_R = (SB_TOP - 32768, SB_TOP)
AT_R = (SB_TOP - 65536, SB_TOP - 32768)
PT_R = [(SB_TOP - 131072 + i * 16384, SB_TOP - 131072 + (i + 1) * 16384)
        for i in range(4)]
Y_R = (SB_TOP - 131072, SB_TOP - 65536)
X2_R = (SB_TOP - 131072, SB_TOP)

ENGS = ["sync", "tensor", "vector", "scalar", "gpsimd"]
SEM_OF = {"sync": "dS", "tensor": "tsem", "vector": "vsem", "scalar": "ssem",
          "gpsimd": "gsem"}


class Prog:
    def __init__(self):
        self.blocks = []
        self.ev = {}

    def block(self):
        b = {e: [] for e in ENGS}
        self.blocks.append(b)
        return b

    def op(self, b, eng, fn, *, mark=None, sem=None, k=1, waits=()):
        b[eng].append((fn, mark, sem or SEM_OF[eng], k, tuple(waits)))

    def resolve(self):
        self.ev.clear()
        self.drains = {}
        self.selfw = {}
        self.blockstart = {}
        cnt = {}
        for bi, b in enumerate(self.blocks):
            self.blockstart[bi] = [(s, cnt.get(s, 0))
                                   for s in ("tsem", "vsem", "ssem", "gsem")
                                   if cnt.get(s, 0)]
            for eng in ENGS:
                seen = {}
                own = SEM_OF[eng]
                for oi, (fn, mark, sem, k, waits) in enumerate(b[eng]):
                    if eng in ("vector", "scalar", "gpsimd"):
                        self.selfw[(bi, eng, oi)] = cnt.get(own, 0)
                    cnt[sem] = cnt.get(sem, 0) + k
                    if mark is not None:
                        assert mark not in self.ev, f"duplicate mark {mark}"
                        self.ev[mark] = (sem, cnt[sem])
                    if sem not in ("tsem", "vsem", "ssem", "gsem"):
                        seen[sem] = cnt[sem]
                if seen:
                    self.drains[(bi, eng)] = sorted(seen.items())

    def emit(self, nc, sems):
        for bi, b in enumerate(self.blocks):
            with nc.Block() as blk:
                for eng in ENGS:
                    ops = b[eng]
                    if not ops:
                        continue
                    drains = self.drains.get((bi, eng), [])

                    def body(e, ops=ops, drains=drains, bi=bi, eng=eng):
                        own = sems.get(SEM_OF[eng])
                        for s_, v_ in self.blockstart.get(bi, []):
                            e.wait_ge(sems[s_], v_)
                        for oi, (fn, mark, sem, k, waits) in enumerate(ops):
                            sv = self.selfw.get((bi, eng, oi))
                            if sv:
                                e.wait_ge(own, sv)
                            for w in waits:
                                if w in self.ev:
                                    ws, wc = self.ev[w]
                                    e.wait_ge(sems[ws], wc)
                            ins = fn(e)
                            assert ins is not None
                            ins.then_inc(sems[sem], k)
                        for s_, v_ in drains:
                            e.wait_ge(sems[s_], v_)
                    getattr(blk, eng)(body)


def build_program(debug=False):
    nc = bass.Bass("TRN2", target_bir_lowering=False)
    gs = ExitStack()
    P = Prog()

    xt = nc.dram_tensor("xt", [D, S], F32, kind="ExternalInput")
    xq = nc.dram_tensor("xq", [D, Q], F32, kind="ExternalInput")
    wq = nc.dram_tensor("wq", [D, D], F32, kind="ExternalInput")
    wk = nc.dram_tensor("wk", [D, D], F32, kind="ExternalInput")
    wv = nc.dram_tensor("wv", [D, D], F32, kind="ExternalInput")
    wo = nc.dram_tensor("wo", [D, D], F32, kind="ExternalInput")
    sq = nc.dram_tensor("sq", [D, D], F32, kind="ExternalInput")
    sk = nc.dram_tensor("sk", [D, D], F32, kind="ExternalInput")
    sv = nc.dram_tensor("sv", [D, D], F32, kind="ExternalInput")
    so = nc.dram_tensor("so", [D, D], F32, kind="ExternalInput")
    bqd = nc.dram_tensor("bq", [128, 16], F32, kind="ExternalInput")
    bkd = nc.dram_tensor("bk", [128, 16], F32, kind="ExternalInput")
    bvd = nc.dram_tensor("bv", [128, 16], F32, kind="ExternalInput")
    bod = nc.dram_tensor("bo", [128, 16], F32, kind="ExternalInput")
    wfc = nc.dram_tensor("wfc", [D, FF], F32, kind="ExternalInput")
    wpr = nc.dram_tensor("wpr", [FF, D], F32, kind="ExternalInput")
    out = nc.dram_tensor("out", [D, Q], F32, kind="ExternalOutput")

    kd = {"kind": "ExternalOutput"} if debug else {}
    kspill = nc.dram_tensor("kspill", [D, S], BF16, **kd)
    vspill = nc.dram_tensor("vspill", [H, S, 128], BF16, **kd)
    tproj = nc.dram_tensor("tproj", [FF, D], FP8, **kd)
    hspill = nc.dram_tensor("hspill", [FF, Q], BF16, **kd)
    xnspill = nc.dram_tensor("xnspill", [D, Q], F32, **kd)
    y2spill = nc.dram_tensor("y2spill", [D, Q], F32, **kd)

    semnames = (["tsem", "vsem", "ssem", "gsem", "ost", "dX", "gX", "kSp", "xR"]
                + [f"{p}{i}" for p in ["gW", "vT", "vP", "kR", "vR", "tP",
                                       "xQ", "oW", "hW", "tW", "hR",
                                       "tR", "yW", "yR", "oD"] for i in range(2)]
                + [f"{p}{i}" for p in ["fW", "pW", "gG", "gS"] for i in range(4)])
    sems = {n: gs.enter_context(nc.semaphore(n)) for n in semnames}

    def sb(name, shape, dt, scope, **kw):
        return scope.enter_context(nc.sbuf_tensor(name, shape, dt, **kw))

    def psum(name, shape, scope):
        return scope.enter_context(nc.psum_tensor(name, shape, F32))

    ones_bf = sb("ones_bf", [128, 1], BF16, gs)
    ones_f = sb("ones_f", [128, 1], F32, gs)
    onesr_f = sb("onesr_f", [1, 128], F32, gs)
    id128 = sb("id128", [128, 128], F32, gs)
    id128b = sb("id128b", [128, 128], BF16, gs)
    sel8 = sb("sel8", [8, 1024], F32, gs)
    bq_s = sb("bq_s", [128, 16], F32, gs)
    bk_t = sb("bk_t", [128, 16], F32, gs)
    bv_t = sb("bv_t", [128, 16], F32, gs)
    bo_t = sb("bo_t", [128, 16], F32, gs)
    thf = sb("thf", [128, 1], F32, gs)
    nthf = sb("nthf", [128, 1], F32, gs)
    thp = sb("thp", [128, 1], F32, gs)
    nthp = sb("nthp", [128, 1], F32, gs)
    gfb = sb("gfb", [128, 1], F32, gs)
    gpb = sb("gpb", [128, 1], F32, gs)
    gacc_f = sb("gacc_f", [128, 64], F32, gs)
    gacc_p = sb("gacc_p", [128, 64], F32, gs)
    g1 = sb("g1", [1, 16], F32, gs)
    sums = None
    stn = ["sums1", "sumsq1", "mu", "var", "R", "rho", "gx", "cx", "ncx",
           "mu2", "R2", "c2", "nc2", "mu3", "R3", "tmp1", "tmp2", "tmp3",
           "mx", "mn"]
    stt = {n: sb("st_" + n, [128, 8], F32, gs) for n in stn}
    muB = sb("muB", [128, Q], F32, gs)
    RB = sb("RB", [128, Q], F32, gs)
    cB = sb("cB", [128, Q], BF16, gs)
    ncB = sb("ncB", [128, Q], BF16, gs)
    rhoB = sb("rhoB", [128, Q], F32, gs)
    rmax = sb("rmax", [128, Q], BF16, gs)
    rmin = sb("rmin", [128, Q], BF16, gs)
    rminf = rmin.bitcast(F32)
    sums = rminf[:, 336:464]
    trT = sb("trT", [8, 128], F32, gs)
    arena1 = sb("arena1", [128, 16384], F32, gs)
    arena2 = sb("arena2", [128, 16384], BF16, gs)
    arena3 = sb("arena3", [128, 16384], BF16, gs)
    a1b = arena1.bitcast(BF16)
    a2f = arena2.bitcast(F32)
    Qt = arena3.ap().rearrange("p (a c) -> p a c", a=16)
    attn = arena2.ap().rearrange("p (a c) -> p a c", a=16)
    PT = [a1b[:, i * 8192:(i + 1) * 8192].rearrange("p (kc i) -> p kc i", kc=16)
          for i in range(4)]
    y = arena1.ap().rearrange("p (a c) -> p a c", a=16)
    xh = arena3.ap().rearrange("p (a c) -> p a c", a=16)
    x2parts = [a1b.rearrange("p (a c) -> p a c", a=32),
               arena2.ap().rearrange("p (a c) -> p a c", a=16),
               arena3.ap().rearrange("p (a c) -> p a c", a=16)]

    def x2w(oc0):
        if oc0 < 32:
            return x2parts[0][:, oc0:oc0 + 2, :]
        if oc0 < 48:
            return x2parts[1][:, oc0 - 32:oc0 - 30, :]
        return x2parts[2][:, oc0 - 48:oc0 - 46, :]

    def x2sl(oc, c0, c1):
        if oc < 32:
            return x2parts[0][:, oc, c0:c1]
        if oc < 48:
            return x2parts[1][:, oc - 32, c0:c1]
        return x2parts[2][:, oc - 48, c0:c1]

    b = P.block()
    for dst, src in [(bq_s, bqd), (bk_t, bkd), (bv_t, bvd), (bo_t, bod)]:
        P.op(b, "sync", lambda e, d=dst, s=src: e.dma_start(d[:, :], s[:, :]),
             sem="dX", k=16, mark=("bld", dst.name))
    for t_ in [ones_f, onesr_f, ones_bf, id128]:
        P.op(b, "vector", lambda e, t_=t_: e.memset(t_[:, :], 1.0))
    P.op(b, "vector", lambda e: e.memset(sel8[:, :], 1.0), mark="consts")
    P.op(b, "vector",
         lambda e: e.tensor_scalar(bq_s[:, :], bq_s[:, :], ISQ, None, ALU.mult),
         waits=[("bld", "bo_t")], mark="bqs")
    P.op(b, "vector", lambda e: e.memset(id128b[:, :], 1.0), mark="constsb")
    P.op(b, "gpsimd",
         lambda e: e.affine_select(id128[:, :], id128[:, :], [[1, 128]],
                                   ALU.is_equal, 0.0, base=0, channel_multiplier=-1),
         waits=["consts", "constsb"], mark="id128")
    P.op(b, "gpsimd",
         lambda e: e.affine_select(id128b[:, :], id128b[:, :], [[1, 128]],
                                   ALU.is_equal, 0.0, base=0, channel_multiplier=-1),
         mark="id128b")
    P.op(b, "gpsimd",
         lambda e: e.affine_select(sel8.ap().rearrange("p (b c) -> p b c", b=8),
                                   sel8.ap().rearrange("p (b c) -> p b c", b=8),
                                   [[1, 8], [0, 128]], ALU.is_equal, 0.0,
                                   base=0, channel_multiplier=-1),
         mark="sel8")

    s1 = ExitStack()
    x_fm = a1b.rearrange("p (a c) -> p a c", a=16)
    kev = sb("kev", [128, 4, 512], BF16, s1)
    ones8c = sb("ones8c", [128, 8], BF16, s1)
    ones8r = sb("ones8r", [8, 128], BF16, s1)
    mk2b8 = sb("mk2b8", [8, 16], BF16, s1)
    k2c = sb("k2c", [128, 2, 512], BF16, s1)
    kmx8 = rminf
    q2c = sb("q2c", [128, 8, 128], BF16, s1)
    ckb = [sb(f"ckb{i}", [128, 16, 128], BF16, s1) for i in range(2)]
    skb = [sb(f"skb{i}", [128, 16, 128], BF16, s1) for i in range(2)]
    wkb = [sb(f"wkb{i}", [128, 16, 128], BF16, s1) for i in range(2)]
    vrow = [sb(f"vrow{i}", [128, 4, 512], BF16, s1) for i in range(2)]
    vtrb = [sb(f"vtrb{i}", [128, 16, 128], BF16, s1, is_transpose_dest=True)
            for i in range(2)]
    pk = psum("pk", [128, 4, 512], s1)
    pkn = psum("pkn", [8, 2, 512], s1)

    def wcon(mi):
        if mi < 16:
            return ("pkg", mi * 4 + 3)
        if mi < 32:
            return ("pkg", 64 + (mi - 16) * 4 + 3)
        return ("pkg", 128 + (mi - 32) * 2 + 1)

    b = P.block()
    P.op(b, "gpsimd",
         lambda e: e.dma_start(x_fm,
                               xt.ap().rearrange("(dc p) t -> p dc t", p=128)),
         sem="gX", k=16, mark="x_fm")
    mats = [(wk, sk), (wv, sv), (wq, sq)]
    wdst = {"i": 0}

    def emit_wd(upto):
        while wdst["i"] <= min(upto, 47):
            mi = wdst["i"]
            m, fc = divmod(mi, 16)
            cod, scl = mats[m]
            wt = [wcon(mi - 2)] if mi >= 2 else []
            P.op(b, "gpsimd",
                 lambda e, cod=cod, fc=fc, mi=mi: e.dma_start(
                     ckb[mi % 2].ap(),
                     cod[:, fc * 128:(fc + 1) * 128]
                     .rearrange("(dc p) f -> p dc f", p=128)),
                 sem=f"gW{mi % 2}", k=16, waits=wt)
            P.op(b, "gpsimd",
                 lambda e, scl=scl, fc=fc, mi=mi: e.dma_start(
                     skb[mi % 2].ap(),
                     scl[:, fc * 128:(fc + 1) * 128]
                     .rearrange("(dc p) f -> p dc f", p=128)),
                 sem=f"gW{mi % 2}", k=16, mark=("wdma", mi))
            wdst["i"] += 1
    wbst = {"i": 0}

    def emit_wb(upto):
        while wbst["i"] <= min(upto, 47):
            mi = wbst["i"]
            wt = [("wdma", mi)] + ([wcon(mi - 2)] if mi >= 2 else [])
            P.op(b, "vector",
                 lambda e, mi=mi: e.tensor_tensor(
                     wkb[mi % 2].ap().rearrange("p a c -> p (a c)"),
                     ckb[mi % 2].ap().rearrange("p a c -> p (a c)"),
                     skb[mi % 2].ap().rearrange("p a c -> p (a c)"), ALU.mult),
                 mark=("wbuild", mi), waits=wt)
            wbst["i"] += 1
    P.op(b, "vector", lambda e: e.memset(ones8c[:, :], 1.0))
    P.op(b, "vector", lambda e: e.memset(ones8r[:, :], 1.0), mark="ones8")
    emit_wd(2)
    emit_wb(1)

    groups = []
    for fc in range(16):
        for th in range(4):
            groups.append((len(groups), fc, th * 512))
    for fc in range(16):
        for th in range(4):
            groups.append((len(groups), 16 + fc, th * 512))
    for fc in range(16):
        for th in range(2):
            groups.append((len(groups), 32 + fc, th * 512))

    for n, mi, tok0 in groups:
        def mmg(e, n=n, mi=mi, tok0=tok0):
            ins = None
            for dc in range(16):
                ins = e.matmul(pk[:, n % 4, :], wkb[mi % 2][:, dc, :],
                               x_fm[:, dc, tok0:tok0 + 512],
                               start=(dc == 0), stop=(dc == 15))
            return ins
        wt = ["x_fm", ("wbuild", mi)]
        if n >= 4:
            wt.append(("pkev", n - 4))
        P.op(b, "tensor", mmg, mark=("pkg", n), waits=wt)
        j = n - 1
        if 0 <= j < 64:
            wtk = [("k2q", j), "ones8"] + ([("kmxr", j - 2)] if j >= 2 else [])
            P.op(b, "tensor",
                 lambda e, j=j: e.matmul(pkn[:, j % 2, :], ones8c[:, :],
                                         k2c[:, j % 2, :], start=True, stop=True),
                 mark=("pknm", j), waits=wtk)

    gplan = [("f", i) for i in range(64)] + [("p", i) for i in range(64)]

    a3f = arena3.bitcast(F32)

    def g_src(j):
        which, i = gplan[j]
        if which == "f":
            nn, mm_ = divmod(i, 4)
            return wfc.ap().rearrange("(n p) (m f) -> n m p f", p=128, f=2048)[nn, mm_]
        return wpr.ap().rearrange("(n p) f -> n p f", p=128)[i]

    def g_buf(j):
        if j < 40:
            t = j % 4
            return a2f[:, t * 2048:(t + 1) * 2048]
        if j < 64:
            t = (j - 40) % 3
            return a2f[:, t * 2048:(t + 1) * 2048]
        if j < 96:
            t = j % 4
            return a3f[:, t * 2048:(t + 1) * 2048]
        return a3f[:, (2 + j % 2) * 2048:(3 + j % 2) * 2048]

    def g_prev(j):
        if j >= 96:
            return j - 2
        if j >= 43:
            return j - 3 if j < 64 else j - 4
        return j - 4

    def emit_gdma(j, eng):
        wt = [("gtile", g_prev(j))] if g_prev(j) >= 0 else []
        fam = "gG" if eng == "gpsimd" else "gS"
        P.op(b, eng, lambda e, j=j: e.dma_start(g_buf(j), g_src(j)),
             sem=f"{fam}{j % 4}", k=16, mark=("gdma", j), waits=wt)

    def emit_gred_v(j):
        which, i = gplan[j]
        acc = gacc_f if which == "f" else gacc_p
        P.op(b, "vector",
             lambda e, j=j, acc=acc, i=i: e.reduce_sum(
                 acc[:, i:i + 1], g_buf(j), axis=AX.X,
                 apply_absolute_value=True),
             mark=("gtile", j), waits=[("gdma", j)])

    def emit_gred_s(j):
        which, i = gplan[j]
        acc = gacc_f if which == "f" else gacc_p
        P.op(b, "scalar",
             lambda e, j=j, acc=acc, i=i: e.activation(
                 g_buf(j), g_buf(j), AF.Abs, accum_out=acc[:, i:i + 1]),
             mark=("gtile", j), waits=[("gdma", j)])

    gfst = {"d": 0, "r": 0}

    def g_dma_upto(upto):
        while gfst["d"] < min(upto, 40):
            emit_gdma(gfst["d"], "sync")
            gfst["d"] += 1

    def g_red_upto(upto):
        while gfst["r"] < min(upto, 40):
            emit_gred_v(gfst["r"])
            gfst["r"] += 1
    KMX = 7424
    MKH = 7488
    MKB = 7504
    MH2 = 7168
    MHP = 7296
    NMH = 8064
    for n in range(64):
        fc, th = divmod(n, 4)
        wt = [("pkg", n)]
        if th == 0 and fc >= 1:
            wt.append(("kspill", fc - 1))
        wtq = [("pkg", n)] + ([("pknm", n - 2)] if n >= 2 else [])
        P.op(b, "scalar",
             lambda e, fc=fc, th=th, n=n: e.activation(
                 k2c[:, n % 2, :], pk[:, n % 4, :], AF.Square,
                 bias=bk_t[:, fc:fc + 1]),
             mark=("k2q", n), waits=wtq)
        P.op(b, "scalar",
             lambda e, fc=fc, th=th, n=n: e.activation(
                 kev[:, th, :], pk[:, n % 4, :], AF.Identity, bias=bk_t[:, fc:fc + 1]),
             mark=("pkev", n), waits=wt)
        emit_wb(fc + 1)
        P.op(b, "vector",
             lambda e, n=n: e.reduce_max(kmx8[0:8, 128 + n:129 + n],
                                         pkn[0:8, n % 2, :], axis=AX.X),
             mark=("kmxr", n), waits=[("pknm", n)])
        if th == 3:
            P.op(b, "vector",
                 lambda e, fc=fc: e.reduce_max(kmx8[0:8, 192 + fc:193 + fc],
                                               kmx8[0:8, 128 + 4 * fc:132 + 4 * fc],
                                               axis=AX.X),
                 mark=("mkh", fc))
        if th == 3:
            P.op(b, "sync",
                 lambda e, fc=fc: e.dma_start(
                     kspill[fc * 128:(fc + 1) * 128, :],
                     kev.rearrange("p a c -> p (a c)")),
                 sem="kSp", k=16, mark=("kspill", fc), waits=[("pkev", fc * 4 + 3)])
        emit_wd(n // 4 + 3)
        g_dma_upto(n // 2 + 1)
        g_red_upto(n // 2 - 2)
    for n in range(64, 128):
        fc, th = divmod(n - 64, 4)
        wt = [("pkg", n)]
        if th == 0 and fc >= 2:
            wt.append(("vtr", fc - 2))
        P.op(b, "scalar",
             lambda e, fc=fc, th=th, n=n: e.activation(
                 vrow[fc % 2][:, th, :], pk[:, n % 4, :], AF.Identity,
                 bias=bv_t[:, fc:fc + 1]),
             mark=("pkev", n), waits=wt)
        emit_wd((n - 64) // 4 + 19)
        emit_wb((n - 64) // 4 + 17)
        g_dma_upto(32 + (n - 64) // 2)
        g_red_upto(30 + (n - 64) // 2)
        if th == 3:
            wt2 = [("pkev", n)] + ([("vsp", fc - 2)] if fc >= 2 else [])
            P.op(b, "sync",
                 lambda e, fc=fc: e.dma_start(
                     vtrb[fc % 2].ap(),
                     vrow[fc % 2].ap().rearrange("p a c -> p (a c)")
                     .rearrange("p (tc i) -> p tc i", tc=16),
                     transpose=True),
                 sem=f"vT{fc % 2}", k=16, mark=("vtr", fc), waits=wt2)
            P.op(b, "sync",
                 lambda e, fc=fc: e.dma_start(
                     vspill[fc].rearrange("(tc p) i -> p tc i", p=128),
                     vtrb[fc % 2].ap()),
                 sem=f"vP{fc % 2}", k=16, mark=("vsp", fc), waits=[("vtr", fc)])
    for n in range(128, 160):
        fc, th = divmod(n - 128, 2)
        P.op(b, "scalar",
             lambda e, fc=fc, th=th, n=n: e.activation(
                 Qt[:, fc, th * 512:(th + 1) * 512], pk[:, n % 4, :],
                 AF.Identity, scale=ISQ, bias=bq_s[:, fc:fc + 1]),
             mark=("pkev", n), waits=[("pkg", n)])
        emit_wd((n - 128) // 2 + 35)
        emit_wb((n - 128) // 2 + 33)
        g_red_upto(34 + (n - 128) // 2)
    emit_wd(47)
    emit_wb(47)
    g_red_upto(40)


    s2 = ExitStack()
    pB2 = psum("pB2", [128, 512], s2)
    pqn = psum("pqn", [128, 128], s2)
    b = P.block()
    P.op(b, "vector", lambda e: e.tensor_copy(mk2b8[:, :], kmx8[0:8, 192:208]),
         mark="mk2b")
    P.op(b, "tensor",
         lambda e: e.matmul(pB2[:, 256:272], ones8r[:, :], mk2b8[:, :],
                            start=True, stop=True),
         mark="mkbc", waits=["mk2b"])
    P.op(b, "scalar", lambda e: e.copy(a2f[:, MKB:MKB + 16], pB2[:, 256:272]),
         mark="mkev", waits=["mkbc"])
    for j2 in range(128):
        h_, qb_ = divmod(j2, 8)
        P.op(b, "vector",
             lambda e, j2=j2, h_=h_, qb_=qb_: e.tensor_tensor(
                 q2c[:, j2 % 8, :], Qt[:, h_, qb_ * 128:(qb_ + 1) * 128],
                 Qt[:, h_, qb_ * 128:(qb_ + 1) * 128], ALU.mult),
             mark=("q2c", j2), waits=[("qnm", j2 - 8)] if j2 >= 8 else [])
        P.op(b, "tensor",
             lambda e, j2=j2: e.matmul(pqn[:, j2:j2 + 1], q2c[:, j2 % 8, :],
                                       ones_bf[:, :], start=True, stop=True),
             mark=("qnm", j2), waits=[("q2c", j2)])
        if j2 % 5 == 0 and j2 // 5 < 24:
            emit_gdma(40 + j2 // 5, "gpsimd")
            if j2 // 5 >= 2:
                emit_gred_s(38 + j2 // 5)
    for h_ in range(16):
        P.op(b, "vector",
             lambda e, h_=h_: e.tensor_scalar(
                 a2f[:, MH2 + h_ * 8:MH2 + h_ * 8 + 8], pqn[:, h_ * 8:h_ * 8 + 8],
                 a2f[:, MKB + h_:MKB + h_ + 1], None, ALU.mult),
             mark=("mh2", h_), waits=[("qnm", h_ * 8 + 7)] + (["mkev"] if h_ == 0 else []))
    P.op(b, "scalar",
         lambda e: e.activation(a2f[:, MHP:MHP + 128], a2f[:, MH2:MH2 + 128], AF.Sqrt),
         mark="mhs", waits=[("mh2", 15)])
    P.op(b, "vector",
         lambda e: e.tensor_scalar(a2f[:, NMH:NMH + 128], a2f[:, MHP:MHP + 128],
                                   -1.0, None, ALU.mult),
         mark="nmh", waits=["mhs"])
    for j in range(62, 64):
        emit_gred_s(j)

    s2.close()
    s1.close()

    s3 = ExitStack()
    kslb = [sb(f"kslb{i}", [128, S], BF16, s3) for i in range(2)]
    vslb = [sb(f"vslb{i}", [128, 16, 128], BF16, s3) for i in range(2)]
    Pt = [sb(f"Pt{i}", [128, S], BF16, s3) for i in range(2)]
    sums2 = rminf
    invT = rminf
    Sinv = sb("Sinv", [128, 8], F32, s3)
    invb = [sb(f"invb{i}", [128, Q], F32, s3) for i in range(2)]
    pscore = psum("pscore", [128, 2048], s3)
    pav = psum("pav", [128, 2, 512], s3)
    pinvb = psum("pinvb", [128, Q], s3)

    b = P.block()

    def emit_ptr(h, qb):
        qh = 0 if qb < 4 else 1
        wt2 = []
        if h >= 2 and qb % 4 == 0:
            wt2.append(("avmm", h - 2, qh))

        def ptrf(e, h=h, qb=qb, qh=qh):
            return e.dma_start(
                PT[(h % 2) * 2 + qh][:, :, (qb % 4) * 128:(qb % 4 + 1) * 128],
                Pt[qb % 2].ap().rearrange("p (kc i) -> p kc i", kc=16),
                transpose=True)
        P.op(b, "sync", ptrf, sem=f"tP{qb % 2}", k=16, mark=("ptr", h, qb),
             waits=[("exB", h, qb)] + wt2)

    for h in range(H):
        if h >= 1:
            for qb in range(4):
                emit_ptr(h - 1, qb)
        P.op(b, "sync",
             lambda e, h=h: e.dma_start(kslb[h % 2][:, :],
                                        kspill[h * 128:(h + 1) * 128, :]),
             sem=f"kR{h % 2}", k=16, mark=("ksl", h),
             waits=[("smB", h - 2, 7)] if h >= 2 else [])
        if h >= 1:
            for qb in range(4, 8):
                emit_ptr(h - 1, qb)
        P.op(b, "sync",
             lambda e, h=h: e.dma_start(
                 vslb[h % 2].ap(),
                 vspill[h].rearrange("(tc p) i -> p tc i", p=128)),
             sem=f"vR{h % 2}", k=16, mark=("vsl", h),
             waits=[("avmm", h - 2, 1)] if h >= 2 else [])
    for qb in range(8):
        emit_ptr(H - 1, qb)

    def emit_inv_tr(h):
        wt = [("recip", h)] + ([("invev", h - 1)] if h >= 1 else ["id128"])
        P.op(b, "tensor",
             lambda e: e.transpose(pinvb[0:8, 0:128], Sinv[:, :], id128[:, :]),
             mark=("invtr", h), waits=wt)

    def emit_inv_bc(h):
        def f(e):
            ins = None
            for qb in range(8):
                ins = e.matmul(pinvb[:, qb * 128:(qb + 1) * 128],
                               sel8[:, qb * 128:(qb + 1) * 128], invT[0:8, 208:336],
                               start=True, stop=True)
            return ins
        P.op(b, "tensor", f, mark=("invbc", h), waits=[("invTe", h)])

    def emit_av(hp, qh, kc0, kc1, markkey=None):
        def f(e, hp=hp, qh=qh, kc0=kc0, kc1=kc1):
            ins = None
            for kc in range(kc0, kc1):
                ins = e.matmul(pav[:, qh, :], vslb[hp % 2][:, kc, :],
                               PT[(hp % 2) * 2 + qh][:, kc, :],
                               start=(kc == 0), stop=(kc == 15))
            return ins
        wt = []
        if kc0 == 0:
            wt = [("ptr", hp, qh * 4 + 2), ("ptr", hp, qh * 4 + 3), ("vsl", hp)]
            if hp >= 1:
                wt.append(("avev", hp - 1, qh))
        P.op(b, "tensor", f, mark=markkey, waits=wt)

    for h in range(H):
        for qb in range(8):
            pq = (h, qb - 1) if qb > 0 else (h - 1, 7)

            def smmh(e, h=h, qb=qb, j0=0):
                ins = None
                for j in range(j0, j0 + 2):
                    ins = e.matmul(pscore[:, j * 512:(j + 1) * 512],
                                   Qt[:, h, qb * 128:(qb + 1) * 128],
                                   kslb[h % 2][:, j * 512:(j + 1) * 512],
                                   start=True, stop=True)
                return ins
            wtA = [("ksl", h)]
            wtA.append(("exA",) + pq if (h, qb) != (0, 0) else ("pkev", 159))
            P.op(b, "tensor", lambda e, f=smmh: f(e, j0=0), mark=("smA", h, qb),
                 waits=wtA)
            if h >= 1:
                hp, qh = h - 1, (0 if qb < 4 else 1)
                kc0 = (qb % 4) * 4
                emit_av(hp, qh, kc0, kc0 + 4,
                        markkey=("avmm", hp, qh) if qb % 4 == 3 else None)
            wtB = [("exB",) + pq] if (h, qb) != (0, 0) else []
            P.op(b, "tensor", lambda e, f=smmh: f(e, j0=2), mark=("smB", h, qb),
                 waits=wtB)
            if h >= 1 and qb == 0:
                emit_inv_tr(h - 1)
            if h >= 1 and qb == 1:
                emit_inv_bc(h - 1)
    hl = H - 1
    emit_inv_tr(hl)
    emit_inv_bc(hl)
    for qh in range(2):
        emit_av(hl, qh, 0, 16, markkey=("avmm", hl, qh))

    for h in range(H):
        if h >= 1:
            P.op(b, "vector",
                 lambda e, h=h: e.tensor_copy(invb[(h - 1) % 2][:, :], pinvb[:, :]),
                 mark=("invev", h - 1), waits=[("invbc", h - 1)])
            for qh in range(2):
                P.op(b, "vector",
                     lambda e, h=h, qh=qh: e.tensor_tensor(
                         attn[:, h - 1, qh * 512:(qh + 1) * 512], pav[:, qh, :],
                         invb[(h - 1) % 2][:, qh * 512:(qh + 1) * 512], ALU.mult),
                     mark=("avev", h - 1, qh),
                     waits=[("avmm", h - 1, qh), ("invev", h - 1)])
        P.op(b, "vector",
             lambda e, h=h: e.tensor_tensor(sums[:, h * 8:(h + 1) * 8],
                                            sums[:, h * 8:(h + 1) * 8],
                                            sums2[:, h * 8:(h + 1) * 8], ALU.add),
             mark=("scomb", h), waits=[("exB", h, 7)])
        P.op(b, "vector",
             lambda e, h=h: e.reciprocal(Sinv[:, :], sums[:, h * 8:(h + 1) * 8]),
             mark=("recip", h), waits=[("scomb", h)])

    P.op(b, "vector",
         lambda e: e.tensor_copy(invb[hl % 2][:, :], pinvb[:, :]),
         mark=("invev", hl), waits=[("invbc", hl)])
    for qh in range(2):
        P.op(b, "vector",
             lambda e, qh=qh: e.tensor_tensor(
                 attn[:, hl, qh * 512:(qh + 1) * 512], pav[:, qh, :],
                 invb[hl % 2][:, qh * 512:(qh + 1) * 512], ALU.mult),
             mark=("avev", hl, qh),
             waits=[("avmm", hl, qh), ("invev", hl)])

    for h in range(H):
        for qb in range(8):
            wt = [("smA", h, qb)]
            if h * 8 + qb >= 2:
                pq = (h, qb - 2) if qb >= 2 else (h - 1, 6 + qb)
                wt.append(("ptr",) + pq)
            j2 = h * 8 + qb
            P.op(b, "scalar",
                 lambda e, h=h, qb=qb, j2=j2: e.activation(
                     Pt[qb % 2][:, 0:1024], pscore[:, 0:1024], AF.Exp,
                     bias=a2f[:, NMH + j2:NMH + j2 + 1],
                     accum_out=sums[:, j2:j2 + 1]),
                 mark=("exA", h, qb), waits=wt)
            P.op(b, "scalar",
                 lambda e, h=h, qb=qb, j2=j2: e.activation(
                     Pt[qb % 2][:, 1024:2048], pscore[:, 1024:2048], AF.Exp,
                     bias=a2f[:, NMH + j2:NMH + j2 + 1],
                     accum_out=sums2[:, j2:j2 + 1]),
                 mark=("exB", h, qb), waits=[("smB", h, qb)])
        P.op(b, "scalar", lambda e: e.copy(invT[0:8, 208:336], pinvb[0:8, 0:128]),
             mark=("invTe", h), waits=[("invtr", h)])

    s3.close()

    s4 = ExitStack()
    cob = [sb(f"cob{i}", [128, 16, 128], BF16, s4) for i in range(2)]
    sob = [sb(f"sob{i}", [128, 16, 128], BF16, s4) for i in range(2)]
    wow = [sb(f"wow{i}", [128, 16, 128], BF16, s4) for i in range(2)]
    xqt = [sb(f"xqt{i}", [128, Q], F32, s4) for i in range(2)]
    t1t = [sb(f"t1t{i}", [128, Q], F32, s4) for i in range(2)]
    ybf = [sb(f"ybf{i}", [128, Q], BF16, s4) for i in range(2)]
    yb2 = sb("yb2", [128, Q], BF16, s4)
    poc = psum("poc", [128, 2, 2, 512], s4)
    pst1 = psum("pst1", [128, 512], s4)
    pst2 = psum("pst2", [128, 512], s4)

    b = P.block()
    for doc in range(16):
        P.op(b, "gpsimd",
             lambda e, doc=doc: e.dma_start(
                 cob[doc % 2].ap(),
                 wo[:, doc * 128:(doc + 1) * 128].rearrange("(fc p) f -> p fc f", p=128)),
             sem=f"oW{doc % 2}", k=16, waits=[("wob", doc - 2)] if doc >= 2 else [])
        P.op(b, "gpsimd",
             lambda e, doc=doc: e.dma_start(
                 sob[doc % 2].ap(),
                 so[:, doc * 128:(doc + 1) * 128].rearrange("(fc p) f -> p fc f", p=128)),
             sem=f"oW{doc % 2}", k=16, mark=("wod", doc))
        P.op(b, "sync",
             lambda e, doc=doc: e.dma_start(xqt[doc % 2][:, :],
                                            xq[doc * 128:(doc + 1) * 128, :]),
             sem=f"xQ{doc % 2}", k=16, mark=("xqd", doc),
             waits=[("yadd", doc - 2)] if doc >= 2 else [])
        emit_gdma(64 + 2 * doc, "sync")
        emit_gdma(65 + 2 * doc, "gpsimd")
    def emit_wob(doc):
        P.op(b, "vector",
             lambda e, doc=doc: e.tensor_tensor(
                 wow[doc % 2].ap().rearrange("p a c -> p (a c)"),
                 cob[doc % 2].ap().rearrange("p a c -> p (a c)"),
                 sob[doc % 2].ap().rearrange("p a c -> p (a c)"), ALU.mult),
             mark=("wob", doc),
             waits=[("wod", doc)] + ([("omm", doc - 2)] if doc >= 2 else []))

    emit_wob(0)
    emit_wob(1)
    for doc in range(16):
        def ommf(e, doc=doc):
            ins = None
            for qh in range(2):
                for fc in range(16):
                    ins = e.matmul(poc[:, doc % 2, qh, :], wow[doc % 2][:, fc, :],
                                   attn[:, fc, qh * 512:(qh + 1) * 512],
                                   start=(fc == 0), stop=(fc == 15))
            return ins
        P.op(b, "tensor", ommf, mark=("omm", doc),
             waits=[("wob", doc)] + ([("oev", doc - 2)] if doc >= 2 else []))
        if doc >= 1:
            dp = doc - 1

            def stf(e, dp=dp):
                ins = None
                for qb in range(8):
                    ins = e.matmul(pst1[:, qb:qb + 1],
                                   ybf[dp % 2][:, qb * 128:(qb + 1) * 128],
                                   ones_bf[:, :], start=(dp == 0 and qb == 0), stop=False)
                    ins = e.matmul(pst2[:, qb:qb + 1],
                                   yb2[:, qb * 128:(qb + 1) * 128],
                                   ones_bf[:, :], start=(dp == 0 and qb == 0), stop=False)
                return ins
            P.op(b, "tensor", stf, mark=("stmm", dp), waits=[("yb2", dp)])

    def stfl(e):
        ins = None
        for qb in range(8):
            ins = e.matmul(pst1[:, qb:qb + 1], ybf[15 % 2][:, qb * 128:(qb + 1) * 128],
                           ones_bf[:, :], start=False, stop=(qb == 7))
            ins = e.matmul(pst2[:, qb:qb + 1], yb2[:, qb * 128:(qb + 1) * 128],
                           ones_bf[:, :], start=False, stop=(qb == 7))
        return ins
    P.op(b, "tensor", stfl, mark=("stmm", 15), waits=[("yb2", 15)])

    for doc in range(16):
        P.op(b, "scalar",
             lambda e, doc=doc: e.activation(
                 t1t[doc % 2][:, :],
                 poc.ap()[:, doc % 2, :, :].rearrange("p a c -> p (a c)"),
                 AF.Identity, bias=bo_t[:, doc:doc + 1]),
             mark=("oev", doc),
             waits=[("omm", doc)] + ([("yadd", doc - 2)] if doc >= 2 else []))
        if doc >= 1:
            emit_gred_s(62 + 2 * doc)
            emit_gred_s(63 + 2 * doc)
        if doc == 15:
            emit_gred_s(94)
            emit_gred_s(95)
    for doc in range(16):
        P.op(b, "vector",
             lambda e, doc=doc: e.tensor_tensor(y[:, doc, :], t1t[doc % 2][:, :],
                                                xqt[doc % 2][:, :], ALU.add),
             mark=("yadd", doc), waits=[("oev", doc), ("xqd", doc)])
        P.op(b, "vector",
             lambda e, doc=doc: e.tensor_copy(ybf[doc % 2][:, :], y[:, doc, :]),
             mark=("ybf", doc),
             waits=[("stmm", doc - 2)] if doc >= 2 else [])
        P.op(b, "vector",
             lambda e, doc=doc: e.tensor_tensor(yb2[:, :], ybf[doc % 2][:, :],
                                                ybf[doc % 2][:, :], ALU.mult),
             mark=("yb2", doc),
             waits=[("stmm", doc - 1)] if doc >= 1 else [])
        if doc == 0:
            P.op(b, "vector", lambda e: e.tensor_copy(rmax[:, :], ybf[0][:, :]))
            P.op(b, "vector", lambda e: e.tensor_copy(rmin[:, :], ybf[0][:, :]),
                 mark=("tree", 0))
        else:
            P.op(b, "vector",
                 lambda e, doc=doc: e.tensor_tensor(rmax[:, :], rmax[:, :],
                                                    ybf[doc % 2][:, :], ALU.max))
            P.op(b, "vector",
                 lambda e, doc=doc: e.tensor_tensor(rmin[:, :], rmin[:, :],
                                                    ybf[doc % 2][:, :], ALU.min),
                 mark=("tree", doc))
        if doc + 2 < 16:
            emit_wob(doc + 2)

    P.op(b, "scalar", lambda e: e.copy(stt["sums1"][:, :], pst1[:, 0:8]),
         waits=[("stmm", 15)])
    P.op(b, "scalar", lambda e: e.copy(stt["sumsq1"][:, :], pst2[:, 0:8]),
         mark="ln1ev")

    s4.close()

    s5 = ExitStack()
    txn = [sb(f"txn{i}", [128, Q], F32, s5) for i in range(2)]
    ptr_ = psum("ptr_", [128, 1024], s5)
    pbc = psum("pbc", [128, 1024], s5)

    b = P.block()

    def ln_chain(nfeat, mu_t, R_t, composed, pre, wait0):
        P.op(b, "vector", lambda e, mu_t=mu_t: e.tensor_scalar(
            mu_t[:, :], stt["sums1"][:, :], 1.0 / nfeat, None, ALU.mult),
            waits=[wait0])
        P.op(b, "vector", lambda e: e.tensor_scalar(
            stt["tmp1"][:, :], stt["sumsq1"][:, :], 1.0 / nfeat, None, ALU.mult))
        P.op(b, "vector", lambda e, mu_t=mu_t: e.tensor_tensor(
            stt["tmp2"][:, :], mu_t[:, :], mu_t[:, :], ALU.mult))
        P.op(b, "vector", lambda e: e.tensor_tensor(
            stt["var"][:, :], stt["tmp1"][:, :], stt["tmp2"][:, :], ALU.subtract))
        if composed:
            P.op(b, "vector", lambda e: e.tensor_scalar(
                stt["tmp1"][:, :], stt["var"][:, :], 1.0 + EPS2, EPS1 * EPS2,
                ALU.mult, ALU.add), mark=pre + "a")
        else:
            eps = EPS2 if nfeat == FF else EPS1
            P.op(b, "vector", lambda e, eps=eps: e.tensor_scalar(
                stt["tmp1"][:, :], stt["var"][:, :], eps, None, ALU.add),
                mark=pre + "a")
        P.op(b, "scalar", lambda e: e.activation(stt["tmp2"][:, :], stt["tmp1"][:, :],
                                                 AF.Sqrt),
             mark=pre + "sq", waits=[pre + "a"])
        P.op(b, "vector", lambda e, R_t=R_t: e.reciprocal(R_t[:, :], stt["tmp2"][:, :]),
             mark=pre + "R", waits=[pre + "sq"])

    for doc in range(16):
        emit_gdma(96 + 2 * doc, "sync")
        emit_gdma(97 + 2 * doc, "gpsimd")
    ln_chain(D, stt["mu"], stt["R"], True, "ln1", "ln1ev")
    P.op(b, "vector", lambda e: e.tensor_scalar(stt["tmp3"][:, :], stt["var"][:, :],
                                                EPS1, None, ALU.add),
         mark="ln1add", waits=["ln1R"])
    P.op(b, "scalar", lambda e: e.activation(stt["tmp3"][:, :], stt["tmp3"][:, :],
                                             AF.Sqrt), mark="ln1sA", waits=["ln1add"])
    P.op(b, "vector", lambda e: e.reciprocal(stt["tmp1"][:, :], stt["tmp3"][:, :]),
         waits=["ln1sA"])
    P.op(b, "vector", lambda e: e.tensor_tensor(stt["rho"][:, :], stt["tmp2"][:, :],
                                                stt["tmp1"][:, :], ALU.mult),
         mark="rho")

    def trred(src, dst_t, op, mk, wt, pp):
        ppb = pp.bitcast(BF16)
        def trf(e, src=src, ppb=ppb):
            ins = None
            for j in range(8):
                ins = e.transpose(ppb[:, j * 128:(j + 1) * 128],
                                  src[:, j * 128:(j + 1) * 128], id128b[:, :])
            return ins
        P.op(b, "tensor", trf, mark=mk + "t", waits=wt)

        def rf(e, dst_t=dst_t, op=op, ppb=ppb):
            ins = None
            for j in range(8):
                ins = e.tensor_reduce(dst_t[:, j:j + 1], ppb[:, j * 128:(j + 1) * 128],
                                      AX.X, op)
            return ins
        P.op(b, "vector", rf, mark=mk + "r", waits=[mk + "t"])

    trred(rmax, stt["mx"], ALU.max, "mx1", [("tree", 15)], ptr_)
    trred(rmin, stt["mn"], ALU.min, "mn1", ["mx1r"], ptr_)

    P.op(b, "vector", lambda e: e.tensor_tensor(stt["tmp1"][:, :], stt["mx"][:, :],
                                                stt["mu"][:, :], ALU.subtract),
         waits=["mn1r", "rho"])
    P.op(b, "vector", lambda e: e.tensor_tensor(stt["tmp2"][:, :], stt["mu"][:, :],
                                                stt["mn"][:, :], ALU.subtract))
    P.op(b, "vector", lambda e: e.tensor_tensor(stt["tmp1"][:, :], stt["tmp1"][:, :],
                                                stt["tmp2"][:, :], ALU.max))
    P.op(b, "vector", lambda e: e.tensor_tensor(stt["gx"][:, :], stt["tmp1"][:, :],
                                                stt["R"][:, :], ALU.mult))
    P.op(b, "vector", lambda e: e.tensor_scalar(stt["cx"][:, :], stt["gx"][:, :],
                                                CLIPB, None, ALU.mult))
    P.op(b, "vector", lambda e: e.tensor_scalar(stt["ncx"][:, :], stt["gx"][:, :],
                                                -CLIPB, None, ALU.mult), mark="cx1")

    def bcast(stat, dstB, key, wkey, pp):
        P.op(b, "tensor", lambda e, stat=stat: e.transpose(pp[0:8, 0:128],
                                                           stat[:, :], id128[:, :]),
             mark=key + "t", waits=[wkey])
        P.op(b, "scalar", lambda e: e.copy(trT[:, :], pp[0:8, 0:128]),
             mark=key + "e", waits=[key + "t"])

        def bmmf(e):
            ins = None
            for qb in range(8):
                ins = e.matmul(pp[:, qb * 128:(qb + 1) * 128],
                               sel8[:, qb * 128:(qb + 1) * 128], trT[:, :],
                               start=True, stop=True)
            return ins
        P.op(b, "tensor", bmmf, mark=key + "m", waits=[key + "e"])
        P.op(b, "vector", lambda e, dstB=dstB: e.tensor_copy(dstB[:, :], pp[:, :]),
             mark=key + "b", waits=[key + "m"])

    bcast(stt["mu"], muB, "bmu", "cx1", pbc)
    bcast(stt["R"], RB, "bR", "bmub", pbc)
    bcast(stt["cx"], cB, "bcx", "bRb", pbc)
    bcast(stt["ncx"], ncB, "bncx", "bcxb", pbc)
    bcast(stt["rho"], rhoB, "brho", "bncxb", pbc)

    for doc in range(16):
        P.op(b, "vector",
             lambda e, doc=doc: e.tensor_tensor(txn[doc % 2][:, :], y[:, doc, :],
                                                muB[:, :], ALU.subtract),
             waits=(["brhob"] if doc == 0 else []))
        P.op(b, "vector",
             lambda e, doc=doc: e.tensor_tensor(y[:, doc, :], txn[doc % 2][:, :],
                                                RB[:, :], ALU.mult),
             mark=("xn", doc))
        P.op(b, "vector",
             lambda e, doc=doc: e.tensor_tensor(txn[doc % 2][:, :], y[:, doc, :],
                                                cB[:, :], ALU.min))
        xw = [] if doc < 8 else [("gtile", 126 if doc < 12 else 127)]
        P.op(b, "vector",
             lambda e, doc=doc: e.tensor_tensor(xh[:, doc, :], txn[doc % 2][:, :],
                                                ncB[:, :], ALU.max),
             mark=("xhat", doc), waits=xw)
        P.op(b, "sync",
             lambda e, doc=doc: e.dma_start(xnspill[doc * 128:(doc + 1) * 128, :],
                                            y[:, doc, :]),
             sem="dX", k=16, mark=("xnsp", doc), waits=[("xn", doc)])
        if doc < 8:
            emit_gred_v(112 + 2 * doc)
            emit_gred_v(113 + 2 * doc)
    for j in range(96, 112):
        emit_gred_s(j)

    s5.close()

    s6 = ExitStack()
    wfw = [arena1[:, i * 2048:(i + 1) * 2048].rearrange("p (a c) -> p a c", a=16)
           for i in range(4)]
    wps = [arena1[:, 8192 + i * 2048:8192 + (i + 1) * 2048] for i in range(4)]
    pf = sb("pf", [128, 2048], BF16, s6)
    nf = sb("nf", [128, 2048], BF16, s6)
    tf = [sb(f"tf{i}", [128, 16, 128], BF16, s6) for i in range(2)]
    pa = sb("pa", [128, 2048], BF16, s6)
    pb_ = sb("pb_", [128, 2048], BF16, s6)
    tp8 = [sb(f"tp8{i}", [128, 2048], FP8, s6) for i in range(2)]
    hsl = [sb(f"hsl{i}", [128, Q], BF16, s6) for i in range(2)]
    h2t = sb("h2t", [128, Q], BF16, s6)
    ph = psum("ph", [128, 2, 1024], s6)
    psh1 = psum("psh1", [128, 512], s6)
    psh2 = psum("psh2", [128, 512], s6)
    pg6 = psum("pg6", [128, 8], s6)

    b = P.block()
    P.op(b, "vector", lambda e: e.reduce_sum(gacc_f[:, 0:1], gacc_f[:, :], axis=AX.X),
         waits=[("gtile", 127)])
    P.op(b, "vector", lambda e: e.reduce_sum(gacc_p[:, 0:1], gacc_p[:, :], axis=AX.X),
         mark="gred")
    P.op(b, "tensor", lambda e: e.matmul(pg6[0:1, 0:1], gacc_f[:, 0:1], ones_f[:, :],
                                         start=True, stop=False), waits=["gred"])
    P.op(b, "tensor", lambda e: e.matmul(pg6[0:1, 1:2], gacc_p[:, 0:1], ones_f[:, :],
                                         start=False, stop=True), mark="gsum")
    P.op(b, "scalar", lambda e: e.copy(g1[0:1, 0:2], pg6[0:1, 0:2]),
         mark="g1ev", waits=["gsum"])
    P.op(b, "vector", lambda e: e.tensor_scalar(g1[0:1, 2:4], g1[0:1, 0:2],
                                                1.0 / NW, None, ALU.mult),
         waits=["g1ev"])
    P.op(b, "vector", lambda e: e.tensor_scalar(g1[0:1, 4:6], g1[0:1, 2:4],
                                                1e-6, 0.5, ALU.add, ALU.mult))
    P.op(b, "vector", lambda e: e.tensor_scalar(g1[0:1, 6:8], g1[0:1, 4:6],
                                                -1.0, None, ALU.mult))
    P.op(b, "vector", lambda e: e.tensor_scalar(g1[0:1, 8:10], g1[0:1, 2:4],
                                                0.5, None, ALU.mult), mark="gscal")
    gcols = [(4, thf), (5, thp), (6, nthf), (7, nthp), (2, gfb), (9, gpb)]
    for i, (col, dst) in enumerate(gcols):
        P.op(b, "tensor",
             lambda e, i=i, col=col: e.matmul(pg6[:, 2 + i:3 + i], onesr_f[:, :],
                                              g1[0:1, col:col + 1], start=True, stop=True),
             mark=("gbc", i), waits=(["gscal"] if i == 0 else []))
    for i, (col, dst) in enumerate(gcols):
        P.op(b, "scalar", lambda e, i=i, dst=dst: e.copy(dst[:, :], pg6[:, 2 + i:3 + i]),
             mark=("gbcev", i), waits=([("gbc", 5)] if i == 0 else []))

    def emit_wfd(oc):
        P.op(b, "sync",
             lambda e, oc=oc: e.dma_start(
                 wfw[oc % 4],
                 wfc[:, oc * 128:(oc + 1) * 128].rearrange("(dc p) f -> p dc f", p=128)),
             sem=f"fW{oc % 4}", k=16, mark=("wfd", oc),
             waits=([("nfc", oc - 4)] if oc >= 4 else []))

    def emit_trio(oc):
        P.op(b, "vector",
             lambda e, oc=oc: e.tensor_scalar(
                 pf[:, :], wfw[oc % 4].rearrange("p a c -> p (a c)"),
                 thf[:, :], None, ALU.is_gt),
             mark=("pfc", oc),
             waits=[("wfd", oc)] + ([("fmm", oc - 2)] if oc >= 2 else [("gbcev", 5)]))
        P.op(b, "vector",
             lambda e, oc=oc: e.tensor_scalar(
                 nf[:, :], wfw[oc % 4].rearrange("p a c -> p (a c)"),
                 nthf[:, :], None, ALU.is_lt),
             mark=("nfc", oc))
        P.op(b, "vector",
             lambda e, oc=oc: e.tensor_tensor(
                 tf[oc % 2].ap().rearrange("p a c -> p (a c)"),
                 pf[:, :], nf[:, :], ALU.subtract),
             mark=("tfc", oc))

    def emit_slow(j):
        P.op(b, "vector",
             lambda e, j=j: e.tensor_tensor(tp8[j % 2][:, :], pa[:, :], pb_[:, :],
                                            ALU.add),
             mark=("tpc", j),
             waits=[("sgb", j)] + ([("tpw", j - 2)] if j >= 2 else []))
        if j >= 1:
            P.op(b, "vector",
                 lambda e, j=j: e.tensor_tensor(h2t[:, :], hsl[(j - 1) % 2][:, :],
                                                hsl[(j - 1) % 2][:, :], ALU.mult),
                 mark=("h2", j - 1),
                 waits=[("hev", j - 1)] + ([("hst", j - 2)] if j >= 2 else []))
        if j == 0:
            P.op(b, "vector", lambda e: e.tensor_copy(rmax[:, :], hsl[0][:, :]),
                 waits=[("hev", 0)])
            P.op(b, "vector", lambda e: e.tensor_copy(rmin[:, :], hsl[0][:, :]),
                 mark=("htree", 0))
        else:
            P.op(b, "vector",
                 lambda e, j=j: e.tensor_tensor(rmax[:, :], rmax[:, :],
                                                hsl[j % 2][:, :], ALU.max),
                 waits=[("hev", j)])
            P.op(b, "vector",
                 lambda e, j=j: e.tensor_tensor(rmin[:, :], rmin[:, :],
                                                hsl[j % 2][:, :], ALU.min),
                 mark=("htree", j))

    emit_wfd(0)
    emit_wfd(1)
    for oc in range(64):
        if oc + 2 < 64:
            emit_wfd(oc + 2)
        P.op(b, "gpsimd",
             lambda e, oc=oc: e.dma_start(wps[oc % 4][:, :],
                                          wpr[oc * 128:(oc + 1) * 128, :]),
             sem=f"pW{oc % 4}", k=16, mark=("wpd", oc),
             waits=[("sgb", oc - 4)] if oc >= 4 else [])
        emit_trio(oc)
        if oc >= 1:
            emit_slow(oc - 1)
        P.op(b, "scalar",
             lambda e, oc=oc: e.activation(pa[:, :], wps[oc % 4][:, :], AF.Sign,
                                           bias=nthp[:, :]),
             waits=[("wpd", oc)] + ([("tpc", oc - 1)] if oc >= 1 else [("gbcev", 5)]))
        P.op(b, "scalar",
             lambda e, oc=oc: e.activation(pb_[:, :], wps[oc % 4][:, :], AF.Sign,
                                           bias=thp[:, :]),
             mark=("sgb", oc))

        def fmmf(e, oc=oc):
            ins = None
            for qh in range(2):
                for dc in range(16):
                    ins = e.matmul(ph[:, oc % 2, qh * 512:(qh + 1) * 512],
                                   tf[oc % 2][:, dc, :],
                                   xh[:, dc, qh * 512:(qh + 1) * 512],
                                   start=(dc == 0), stop=(dc == 15))
            return ins
        P.op(b, "tensor", fmmf, mark=("fmm", oc),
             waits=[("tfc", oc)] + ([("hev", oc - 2)] if oc >= 2 else []))
        if oc >= 1:
            def hstf(e, op=oc - 1):
                ins = None
                for qb in range(8):
                    ins = e.matmul(psh1[:, qb:qb + 1],
                                   hsl[op % 2][:, qb * 128:(qb + 1) * 128],
                                   ones_bf[:, :], start=(op == 0 and qb == 0), stop=False)
                    ins = e.matmul(psh2[:, qb:qb + 1],
                                   h2t[:, qb * 128:(qb + 1) * 128],
                                   ones_bf[:, :], start=(op == 0 and qb == 0), stop=False)
                return ins
            P.op(b, "tensor", hstf, mark=("hst", oc - 1), waits=[("h2", oc - 1)])
        P.op(b, "scalar",
             lambda e, oc=oc: e.activation(hsl[oc % 2][:, :],
                                           ph.ap()[:, oc % 2, :], AF.Gelu,
                                           scale=gfb[:, :]),
             mark=("hev", oc),
             waits=[("fmm", oc)] + ([("hsp", oc - 2), ("hst", oc - 2),
                                     ("htree", oc - 2)] if oc >= 2 else []))
        P.op(b, "sync",
             lambda e, oc=oc: e.dma_start(hspill[oc * 128:(oc + 1) * 128, :],
                                          hsl[oc % 2][:, :]),
             sem=f"hW{oc % 2}", k=16, mark=("hsp", oc), waits=[("hev", oc)])
        if oc >= 1:
            P.op(b, "sync",
                 lambda e, oc=oc: e.dma_start(tproj[(oc - 1) * 128:oc * 128, :],
                                              tp8[(oc - 1) % 2][:, :]),
                 sem=f"tW{(oc - 1) % 2}", k=16, mark=("tpw", oc - 1),
                 waits=[("tpc", oc - 1)])
    emit_slow(63)
    P.op(b, "sync",
         lambda e: e.dma_start(tproj[63 * 128:64 * 128, :], tp8[63 % 2][:, :]),
         sem=f"tW{63 % 2}", k=16, mark=("tpw", 63), waits=[("tpc", 63)])
    P.op(b, "vector",
         lambda e: e.tensor_tensor(h2t[:, :], hsl[63 % 2][:, :], hsl[63 % 2][:, :],
                                   ALU.mult),
         mark=("h2", 63), waits=[("hev", 63), ("hst", 62)])

    def hstl(e):
        ins = None
        for qb in range(8):
            ins = e.matmul(psh1[:, qb:qb + 1], hsl[63 % 2][:, qb * 128:(qb + 1) * 128],
                           ones_bf[:, :], start=False, stop=(qb == 7))
            ins = e.matmul(psh2[:, qb:qb + 1], h2t[:, qb * 128:(qb + 1) * 128],
                           ones_bf[:, :], start=False, stop=(qb == 7))
        return ins
    P.op(b, "tensor", hstl, mark=("hst", 63), waits=[("h2", 63)])
    P.op(b, "scalar", lambda e: e.copy(stt["sums1"][:, :], psh1[:, 0:8]),
         waits=[("hst", 63)])
    P.op(b, "scalar", lambda e: e.copy(stt["sumsq1"][:, :], psh2[:, 0:8]),
         mark="ln2ev")

    s6.close()

    s7 = ExitStack()
    hrb = [sb(f"hrb{i}", [128, 2, Q], BF16, s7) for i in range(2)]
    th1 = [sb(f"th1{i}", [128, 2, Q], F32, s7) for i in range(1)]
    tpb = [sb(f"tpb{i}", [128, 8, 256], BF16, s7) for i in range(2)]
    xnr = sb("xnr", [128, Q], F32, s7)
    t2t = sb("t2t", [128, Q], F32, s7)
    y2t = [sb(f"y2t{i}", [128, Q], F32, s7) for i in range(2)]
    y2b = sb("y2b", [128, Q], BF16, s7)
    y2b2 = sb("y2b2", [128, Q], BF16, s7)
    py = psum("py", [128, 2, 2, 512], s7)
    psy1 = psum("psy1", [128, 512], s7)
    psy2 = psum("psy2", [128, 512], s7)
    pbc7 = psum("pbc7", [128, 1024], s7)

    b = P.block()
    ln_chain(FF, stt["mu2"], stt["R2"], False, "ln2", "ln2ev")
    trred(rmax, stt["mx"], ALU.max, "mx2", [("htree", 63)], pbc7)
    trred(rmin, stt["mn"], ALU.min, "mn2", ["mx2r"], pbc7)
    P.op(b, "vector", lambda e: e.tensor_tensor(stt["tmp1"][:, :], stt["mx"][:, :],
                                                stt["mu2"][:, :], ALU.subtract),
         waits=["mn2r", "ln2R"])
    P.op(b, "vector", lambda e: e.tensor_tensor(stt["tmp2"][:, :], stt["mu2"][:, :],
                                                stt["mn"][:, :], ALU.subtract))
    P.op(b, "vector", lambda e: e.tensor_tensor(stt["tmp1"][:, :], stt["tmp1"][:, :],
                                                stt["tmp2"][:, :], ALU.max))
    P.op(b, "vector", lambda e: e.tensor_tensor(stt["gx"][:, :], stt["tmp1"][:, :],
                                                stt["R2"][:, :], ALU.mult))
    P.op(b, "vector", lambda e: e.tensor_scalar(stt["c2"][:, :], stt["gx"][:, :],
                                                CLIPB, None, ALU.mult))
    P.op(b, "vector", lambda e: e.tensor_scalar(stt["nc2"][:, :], stt["gx"][:, :],
                                                -CLIPB, None, ALU.mult), mark="cx2")
    bcast(stt["mu2"], muB, "cmu", "cx2", pbc7)
    bcast(stt["R2"], RB, "cR", "cmub", pbc7)
    bcast(stt["c2"], cB, "ccx", "cRb", pbc7)
    bcast(stt["nc2"], ncB, "cncx", "ccxb", pbc7)

    def emit_tpr(dw, ocg):
        if ocg >= 2:
            wt = [("pmm", dw, ocg - 2)]
        elif dw >= 1:
            wt = [("pmm", dw - 1, 6 + ocg)]
        else:
            wt = [("tpw", 63)]
        P.op(b, "gpsimd",
             lambda e, dw=dw, ocg=ocg: e.dma_start(
                 tpb[ocg % 2].ap(),
                 tproj[ocg * 1024:(ocg + 1) * 1024, dw * 256:(dw + 1) * 256]
                 .rearrange("(g p) d -> p g d", p=128)),
             sem=f"tR{ocg % 2}", k=16, mark=("tpr", dw, ocg), waits=wt)

    BC = lambda t: t.ap().unsqueeze(1).broadcast_to([128, 2, Q])
    for tl in range(32):
        oc0 = tl * 2
        P.op(b, "sync",
             lambda e, tl=tl, oc0=oc0: e.dma_start(
                 hrb[tl % 2].ap(),
                 hspill[oc0 * 128:(oc0 + 2) * 128, :]
                 .rearrange("(a p) c -> p a c", p=128)),
             sem=f"hR{tl % 2}", k=16, mark=("hrd", tl),
             waits=([("x2a", tl - 2)] if tl >= 2 else []))
        buf = th1[0]
        P.op(b, "vector",
             lambda e, tl=tl, buf=buf: e.tensor_tensor(
                 buf[:, :, :], hrb[tl % 2][:, :, :], BC(muB), ALU.subtract),
             waits=[("hrd", tl), "cncxb"])
        P.op(b, "vector",
             lambda e, tl=tl, buf=buf: e.tensor_tensor(
                 hrb[tl % 2][:, :, :], buf[:, :, :], BC(RB), ALU.mult))
        P.op(b, "vector",
             lambda e, tl=tl: e.tensor_tensor(hrb[tl % 2][:, :, :],
                                              hrb[tl % 2][:, :, :], BC(cB),
                                              ALU.min))
        P.op(b, "vector",
             lambda e, tl=tl, oc0=oc0: e.tensor_tensor(
                 x2w(oc0), hrb[tl % 2][:, :, :], BC(ncB), ALU.max),
             mark=("x2a", tl))
        if tl % 4 == 3:
            emit_tpr(0, tl // 4)
    for dw in range(8):
        for ocg in range(8):
            if dw >= 1:
                emit_tpr(dw, ocg)

            def pmmf(e, dw=dw, ocg=ocg):
                ins = None
                for gi in range(8):
                    oc = ocg * 8 + gi
                    for dj in range(2):
                        for qh in range(2):
                            ins = e.matmul(py[:, dj, qh, :],
                                           tpb[ocg % 2][:, gi, dj * 128:(dj + 1) * 128],
                                           x2sl(oc, qh * 512, (qh + 1) * 512),
                                           start=(oc == 0), stop=(oc == 63))
                return ins
            wt2 = [("tpr", dw, ocg), ("x2a", ocg * 4 + 3)]
            if ocg == 0 and dw >= 1:
                wt2.append(("y2e", dw - 1, 1))
            P.op(b, "tensor", pmmf, mark=("pmm", dw, ocg), waits=wt2)
        for dj in range(2):
            doc = dw * 2 + dj
            P.op(b, "scalar",
                 lambda e, dj=dj: e.activation(
                     t2t[:, :], py.ap()[:, dj, :, :].rearrange("p a c -> p (a c)"),
                     AF.Identity, scale=gpb[:, :]),
                 mark=("y2e", dw, dj),
                 waits=[("pmm", dw, 7)] + ([("y2add", doc - 1)] if doc >= 1 else []))
            P.op(b, "sync",
                 lambda e, doc=doc: e.dma_start(xnr[:, :],
                                                xnspill[doc * 128:(doc + 1) * 128, :]),
                 sem="xR", k=16, mark=("xnr", doc),
                 waits=[("y2add", doc - 1)] if doc >= 1 else [])
            P.op(b, "vector",
                 lambda e, doc=doc: e.tensor_tensor(y2t[doc % 2][:, :], xnr[:, :],
                                                    rhoB[:, :], ALU.mult),
                 waits=[("xnr", doc), ("y2e", dw, dj)] +
                       ([("y2sp", doc - 2)] if doc >= 2 else []))
            P.op(b, "vector",
                 lambda e, doc=doc: e.tensor_tensor(y2t[doc % 2][:, :],
                                                    y2t[doc % 2][:, :], t2t[:, :],
                                                    ALU.add),
                 mark=("y2add", doc))
            P.op(b, "vector",
                 lambda e, doc=doc: e.tensor_copy(y2b[:, :], y2t[doc % 2][:, :]),
                 waits=[("y2st", doc - 1)] if doc >= 1 else [])
            P.op(b, "vector",
                 lambda e: e.tensor_tensor(y2b2[:, :], y2b[:, :], y2b[:, :], ALU.mult),
                 mark=("y2b2", doc))
            if doc == 0:
                P.op(b, "vector", lambda e: e.tensor_copy(rmax[:, :], y2b[:, :]),
                     waits=["mn2t"])
                P.op(b, "vector", lambda e: e.tensor_copy(rmin[:, :], y2b[:, :]),
                     mark=("y2tree", 0))
            else:
                P.op(b, "vector",
                     lambda e: e.tensor_tensor(rmax[:, :], rmax[:, :], y2b[:, :],
                                               ALU.max))
                P.op(b, "vector",
                     lambda e: e.tensor_tensor(rmin[:, :], rmin[:, :], y2b[:, :],
                                               ALU.min),
                     mark=("y2tree", doc))

            def y2stf(e, doc=doc):
                ins = None
                for qb in range(8):
                    ins = e.matmul(psy1[:, qb:qb + 1], y2b[:, qb * 128:(qb + 1) * 128],
                                   ones_bf[:, :], start=(doc == 0 and qb == 0),
                                   stop=(doc == 15 and qb == 7))
                    ins = e.matmul(psy2[:, qb:qb + 1], y2b2[:, qb * 128:(qb + 1) * 128],
                                   ones_bf[:, :], start=(doc == 0 and qb == 0),
                                   stop=(doc == 15 and qb == 7))
                return ins
            P.op(b, "tensor", y2stf, mark=("y2st", doc), waits=[("y2b2", doc)])
            P.op(b, "sync",
                 lambda e, doc=doc: e.dma_start(y2spill[doc * 128:(doc + 1) * 128, :],
                                                y2t[doc % 2][:, :]),
                 sem=f"yW{doc % 2}", k=16, mark=("y2sp", doc), waits=[("y2add", doc)])

    P.op(b, "scalar", lambda e: e.copy(stt["sums1"][:, :], psy1[:, 0:8]),
         waits=[("y2st", 15)])
    P.op(b, "scalar", lambda e: e.copy(stt["sumsq1"][:, :], psy2[:, 0:8]),
         mark="ln3ev")

    s7.close()

    s9 = ExitStack()
    y2r = [sb(f"y2r{i}", [128, Q], F32, s9) for i in range(2)]
    to1 = [sb(f"to1{i}", [128, Q], F32, s9) for i in range(2)]
    pbc9 = psum("pbc9", [128, 1024], s9)

    b = P.block()
    ln_chain(D, stt["mu3"], stt["R3"], False, "ln3", "ln3ev")
    bcast(stt["mu3"], muB, "dmu", "ln3R", pbc9)
    bcast(stt["R3"], RB, "dR", "dmub", pbc9)

    for doc in range(16):
        P.op(b, "sync",
             lambda e, doc=doc: e.dma_start(y2r[doc % 2][:, :],
                                            y2spill[doc * 128:(doc + 1) * 128, :]),
             sem=f"yR{doc % 2}", k=16, mark=("y2r", doc),
             waits=[("outv", doc - 2)] if doc >= 2 else [])
        P.op(b, "vector",
             lambda e, doc=doc: e.tensor_tensor(to1[doc % 2][:, :], y2r[doc % 2][:, :],
                                                muB[:, :], ALU.subtract),
             waits=[("y2r", doc), "dRb"] + ([("outd", doc - 2)] if doc >= 2 else []))
        P.op(b, "vector",
             lambda e, doc=doc: e.tensor_tensor(to1[doc % 2][:, :], to1[doc % 2][:, :],
                                                RB[:, :], ALU.mult),
             mark=("outv", doc))
        P.op(b, "sync",
             lambda e, doc=doc: e.dma_start(out[doc * 128:(doc + 1) * 128, :],
                                            to1[doc % 2][:, :]),
             sem=f"oD{doc % 2}", k=16, mark=("outd", doc), waits=[("outv", doc)])
    P.op(b, "sync", lambda e: e.nop(), sem="dX", waits=[("outd", 14), ("outd", 15)])

    s9.close()
    P.resolve()
    P.emit(nc, sems)
    gs.close()
    return nc


_NC_CACHE = {}


def _get_nc():
    if "nc" not in _NC_CACHE:
        _NC_CACHE["nc"] = build_program()
    return _NC_CACHE["nc"]


def kernel(L0_attn_x, L0_attn_wq_codes, L0_attn_wq_scales, L0_attn_bq,
           L0_attn_wk_codes, L0_attn_wk_scales, L0_attn_bk,
           L0_attn_wv_codes, L0_attn_wv_scales, L0_attn_bv,
           L0_attn_wo_codes, L0_attn_wo_scales, L0_attn_bo,
           L0_ffn_w_fc, L0_ffn_w_proj):
    x = np.asarray(L0_attn_x, dtype=np.float32)
    B = x.shape[0]
    rep = lambda s: np.ascontiguousarray(np.repeat(np.asarray(s, np.float32), 32, axis=0))
    bias_t = lambda v: np.ascontiguousarray(np.asarray(v, np.float32).reshape(16, 128).T)
    common = {
        "wq": np.ascontiguousarray(np.asarray(L0_attn_wq_codes, np.float32)),
        "wk": np.ascontiguousarray(np.asarray(L0_attn_wk_codes, np.float32)),
        "wv": np.ascontiguousarray(np.asarray(L0_attn_wv_codes, np.float32)),
        "wo": np.ascontiguousarray(np.asarray(L0_attn_wo_codes, np.float32)),
        "sq": rep(L0_attn_wq_scales), "sk": rep(L0_attn_wk_scales),
        "sv": rep(L0_attn_wv_scales), "so": rep(L0_attn_wo_scales),
        "bq": bias_t(L0_attn_bq), "bk": bias_t(L0_attn_bk),
        "bv": bias_t(L0_attn_bv), "bo": bias_t(L0_attn_bo),
        "wfc": np.ascontiguousarray(np.asarray(L0_ffn_w_fc, np.float32).T),
        "wpr": np.ascontiguousarray(np.asarray(L0_ffn_w_proj, np.float32).T),
    }
    in_maps = []
    for c in range(8):
        bb, half = c // 2, c % 2
        xT = x[bb].T
        xr = np.ascontiguousarray(np.concatenate(
            [xT[:, half * Q:], xT[:, :half * Q]], axis=1))
        m = dict(common)
        m["xt"] = xr
        m["xq"] = np.ascontiguousarray(xr[:, :Q])
        in_maps.append(m)
    nc = _get_nc()
    res = run_bass_kernel_spmd(nc, in_maps, core_ids=list(range(8)))
    y_out = np.empty((B, S, D), dtype=np.float32)
    for c in range(8):
        bb, half = c // 2, c % 2
        y_out[bb, half * Q:(half + 1) * Q, :] = np.asarray(res.results[c]["out"]).T
    return y_out


# version 15
# speedup vs baseline: 1.1496x; 1.0345x over previous
import numpy as np
import ml_dtypes
from contextlib import ExitStack
import concourse.bass as bass
import concourse.mybir as mybir
from concourse.bass_utils import run_bass_kernel_spmd

F32 = mybir.dt.float32
BF16 = mybir.dt.bfloat16
FP8 = mybir.dt.float8e4
AF = mybir.ActivationFunctionType
ALU = mybir.AluOpType
AX = mybir.AxisListType

D = 2048
S = 2048
Q = 1024
H = 16
FF = 8192
NW = float(D * FF)
ISQ = float(1.0 / np.sqrt(128.0))
EPS1 = 1e-5
EPS2 = 1e-6
CLIPB = float((128.0 - 1e-6) / 128.0)

SB_TOP = 229344
QT_R = (SB_TOP - 32768, SB_TOP)
XH_R = (SB_TOP - 32768, SB_TOP)
AT_R = (SB_TOP - 65536, SB_TOP - 32768)
PT_R = [(SB_TOP - 131072 + i * 16384, SB_TOP - 131072 + (i + 1) * 16384)
        for i in range(4)]
Y_R = (SB_TOP - 131072, SB_TOP - 65536)
X2_R = (SB_TOP - 131072, SB_TOP)

ENGS = ["sync", "tensor", "vector", "scalar", "gpsimd"]
SEM_OF = {"sync": "dS", "tensor": "tsem", "vector": "vsem", "scalar": "ssem",
          "gpsimd": "gsem"}


class Prog:
    def __init__(self):
        self.blocks = []
        self.ev = {}

    def block(self):
        b = {e: [] for e in ENGS}
        self.blocks.append(b)
        return b

    def op(self, b, eng, fn, *, mark=None, sem=None, k=1, waits=()):
        b[eng].append((fn, mark, sem or SEM_OF[eng], k, tuple(waits)))

    def resolve(self):
        self.ev.clear()
        self.drains = {}
        self.selfw = {}
        self.blockstart = {}
        cnt = {}
        for bi, b in enumerate(self.blocks):
            self.blockstart[bi] = [(s, cnt.get(s, 0))
                                   for s in ("tsem", "vsem", "ssem", "gsem")
                                   if cnt.get(s, 0)]
            for eng in ENGS:
                seen = {}
                own = SEM_OF[eng]
                for oi, (fn, mark, sem, k, waits) in enumerate(b[eng]):
                    if eng in ("vector", "scalar", "gpsimd"):
                        self.selfw[(bi, eng, oi)] = cnt.get(own, 0)
                    cnt[sem] = cnt.get(sem, 0) + k
                    if mark is not None:
                        assert mark not in self.ev, f"duplicate mark {mark}"
                        self.ev[mark] = (sem, cnt[sem])
                    if sem not in ("tsem", "vsem", "ssem", "gsem"):
                        seen[sem] = cnt[sem]
                if seen:
                    self.drains[(bi, eng)] = sorted(seen.items())

    def emit(self, nc, sems):
        for bi, b in enumerate(self.blocks):
            with nc.Block() as blk:
                for eng in ENGS:
                    ops = b[eng]
                    if not ops:
                        continue
                    drains = self.drains.get((bi, eng), [])

                    def body(e, ops=ops, drains=drains, bi=bi, eng=eng):
                        own = sems.get(SEM_OF[eng])
                        for s_, v_ in self.blockstart.get(bi, []):
                            e.wait_ge(sems[s_], v_)
                        for oi, (fn, mark, sem, k, waits) in enumerate(ops):
                            sv = self.selfw.get((bi, eng, oi))
                            if sv:
                                e.wait_ge(own, sv)
                            for w in waits:
                                if w in self.ev:
                                    ws, wc = self.ev[w]
                                    e.wait_ge(sems[ws], wc)
                            ins = fn(e)
                            assert ins is not None
                            ins.then_inc(sems[sem], k)
                        for s_, v_ in drains:
                            e.wait_ge(sems[s_], v_)
                    getattr(blk, eng)(body)


def build_program(debug=False):
    nc = bass.Bass("TRN2", target_bir_lowering=False)
    gs = ExitStack()
    P = Prog()

    xt = nc.dram_tensor("xt", [D, S], F32, kind="ExternalInput")
    xq = nc.dram_tensor("xq", [D, Q], F32, kind="ExternalInput")
    wq = nc.dram_tensor("wq", [D, D], F32, kind="ExternalInput")
    wk = nc.dram_tensor("wk", [D, D], F32, kind="ExternalInput")
    wv = nc.dram_tensor("wv", [D, D], F32, kind="ExternalInput")
    wo = nc.dram_tensor("wo", [D, D], F32, kind="ExternalInput")
    sq = nc.dram_tensor("sq", [D, D], F32, kind="ExternalInput")
    sk = nc.dram_tensor("sk", [D, D], F32, kind="ExternalInput")
    sv = nc.dram_tensor("sv", [D, D], F32, kind="ExternalInput")
    so = nc.dram_tensor("so", [D, D], F32, kind="ExternalInput")
    bqd = nc.dram_tensor("bq", [128, 16], F32, kind="ExternalInput")
    bkd = nc.dram_tensor("bk", [128, 16], F32, kind="ExternalInput")
    bvd = nc.dram_tensor("bv", [128, 16], F32, kind="ExternalInput")
    bod = nc.dram_tensor("bo", [128, 16], F32, kind="ExternalInput")
    wfc = nc.dram_tensor("wfc", [D, FF], F32, kind="ExternalInput")
    wpr = nc.dram_tensor("wpr", [FF, D], F32, kind="ExternalInput")
    out = nc.dram_tensor("out", [Q, D], F32, kind="ExternalOutput")

    kd = {"kind": "ExternalOutput"} if debug else {}
    kspill = nc.dram_tensor("kspill", [D, S], BF16, **kd)
    vspill = nc.dram_tensor("vspill", [H, S, 128], BF16, **kd)
    tproj = nc.dram_tensor("tproj", [FF, D], FP8, **kd)
    hspill = nc.dram_tensor("hspill", [FF, Q], BF16, **kd)
    xnspill = nc.dram_tensor("xnspill", [D, Q], F32, **kd)
    y2Tsp = nc.dram_tensor("y2Tsp", [Q, D], F32, **kd)

    semnames = (["tsem", "vsem", "ssem", "gsem", "ost", "dX", "gX", "kSp", "xR"]
                + [f"{p}{i}" for p in ["gW", "vT", "vP", "kR", "vR", "tP",
                                       "xQ", "oW", "hW", "tW", "hR",
                                       "tR", "yW", "yR", "oD"] for i in range(2)]
                + [f"{p}{i}" for p in ["fW", "pW", "gG", "gS"] for i in range(4)])
    sems = {n: gs.enter_context(nc.semaphore(n)) for n in semnames}

    def sb(name, shape, dt, scope, **kw):
        return scope.enter_context(nc.sbuf_tensor(name, shape, dt, **kw))

    def psum(name, shape, scope):
        return scope.enter_context(nc.psum_tensor(name, shape, F32))

    ones_bf = sb("ones_bf", [128, 1], BF16, gs)
    ones_f = sb("ones_f", [128, 1], F32, gs)
    onesr_f = sb("onesr_f", [1, 128], F32, gs)
    id128 = sb("id128", [128, 128], F32, gs)
    id128b = sb("id128b", [128, 128], BF16, gs)
    sel8 = sb("sel8", [8, 1024], F32, gs)
    bq_s = sb("bq_s", [128, 16], F32, gs)
    bk_t = sb("bk_t", [128, 16], F32, gs)
    bv_t = sb("bv_t", [128, 16], F32, gs)
    bo_t = sb("bo_t", [128, 16], F32, gs)
    thf = sb("thf", [128, 1], F32, gs)
    nthf = sb("nthf", [128, 1], F32, gs)
    thp = sb("thp", [128, 1], F32, gs)
    nthp = sb("nthp", [128, 1], F32, gs)
    gfb = sb("gfb", [128, 1], F32, gs)
    gpb = sb("gpb", [128, 1], F32, gs)
    gacc_f = sb("gacc_f", [128, 64], F32, gs)
    gacc_p = sb("gacc_p", [128, 64], F32, gs)
    g1 = sb("g1", [1, 16], F32, gs)
    sums = None
    stn = ["sums1", "sumsq1", "mu", "var", "R", "rho", "gx", "cx", "ncx",
           "mu2", "R2", "c2", "nc2", "mu3", "R3", "tmp1", "tmp2", "tmp3",
           "mx", "mn"]
    stt = {n: sb("st_" + n, [128, 8], F32, gs) for n in stn}
    muB = sb("muB", [128, Q], F32, gs)
    RB = sb("RB", [128, Q], F32, gs)
    cB = sb("cB", [128, Q], BF16, gs)
    ncB = sb("ncB", [128, Q], BF16, gs)
    rhoB = sb("rhoB", [128, Q], F32, gs)
    rmax = sb("rmax", [128, Q], BF16, gs)
    rmin = sb("rmin", [128, Q], BF16, gs)
    rminf = rmin.bitcast(F32)
    sums = rminf[:, 336:464]
    trT = sb("trT", [8, 128], F32, gs)
    arena1 = sb("arena1", [128, 16384], F32, gs)
    arena2 = sb("arena2", [128, 16384], BF16, gs)
    arena3 = sb("arena3", [128, 16384], BF16, gs)
    a1b = arena1.bitcast(BF16)
    a2f = arena2.bitcast(F32)
    Qt = arena3.ap().rearrange("p (a c) -> p a c", a=16)
    attn = arena2.ap().rearrange("p (a c) -> p a c", a=16)
    PT = [a1b[:, i * 8192:(i + 1) * 8192].rearrange("p (kc i) -> p kc i", kc=16)
          for i in range(4)]
    y = arena1.ap().rearrange("p (a c) -> p a c", a=16)
    xh = arena3.ap().rearrange("p (a c) -> p a c", a=16)
    x2parts = [a1b.rearrange("p (a c) -> p a c", a=32),
               arena2.ap().rearrange("p (a c) -> p a c", a=16),
               arena3.ap().rearrange("p (a c) -> p a c", a=16)]

    def x2w(oc0):
        if oc0 < 32:
            return x2parts[0][:, oc0:oc0 + 2, :]
        if oc0 < 48:
            return x2parts[1][:, oc0 - 32:oc0 - 30, :]
        return x2parts[2][:, oc0 - 48:oc0 - 46, :]

    def x2sl(oc, c0, c1):
        if oc < 32:
            return x2parts[0][:, oc, c0:c1]
        if oc < 48:
            return x2parts[1][:, oc - 32, c0:c1]
        return x2parts[2][:, oc - 48, c0:c1]

    b = P.block()
    for dst, src in [(bq_s, bqd), (bk_t, bkd), (bv_t, bvd), (bo_t, bod)]:
        P.op(b, "sync", lambda e, d=dst, s=src: e.dma_start(d[:, :], s[:, :]),
             sem="dX", k=16, mark=("bld", dst.name))
    for t_ in [ones_f, onesr_f, ones_bf, id128]:
        P.op(b, "vector", lambda e, t_=t_: e.memset(t_[:, :], 1.0))
    P.op(b, "vector", lambda e: e.memset(sel8[:, :], 1.0), mark="consts")
    P.op(b, "vector",
         lambda e: e.tensor_scalar(bq_s[:, :], bq_s[:, :], ISQ, None, ALU.mult),
         waits=[("bld", "bo_t")], mark="bqs")
    P.op(b, "vector", lambda e: e.memset(id128b[:, :], 1.0), mark="constsb")
    P.op(b, "gpsimd",
         lambda e: e.affine_select(id128[:, :], id128[:, :], [[1, 128]],
                                   ALU.is_equal, 0.0, base=0, channel_multiplier=-1),
         waits=["consts", "constsb"], mark="id128")
    P.op(b, "gpsimd",
         lambda e: e.affine_select(id128b[:, :], id128b[:, :], [[1, 128]],
                                   ALU.is_equal, 0.0, base=0, channel_multiplier=-1),
         mark="id128b")
    P.op(b, "gpsimd",
         lambda e: e.affine_select(sel8.ap().rearrange("p (b c) -> p b c", b=8),
                                   sel8.ap().rearrange("p (b c) -> p b c", b=8),
                                   [[1, 8], [0, 128]], ALU.is_equal, 0.0,
                                   base=0, channel_multiplier=-1),
         mark="sel8")

    s1 = ExitStack()
    x_fm = a1b.rearrange("p (a c) -> p a c", a=16)
    kev = sb("kev", [128, 4, 512], BF16, s1)
    ones8c = sb("ones8c", [128, 8], BF16, s1)
    ones8r = sb("ones8r", [8, 128], BF16, s1)
    mk2b8 = sb("mk2b8", [8, 16], BF16, s1)
    k2c = sb("k2c", [128, 2, 512], BF16, s1)
    kmx8 = rminf
    q2c = sb("q2c", [128, 8, 128], BF16, s1)
    ckb = [sb(f"ckb{i}", [128, 16, 128], BF16, s1) for i in range(2)]
    skb = [sb(f"skb{i}", [128, 16, 128], BF16, s1) for i in range(2)]
    wkb = [sb(f"wkb{i}", [128, 16, 128], BF16, s1) for i in range(2)]
    vrow = [sb(f"vrow{i}", [128, 4, 512], BF16, s1) for i in range(2)]
    vtrb = [sb(f"vtrb{i}", [128, 16, 128], BF16, s1, is_transpose_dest=True)
            for i in range(2)]
    pk = psum("pk", [128, 4, 512], s1)
    pkn = psum("pkn", [8, 2, 512], s1)

    def wcon(mi):
        if mi < 16:
            return ("pkg", mi * 4 + 3)
        if mi < 32:
            return ("pkg", 64 + (mi - 16) * 4 + 3)
        return ("pkg", 128 + (mi - 32) * 2 + 1)

    b = P.block()
    P.op(b, "gpsimd",
         lambda e: e.dma_start(x_fm,
                               xt.ap().rearrange("(dc p) t -> p dc t", p=128)),
         sem="gX", k=16, mark="x_fm")
    mats = [(wk, sk), (wv, sv), (wq, sq)]
    wdst = {"i": 0}

    def emit_wd(upto):
        while wdst["i"] <= min(upto, 47):
            mi = wdst["i"]
            m, fc = divmod(mi, 16)
            cod, scl = mats[m]
            wt = [wcon(mi - 2)] if mi >= 2 else []
            P.op(b, "gpsimd",
                 lambda e, cod=cod, fc=fc, mi=mi: e.dma_start(
                     ckb[mi % 2].ap(),
                     cod[:, fc * 128:(fc + 1) * 128]
                     .rearrange("(dc p) f -> p dc f", p=128)),
                 sem=f"gW{mi % 2}", k=16, waits=wt)
            P.op(b, "gpsimd",
                 lambda e, scl=scl, fc=fc, mi=mi: e.dma_start(
                     skb[mi % 2].ap(),
                     scl[:, fc * 128:(fc + 1) * 128]
                     .rearrange("(dc p) f -> p dc f", p=128)),
                 sem=f"gW{mi % 2}", k=16, mark=("wdma", mi))
            wdst["i"] += 1
    wbst = {"i": 0}

    def emit_wb(upto):
        while wbst["i"] <= min(upto, 47):
            mi = wbst["i"]
            wt = [("wdma", mi)] + ([wcon(mi - 2)] if mi >= 2 else [])
            P.op(b, "vector",
                 lambda e, mi=mi: e.tensor_tensor(
                     wkb[mi % 2].ap().rearrange("p a c -> p (a c)"),
                     ckb[mi % 2].ap().rearrange("p a c -> p (a c)"),
                     skb[mi % 2].ap().rearrange("p a c -> p (a c)"), ALU.mult),
                 mark=("wbuild", mi), waits=wt)
            wbst["i"] += 1
    P.op(b, "vector", lambda e: e.memset(ones8c[:, :], 1.0))
    P.op(b, "vector", lambda e: e.memset(ones8r[:, :], 1.0), mark="ones8")
    emit_wd(2)
    emit_wb(1)

    groups = []
    for fc in range(16):
        for th in range(4):
            groups.append((len(groups), fc, th * 512))
    for fc in range(16):
        for th in range(4):
            groups.append((len(groups), 16 + fc, th * 512))
    for fc in range(16):
        for th in range(2):
            groups.append((len(groups), 32 + fc, th * 512))

    for n, mi, tok0 in groups:
        def mmg(e, n=n, mi=mi, tok0=tok0):
            ins = None
            for dc in range(16):
                ins = e.matmul(pk[:, n % 4, :], wkb[mi % 2][:, dc, :],
                               x_fm[:, dc, tok0:tok0 + 512],
                               start=(dc == 0), stop=(dc == 15))
            return ins
        wt = ["x_fm", ("wbuild", mi)]
        if n >= 4:
            wt.append(("pkev", n - 4))
        P.op(b, "tensor", mmg, mark=("pkg", n), waits=wt)
        j = n - 1
        if 0 <= j < 64:
            wtk = [("k2q", j), "ones8"] + ([("kmxr", j - 2)] if j >= 2 else [])
            P.op(b, "tensor",
                 lambda e, j=j: e.matmul(pkn[:, j % 2, :], ones8c[:, :],
                                         k2c[:, j % 2, :], start=True, stop=True),
                 mark=("pknm", j), waits=wtk)

    gplan = [("f", i) for i in range(64)] + [("p", i) for i in range(64)]

    a3f = arena3.bitcast(F32)

    def g_src(j):
        which, i = gplan[j]
        if which == "f":
            nn, mm_ = divmod(i, 4)
            return wfc.ap().rearrange("(n p) (m f) -> n m p f", p=128, f=2048)[nn, mm_]
        return wpr.ap().rearrange("(n p) f -> n p f", p=128)[i]

    def g_buf(j):
        if j < 40:
            t = j % 4
            return a2f[:, t * 2048:(t + 1) * 2048]
        if j < 64:
            t = (j - 40) % 3
            return a2f[:, t * 2048:(t + 1) * 2048]
        if j < 96:
            t = j % 4
            return a3f[:, t * 2048:(t + 1) * 2048]
        return a3f[:, (2 + j % 2) * 2048:(3 + j % 2) * 2048]

    def g_prev(j):
        if j >= 96:
            return j - 2
        if j >= 43:
            return j - 3 if j < 64 else j - 4
        return j - 4

    def emit_gdma(j, eng):
        wt = [("gtile", g_prev(j))] if g_prev(j) >= 0 else []
        fam = "gG" if eng == "gpsimd" else "gS"
        P.op(b, eng, lambda e, j=j: e.dma_start(g_buf(j), g_src(j)),
             sem=f"{fam}{j % 4}", k=16, mark=("gdma", j), waits=wt)

    def emit_gred_v(j):
        which, i = gplan[j]
        acc = gacc_f if which == "f" else gacc_p
        P.op(b, "vector",
             lambda e, j=j, acc=acc, i=i: e.reduce_sum(
                 acc[:, i:i + 1], g_buf(j), axis=AX.X,
                 apply_absolute_value=True),
             mark=("gtile", j), waits=[("gdma", j)])

    def emit_gred_s(j):
        which, i = gplan[j]
        acc = gacc_f if which == "f" else gacc_p
        P.op(b, "scalar",
             lambda e, j=j, acc=acc, i=i: e.activation(
                 g_buf(j), g_buf(j), AF.Abs, accum_out=acc[:, i:i + 1]),
             mark=("gtile", j), waits=[("gdma", j)])

    gfst = {"d": 0, "r": 0}

    def g_dma_upto(upto):
        while gfst["d"] < min(upto, 40):
            emit_gdma(gfst["d"], "sync")
            gfst["d"] += 1

    def g_red_upto(upto):
        while gfst["r"] < min(upto, 40):
            emit_gred_v(gfst["r"])
            gfst["r"] += 1
    KMX = 7424
    MKH = 7488
    MKB = 7504
    MH2 = 7168
    MHP = 7296
    NMH = 8064
    for n in range(64):
        fc, th = divmod(n, 4)
        wt = [("pkg", n)]
        if th == 0 and fc >= 1:
            wt.append(("kspill", fc - 1))
        wtq = [("pkg", n)] + ([("pknm", n - 2)] if n >= 2 else [])
        P.op(b, "scalar",
             lambda e, fc=fc, th=th, n=n: e.activation(
                 k2c[:, n % 2, :], pk[:, n % 4, :], AF.Square,
                 bias=bk_t[:, fc:fc + 1]),
             mark=("k2q", n), waits=wtq)
        P.op(b, "scalar",
             lambda e, fc=fc, th=th, n=n: e.activation(
                 kev[:, th, :], pk[:, n % 4, :], AF.Identity, bias=bk_t[:, fc:fc + 1]),
             mark=("pkev", n), waits=wt)
        emit_wb(fc + 1)
        P.op(b, "vector",
             lambda e, n=n: e.reduce_max(kmx8[0:8, 128 + n:129 + n],
                                         pkn[0:8, n % 2, :], axis=AX.X),
             mark=("kmxr", n), waits=[("pknm", n)])
        if th == 3:
            P.op(b, "vector",
                 lambda e, fc=fc: e.reduce_max(kmx8[0:8, 192 + fc:193 + fc],
                                               kmx8[0:8, 128 + 4 * fc:132 + 4 * fc],
                                               axis=AX.X),
                 mark=("mkh", fc))
        if th == 3:
            P.op(b, "sync",
                 lambda e, fc=fc: e.dma_start(
                     kspill[fc * 128:(fc + 1) * 128, :],
                     kev.rearrange("p a c -> p (a c)")),
                 sem="kSp", k=16, mark=("kspill", fc), waits=[("pkev", fc * 4 + 3)])
        emit_wd(n // 4 + 3)
        g_dma_upto(n // 2 + 1)
        g_red_upto(n // 2 - 2)
    for n in range(64, 128):
        fc, th = divmod(n - 64, 4)
        wt = [("pkg", n)]
        if th == 0 and fc >= 2:
            wt.append(("vtr", fc - 2))
        P.op(b, "scalar",
             lambda e, fc=fc, th=th, n=n: e.activation(
                 vrow[fc % 2][:, th, :], pk[:, n % 4, :], AF.Identity,
                 bias=bv_t[:, fc:fc + 1]),
             mark=("pkev", n), waits=wt)
        emit_wd((n - 64) // 4 + 19)
        emit_wb((n - 64) // 4 + 17)
        g_dma_upto(32 + (n - 64) // 2)
        g_red_upto(30 + (n - 64) // 2)
        if th == 3:
            wt2 = [("pkev", n)] + ([("vsp", fc - 2)] if fc >= 2 else [])
            P.op(b, "sync",
                 lambda e, fc=fc: e.dma_start(
                     vtrb[fc % 2].ap(),
                     vrow[fc % 2].ap().rearrange("p a c -> p (a c)")
                     .rearrange("p (tc i) -> p tc i", tc=16),
                     transpose=True),
                 sem=f"vT{fc % 2}", k=16, mark=("vtr", fc), waits=wt2)
            P.op(b, "sync",
                 lambda e, fc=fc: e.dma_start(
                     vspill[fc].rearrange("(tc p) i -> p tc i", p=128),
                     vtrb[fc % 2].ap()),
                 sem=f"vP{fc % 2}", k=16, mark=("vsp", fc), waits=[("vtr", fc)])
    for n in range(128, 160):
        fc, th = divmod(n - 128, 2)
        P.op(b, "scalar",
             lambda e, fc=fc, th=th, n=n: e.activation(
                 Qt[:, fc, th * 512:(th + 1) * 512], pk[:, n % 4, :],
                 AF.Identity, scale=ISQ, bias=bq_s[:, fc:fc + 1]),
             mark=("pkev", n), waits=[("pkg", n)])
        emit_wd((n - 128) // 2 + 35)
        emit_wb((n - 128) // 2 + 33)
        g_red_upto(34 + (n - 128) // 2)
    emit_wd(47)
    emit_wb(47)
    g_red_upto(40)


    s2 = ExitStack()
    pB2 = psum("pB2", [128, 512], s2)
    pqn = psum("pqn", [128, 128], s2)
    b = P.block()
    P.op(b, "vector", lambda e: e.tensor_copy(mk2b8[:, :], kmx8[0:8, 192:208]),
         mark="mk2b")
    P.op(b, "tensor",
         lambda e: e.matmul(pB2[:, 256:272], ones8r[:, :], mk2b8[:, :],
                            start=True, stop=True),
         mark="mkbc", waits=["mk2b"])
    P.op(b, "scalar", lambda e: e.copy(a2f[:, MKB:MKB + 16], pB2[:, 256:272]),
         mark="mkev", waits=["mkbc"])
    for j2 in range(128):
        h_, qb_ = divmod(j2, 8)
        P.op(b, "vector",
             lambda e, j2=j2, h_=h_, qb_=qb_: e.tensor_tensor(
                 q2c[:, j2 % 8, :], Qt[:, h_, qb_ * 128:(qb_ + 1) * 128],
                 Qt[:, h_, qb_ * 128:(qb_ + 1) * 128], ALU.mult),
             mark=("q2c", j2), waits=[("qnm", j2 - 8)] if j2 >= 8 else [])
        P.op(b, "tensor",
             lambda e, j2=j2: e.matmul(pqn[:, j2:j2 + 1], q2c[:, j2 % 8, :],
                                       ones_bf[:, :], start=True, stop=True),
             mark=("qnm", j2), waits=[("q2c", j2)])
        if j2 % 5 == 0 and j2 // 5 < 24:
            emit_gdma(40 + j2 // 5, "gpsimd")
            if j2 // 5 >= 2:
                emit_gred_s(38 + j2 // 5)
    for h_ in range(16):
        P.op(b, "vector",
             lambda e, h_=h_: e.tensor_scalar(
                 a2f[:, MH2 + h_ * 8:MH2 + h_ * 8 + 8], pqn[:, h_ * 8:h_ * 8 + 8],
                 a2f[:, MKB + h_:MKB + h_ + 1], None, ALU.mult),
             mark=("mh2", h_), waits=[("qnm", h_ * 8 + 7)] + (["mkev"] if h_ == 0 else []))
    P.op(b, "scalar",
         lambda e: e.activation(a2f[:, MHP:MHP + 128], a2f[:, MH2:MH2 + 128], AF.Sqrt),
         mark="mhs", waits=[("mh2", 15)])
    P.op(b, "vector",
         lambda e: e.tensor_scalar(a2f[:, NMH:NMH + 128], a2f[:, MHP:MHP + 128],
                                   -1.0, None, ALU.mult),
         mark="nmh", waits=["mhs"])
    for j in range(62, 64):
        emit_gred_s(j)

    s2.close()
    s1.close()

    s3 = ExitStack()
    kslb = [sb(f"kslb{i}", [128, S], BF16, s3) for i in range(2)]
    vslb = [sb(f"vslb{i}", [128, 16, 128], BF16, s3) for i in range(2)]
    Pt = [sb(f"Pt{i}", [128, S], BF16, s3) for i in range(2)]
    sums2 = rminf
    invT = rminf
    Sinv = sb("Sinv", [128, 8], F32, s3)
    invb = [sb(f"invb{i}", [128, Q], F32, s3) for i in range(2)]
    pscore = psum("pscore", [128, 2048], s3)
    pav = psum("pav", [128, 2, 512], s3)
    pinvb = psum("pinvb", [128, Q], s3)

    b = P.block()

    def emit_ptr(h, qb):
        qh = 0 if qb < 4 else 1
        wt2 = []
        if h >= 2 and qb % 4 == 0:
            wt2.append(("avmm", h - 2, qh))

        def ptrf(e, h=h, qb=qb, qh=qh):
            return e.dma_start(
                PT[(h % 2) * 2 + qh][:, :, (qb % 4) * 128:(qb % 4 + 1) * 128],
                Pt[qb % 2].ap().rearrange("p (kc i) -> p kc i", kc=16),
                transpose=True)
        P.op(b, "sync", ptrf, sem=f"tP{qb % 2}", k=16, mark=("ptr", h, qb),
             waits=[("exB", h, qb)] + wt2)

    for h in range(H):
        if h >= 1:
            for qb in range(4):
                emit_ptr(h - 1, qb)
        P.op(b, "sync",
             lambda e, h=h: e.dma_start(kslb[h % 2][:, :],
                                        kspill[h * 128:(h + 1) * 128, :]),
             sem=f"kR{h % 2}", k=16, mark=("ksl", h),
             waits=[("smB", h - 2, 7)] if h >= 2 else [])
        if h >= 1:
            for qb in range(4, 8):
                emit_ptr(h - 1, qb)
        P.op(b, "sync",
             lambda e, h=h: e.dma_start(
                 vslb[h % 2].ap(),
                 vspill[h].rearrange("(tc p) i -> p tc i", p=128)),
             sem=f"vR{h % 2}", k=16, mark=("vsl", h),
             waits=[("avmm", h - 2, 1)] if h >= 2 else [])
    for qb in range(8):
        emit_ptr(H - 1, qb)

    def emit_inv_tr(h):
        wt = [("recip", h)] + ([("invev", h - 1)] if h >= 1 else ["id128"])
        P.op(b, "tensor",
             lambda e: e.transpose(pinvb[0:8, 0:128], Sinv[:, :], id128[:, :]),
             mark=("invtr", h), waits=wt)

    def emit_inv_bc(h):
        def f(e):
            ins = None
            for qb in range(8):
                ins = e.matmul(pinvb[:, qb * 128:(qb + 1) * 128],
                               sel8[:, qb * 128:(qb + 1) * 128], invT[0:8, 208:336],
                               start=True, stop=True)
            return ins
        P.op(b, "tensor", f, mark=("invbc", h), waits=[("invTe", h)])

    def emit_av(hp, qh, kc0, kc1, markkey=None):
        def f(e, hp=hp, qh=qh, kc0=kc0, kc1=kc1):
            ins = None
            for kc in range(kc0, kc1):
                ins = e.matmul(pav[:, qh, :], vslb[hp % 2][:, kc, :],
                               PT[(hp % 2) * 2 + qh][:, kc, :],
                               start=(kc == 0), stop=(kc == 15))
            return ins
        wt = []
        if kc0 == 0:
            wt = [("ptr", hp, qh * 4 + 2), ("ptr", hp, qh * 4 + 3), ("vsl", hp)]
            if hp >= 1:
                wt.append(("avev", hp - 1, qh))
        P.op(b, "tensor", f, mark=markkey, waits=wt)

    for h in range(H):
        for qb in range(8):
            pq = (h, qb - 1) if qb > 0 else (h - 1, 7)

            def smmh(e, h=h, qb=qb, j0=0):
                ins = None
                for j in range(j0, j0 + 2):
                    ins = e.matmul(pscore[:, j * 512:(j + 1) * 512],
                                   Qt[:, h, qb * 128:(qb + 1) * 128],
                                   kslb[h % 2][:, j * 512:(j + 1) * 512],
                                   start=True, stop=True)
                return ins
            wtA = [("ksl", h)]
            wtA.append(("exA",) + pq if (h, qb) != (0, 0) else ("pkev", 159))
            P.op(b, "tensor", lambda e, f=smmh: f(e, j0=0), mark=("smA", h, qb),
                 waits=wtA)
            if h >= 1:
                hp, qh = h - 1, (0 if qb < 4 else 1)
                kc0 = (qb % 4) * 4
                emit_av(hp, qh, kc0, kc0 + 4,
                        markkey=("avmm", hp, qh) if qb % 4 == 3 else None)
            wtB = [("exB",) + pq] if (h, qb) != (0, 0) else []
            P.op(b, "tensor", lambda e, f=smmh: f(e, j0=2), mark=("smB", h, qb),
                 waits=wtB)
            if h >= 1 and qb == 0:
                emit_inv_tr(h - 1)
            if h >= 1 and qb == 1:
                emit_inv_bc(h - 1)
    hl = H - 1
    emit_inv_tr(hl)
    emit_inv_bc(hl)
    for qh in range(2):
        emit_av(hl, qh, 0, 16, markkey=("avmm", hl, qh))

    for h in range(H):
        if h >= 1:
            P.op(b, "vector",
                 lambda e, h=h: e.tensor_copy(invb[(h - 1) % 2][:, :], pinvb[:, :]),
                 mark=("invev", h - 1), waits=[("invbc", h - 1)])
            for qh in range(2):
                P.op(b, "vector",
                     lambda e, h=h, qh=qh: e.tensor_tensor(
                         attn[:, h - 1, qh * 512:(qh + 1) * 512], pav[:, qh, :],
                         invb[(h - 1) % 2][:, qh * 512:(qh + 1) * 512], ALU.mult),
                     mark=("avev", h - 1, qh),
                     waits=[("avmm", h - 1, qh), ("invev", h - 1)])
        P.op(b, "vector",
             lambda e, h=h: e.tensor_tensor(sums[:, h * 8:(h + 1) * 8],
                                            sums[:, h * 8:(h + 1) * 8],
                                            sums2[:, h * 8:(h + 1) * 8], ALU.add),
             mark=("scomb", h), waits=[("exB", h, 7)])
        P.op(b, "vector",
             lambda e, h=h: e.reciprocal(Sinv[:, :], sums[:, h * 8:(h + 1) * 8]),
             mark=("recip", h), waits=[("scomb", h)])

    P.op(b, "vector",
         lambda e: e.tensor_copy(invb[hl % 2][:, :], pinvb[:, :]),
         mark=("invev", hl), waits=[("invbc", hl)])
    for qh in range(2):
        P.op(b, "vector",
             lambda e, qh=qh: e.tensor_tensor(
                 attn[:, hl, qh * 512:(qh + 1) * 512], pav[:, qh, :],
                 invb[hl % 2][:, qh * 512:(qh + 1) * 512], ALU.mult),
             mark=("avev", hl, qh),
             waits=[("avmm", hl, qh), ("invev", hl)])

    for h in range(H):
        for qb in range(8):
            wt = [("smA", h, qb)]
            if h * 8 + qb >= 2:
                pq = (h, qb - 2) if qb >= 2 else (h - 1, 6 + qb)
                wt.append(("ptr",) + pq)
            j2 = h * 8 + qb
            P.op(b, "scalar",
                 lambda e, h=h, qb=qb, j2=j2: e.activation(
                     Pt[qb % 2][:, 0:1024], pscore[:, 0:1024], AF.Exp,
                     bias=a2f[:, NMH + j2:NMH + j2 + 1],
                     accum_out=sums[:, j2:j2 + 1]),
                 mark=("exA", h, qb), waits=wt)
            P.op(b, "scalar",
                 lambda e, h=h, qb=qb, j2=j2: e.activation(
                     Pt[qb % 2][:, 1024:2048], pscore[:, 1024:2048], AF.Exp,
                     bias=a2f[:, NMH + j2:NMH + j2 + 1],
                     accum_out=sums2[:, j2:j2 + 1]),
                 mark=("exB", h, qb), waits=[("smB", h, qb)])
        P.op(b, "scalar", lambda e: e.copy(invT[0:8, 208:336], pinvb[0:8, 0:128]),
             mark=("invTe", h), waits=[("invtr", h)])

    s3.close()

    s4 = ExitStack()
    cob = [sb(f"cob{i}", [128, 16, 128], BF16, s4) for i in range(2)]
    sob = [sb(f"sob{i}", [128, 16, 128], BF16, s4) for i in range(2)]
    wow = [sb(f"wow{i}", [128, 16, 128], BF16, s4) for i in range(2)]
    xqt = [sb(f"xqt{i}", [128, Q], F32, s4) for i in range(2)]
    t1t = [sb(f"t1t{i}", [128, Q], F32, s4) for i in range(2)]
    ybf = [sb(f"ybf{i}", [128, Q], BF16, s4) for i in range(2)]
    yb2 = sb("yb2", [128, Q], BF16, s4)
    poc = psum("poc", [128, 2, 2, 512], s4)
    pst1 = psum("pst1", [128, 512], s4)
    pst2 = psum("pst2", [128, 512], s4)

    b = P.block()
    for doc in range(16):
        P.op(b, "gpsimd",
             lambda e, doc=doc: e.dma_start(
                 cob[doc % 2].ap(),
                 wo[:, doc * 128:(doc + 1) * 128].rearrange("(fc p) f -> p fc f", p=128)),
             sem=f"oW{doc % 2}", k=16, waits=[("wob", doc - 2)] if doc >= 2 else [])
        P.op(b, "gpsimd",
             lambda e, doc=doc: e.dma_start(
                 sob[doc % 2].ap(),
                 so[:, doc * 128:(doc + 1) * 128].rearrange("(fc p) f -> p fc f", p=128)),
             sem=f"oW{doc % 2}", k=16, mark=("wod", doc))
        P.op(b, "sync",
             lambda e, doc=doc: e.dma_start(xqt[doc % 2][:, :],
                                            xq[doc * 128:(doc + 1) * 128, :]),
             sem=f"xQ{doc % 2}", k=16, mark=("xqd", doc),
             waits=[("yadd", doc - 2)] if doc >= 2 else [])
        emit_gdma(64 + 2 * doc, "sync")
        emit_gdma(65 + 2 * doc, "gpsimd")
    def emit_wob(doc):
        P.op(b, "vector",
             lambda e, doc=doc: e.tensor_tensor(
                 wow[doc % 2].ap().rearrange("p a c -> p (a c)"),
                 cob[doc % 2].ap().rearrange("p a c -> p (a c)"),
                 sob[doc % 2].ap().rearrange("p a c -> p (a c)"), ALU.mult),
             mark=("wob", doc),
             waits=[("wod", doc)] + ([("omm", doc - 2)] if doc >= 2 else []))

    emit_wob(0)
    emit_wob(1)
    for doc in range(16):
        def ommf(e, doc=doc):
            ins = None
            for qh in range(2):
                for fc in range(16):
                    ins = e.matmul(poc[:, doc % 2, qh, :], wow[doc % 2][:, fc, :],
                                   attn[:, fc, qh * 512:(qh + 1) * 512],
                                   start=(fc == 0), stop=(fc == 15))
            return ins
        P.op(b, "tensor", ommf, mark=("omm", doc),
             waits=[("wob", doc)] + ([("oev", doc - 2)] if doc >= 2 else []))
        if doc >= 1:
            dp = doc - 1

            def stf(e, dp=dp):
                ins = None
                for qb in range(8):
                    ins = e.matmul(pst1[:, qb:qb + 1],
                                   ybf[dp % 2][:, qb * 128:(qb + 1) * 128],
                                   ones_bf[:, :], start=(dp == 0 and qb == 0), stop=False)
                    ins = e.matmul(pst2[:, qb:qb + 1],
                                   yb2[:, qb * 128:(qb + 1) * 128],
                                   ones_bf[:, :], start=(dp == 0 and qb == 0), stop=False)
                return ins
            P.op(b, "tensor", stf, mark=("stmm", dp), waits=[("yb2", dp)])

    def stfl(e):
        ins = None
        for qb in range(8):
            ins = e.matmul(pst1[:, qb:qb + 1], ybf[15 % 2][:, qb * 128:(qb + 1) * 128],
                           ones_bf[:, :], start=False, stop=(qb == 7))
            ins = e.matmul(pst2[:, qb:qb + 1], yb2[:, qb * 128:(qb + 1) * 128],
                           ones_bf[:, :], start=False, stop=(qb == 7))
        return ins
    P.op(b, "tensor", stfl, mark=("stmm", 15), waits=[("yb2", 15)])

    for doc in range(16):
        P.op(b, "scalar",
             lambda e, doc=doc: e.activation(
                 t1t[doc % 2][:, :],
                 poc.ap()[:, doc % 2, :, :].rearrange("p a c -> p (a c)"),
                 AF.Identity, bias=bo_t[:, doc:doc + 1]),
             mark=("oev", doc),
             waits=[("omm", doc)] + ([("yadd", doc - 2)] if doc >= 2 else []))
        if doc >= 1:
            emit_gred_s(62 + 2 * doc)
            emit_gred_s(63 + 2 * doc)
        if doc == 15:
            emit_gred_s(94)
            emit_gred_s(95)
    for doc in range(16):
        P.op(b, "vector",
             lambda e, doc=doc: e.tensor_tensor(y[:, doc, :], t1t[doc % 2][:, :],
                                                xqt[doc % 2][:, :], ALU.add),
             mark=("yadd", doc), waits=[("oev", doc), ("xqd", doc)])
        P.op(b, "vector",
             lambda e, doc=doc: e.tensor_copy(ybf[doc % 2][:, :], y[:, doc, :]),
             mark=("ybf", doc),
             waits=[("stmm", doc - 2)] if doc >= 2 else [])
        P.op(b, "vector",
             lambda e, doc=doc: e.tensor_tensor(yb2[:, :], ybf[doc % 2][:, :],
                                                ybf[doc % 2][:, :], ALU.mult),
             mark=("yb2", doc),
             waits=[("stmm", doc - 1)] if doc >= 1 else [])
        if doc == 0:
            P.op(b, "vector", lambda e: e.tensor_copy(rmax[:, :], ybf[0][:, :]))
            P.op(b, "vector", lambda e: e.tensor_copy(rmin[:, :], ybf[0][:, :]),
                 mark=("tree", 0))
        else:
            P.op(b, "vector",
                 lambda e, doc=doc: e.tensor_tensor(rmax[:, :], rmax[:, :],
                                                    ybf[doc % 2][:, :], ALU.max))
            P.op(b, "vector",
                 lambda e, doc=doc: e.tensor_tensor(rmin[:, :], rmin[:, :],
                                                    ybf[doc % 2][:, :], ALU.min),
                 mark=("tree", doc))
        if doc + 2 < 16:
            emit_wob(doc + 2)

    P.op(b, "scalar", lambda e: e.copy(stt["sums1"][:, :], pst1[:, 0:8]),
         waits=[("stmm", 15)])
    P.op(b, "scalar", lambda e: e.copy(stt["sumsq1"][:, :], pst2[:, 0:8]),
         mark="ln1ev")

    s4.close()

    s5 = ExitStack()
    txn = [sb(f"txn{i}", [128, Q], F32, s5) for i in range(2)]
    ptr_ = psum("ptr_", [128, 1024], s5)
    pbc = psum("pbc", [128, 1024], s5)

    b = P.block()

    def ln_chain(nfeat, mu_t, R_t, composed, pre, wait0):
        P.op(b, "vector", lambda e, mu_t=mu_t: e.tensor_scalar(
            mu_t[:, :], stt["sums1"][:, :], 1.0 / nfeat, None, ALU.mult),
            waits=[wait0])
        P.op(b, "vector", lambda e: e.tensor_scalar(
            stt["tmp1"][:, :], stt["sumsq1"][:, :], 1.0 / nfeat, None, ALU.mult))
        P.op(b, "vector", lambda e, mu_t=mu_t: e.tensor_tensor(
            stt["tmp2"][:, :], mu_t[:, :], mu_t[:, :], ALU.mult))
        P.op(b, "vector", lambda e: e.tensor_tensor(
            stt["var"][:, :], stt["tmp1"][:, :], stt["tmp2"][:, :], ALU.subtract))
        if composed:
            P.op(b, "vector", lambda e: e.tensor_scalar(
                stt["tmp1"][:, :], stt["var"][:, :], 1.0 + EPS2, EPS1 * EPS2,
                ALU.mult, ALU.add), mark=pre + "a")
        else:
            eps = EPS2 if nfeat == FF else EPS1
            P.op(b, "vector", lambda e, eps=eps: e.tensor_scalar(
                stt["tmp1"][:, :], stt["var"][:, :], eps, None, ALU.add),
                mark=pre + "a")
        P.op(b, "scalar", lambda e: e.activation(stt["tmp2"][:, :], stt["tmp1"][:, :],
                                                 AF.Sqrt),
             mark=pre + "sq", waits=[pre + "a"])
        P.op(b, "vector", lambda e, R_t=R_t: e.reciprocal(R_t[:, :], stt["tmp2"][:, :]),
             mark=pre + "R", waits=[pre + "sq"])

    for doc in range(16):
        emit_gdma(96 + 2 * doc, "sync")
        emit_gdma(97 + 2 * doc, "gpsimd")
    ln_chain(D, stt["mu"], stt["R"], True, "ln1", "ln1ev")
    P.op(b, "vector", lambda e: e.tensor_scalar(stt["tmp3"][:, :], stt["var"][:, :],
                                                EPS1, None, ALU.add),
         mark="ln1add", waits=["ln1R"])
    P.op(b, "scalar", lambda e: e.activation(stt["tmp3"][:, :], stt["tmp3"][:, :],
                                             AF.Sqrt), mark="ln1sA", waits=["ln1add"])
    P.op(b, "vector", lambda e: e.reciprocal(stt["tmp1"][:, :], stt["tmp3"][:, :]),
         waits=["ln1sA"])
    P.op(b, "vector", lambda e: e.tensor_tensor(stt["rho"][:, :], stt["tmp2"][:, :],
                                                stt["tmp1"][:, :], ALU.mult),
         mark="rho")

    def trred(src, dst_t, op, mk, wt, pp):
        ppb = pp.bitcast(BF16)
        def trf(e, src=src, ppb=ppb):
            ins = None
            for j in range(8):
                ins = e.transpose(ppb[:, j * 128:(j + 1) * 128],
                                  src[:, j * 128:(j + 1) * 128], id128b[:, :])
            return ins
        P.op(b, "tensor", trf, mark=mk + "t", waits=wt)

        def rf(e, dst_t=dst_t, op=op, ppb=ppb):
            ins = None
            for j in range(8):
                ins = e.tensor_reduce(dst_t[:, j:j + 1], ppb[:, j * 128:(j + 1) * 128],
                                      AX.X, op)
            return ins
        P.op(b, "vector", rf, mark=mk + "r", waits=[mk + "t"])

    trred(rmax, stt["mx"], ALU.max, "mx1", [("tree", 15)], ptr_)
    trred(rmin, stt["mn"], ALU.min, "mn1", ["mx1r"], ptr_)

    P.op(b, "vector", lambda e: e.tensor_tensor(stt["tmp1"][:, :], stt["mx"][:, :],
                                                stt["mu"][:, :], ALU.subtract),
         waits=["mn1r", "rho"])
    P.op(b, "vector", lambda e: e.tensor_tensor(stt["tmp2"][:, :], stt["mu"][:, :],
                                                stt["mn"][:, :], ALU.subtract))
    P.op(b, "vector", lambda e: e.tensor_tensor(stt["tmp1"][:, :], stt["tmp1"][:, :],
                                                stt["tmp2"][:, :], ALU.max))
    P.op(b, "vector", lambda e: e.tensor_tensor(stt["gx"][:, :], stt["tmp1"][:, :],
                                                stt["R"][:, :], ALU.mult))
    P.op(b, "vector", lambda e: e.tensor_scalar(stt["cx"][:, :], stt["gx"][:, :],
                                                CLIPB, None, ALU.mult))
    P.op(b, "vector", lambda e: e.tensor_scalar(stt["ncx"][:, :], stt["gx"][:, :],
                                                -CLIPB, None, ALU.mult), mark="cx1")

    def bcast(stat, dstB, key, wkey, pp):
        P.op(b, "tensor", lambda e, stat=stat: e.transpose(pp[0:8, 0:128],
                                                           stat[:, :], id128[:, :]),
             mark=key + "t", waits=[wkey])
        P.op(b, "scalar", lambda e: e.copy(trT[:, :], pp[0:8, 0:128]),
             mark=key + "e", waits=[key + "t"])

        def bmmf(e):
            ins = None
            for qb in range(8):
                ins = e.matmul(pp[:, qb * 128:(qb + 1) * 128],
                               sel8[:, qb * 128:(qb + 1) * 128], trT[:, :],
                               start=True, stop=True)
            return ins
        P.op(b, "tensor", bmmf, mark=key + "m", waits=[key + "e"])
        P.op(b, "vector", lambda e, dstB=dstB: e.tensor_copy(dstB[:, :], pp[:, :]),
             mark=key + "b", waits=[key + "m"])

    bcast(stt["mu"], muB, "bmu", "cx1", pbc)
    bcast(stt["R"], RB, "bR", "bmub", pbc)
    bcast(stt["cx"], cB, "bcx", "bRb", pbc)
    bcast(stt["ncx"], ncB, "bncx", "bcxb", pbc)
    bcast(stt["rho"], rhoB, "brho", "bncxb", pbc)

    for doc in range(16):
        P.op(b, "vector",
             lambda e, doc=doc: e.tensor_tensor(txn[doc % 2][:, :], y[:, doc, :],
                                                muB[:, :], ALU.subtract),
             waits=(["brhob"] if doc == 0 else []))
        P.op(b, "vector",
             lambda e, doc=doc: e.tensor_tensor(y[:, doc, :], txn[doc % 2][:, :],
                                                RB[:, :], ALU.mult),
             mark=("xn", doc))
        P.op(b, "vector",
             lambda e, doc=doc: e.tensor_tensor(txn[doc % 2][:, :], y[:, doc, :],
                                                cB[:, :], ALU.min))
        xw = [] if doc < 8 else [("gtile", 126 if doc < 12 else 127)]
        P.op(b, "vector",
             lambda e, doc=doc: e.tensor_tensor(xh[:, doc, :], txn[doc % 2][:, :],
                                                ncB[:, :], ALU.max),
             mark=("xhat", doc), waits=xw)
        P.op(b, "sync",
             lambda e, doc=doc: e.dma_start(xnspill[doc * 128:(doc + 1) * 128, :],
                                            y[:, doc, :]),
             sem="dX", k=16, mark=("xnsp", doc), waits=[("xn", doc)])
        if doc < 8:
            emit_gred_v(112 + 2 * doc)
            emit_gred_v(113 + 2 * doc)
    for j in range(96, 112):
        emit_gred_s(j)

    s5.close()

    s6 = ExitStack()
    wfw = [arena1[:, i * 2048:(i + 1) * 2048].rearrange("p (a c) -> p a c", a=16)
           for i in range(4)]
    wps = [arena1[:, 8192 + i * 2048:8192 + (i + 1) * 2048] for i in range(4)]
    pf = sb("pf", [128, 2048], BF16, s6)
    nf = sb("nf", [128, 2048], BF16, s6)
    tf = [sb(f"tf{i}", [128, 16, 128], BF16, s6) for i in range(2)]
    pa = sb("pa", [128, 2048], BF16, s6)
    pb_ = sb("pb_", [128, 2048], BF16, s6)
    tp8 = [sb(f"tp8{i}", [128, 2048], FP8, s6) for i in range(2)]
    hsl = [sb(f"hsl{i}", [128, Q], BF16, s6) for i in range(2)]
    h2t = sb("h2t", [128, Q], BF16, s6)
    ph = psum("ph", [128, 2, 1024], s6)
    psh1 = psum("psh1", [128, 512], s6)
    psh2 = psum("psh2", [128, 512], s6)
    pg6 = psum("pg6", [128, 8], s6)

    b = P.block()
    P.op(b, "vector", lambda e: e.reduce_sum(gacc_f[:, 0:1], gacc_f[:, :], axis=AX.X),
         waits=[("gtile", 127)])
    P.op(b, "vector", lambda e: e.reduce_sum(gacc_p[:, 0:1], gacc_p[:, :], axis=AX.X),
         mark="gred")
    P.op(b, "tensor", lambda e: e.matmul(pg6[0:1, 0:1], gacc_f[:, 0:1], ones_f[:, :],
                                         start=True, stop=False), waits=["gred"])
    P.op(b, "tensor", lambda e: e.matmul(pg6[0:1, 1:2], gacc_p[:, 0:1], ones_f[:, :],
                                         start=False, stop=True), mark="gsum")
    P.op(b, "scalar", lambda e: e.copy(g1[0:1, 0:2], pg6[0:1, 0:2]),
         mark="g1ev", waits=["gsum"])
    P.op(b, "vector", lambda e: e.tensor_scalar(g1[0:1, 2:4], g1[0:1, 0:2],
                                                1.0 / NW, None, ALU.mult),
         waits=["g1ev"])
    P.op(b, "vector", lambda e: e.tensor_scalar(g1[0:1, 4:6], g1[0:1, 2:4],
                                                1e-6, 0.5, ALU.add, ALU.mult))
    P.op(b, "vector", lambda e: e.tensor_scalar(g1[0:1, 6:8], g1[0:1, 4:6],
                                                -1.0, None, ALU.mult))
    P.op(b, "vector", lambda e: e.tensor_scalar(g1[0:1, 8:10], g1[0:1, 2:4],
                                                0.5, None, ALU.mult), mark="gscal")
    gcols = [(4, thf), (5, thp), (6, nthf), (7, nthp), (2, gfb), (9, gpb)]
    for i, (col, dst) in enumerate(gcols):
        P.op(b, "tensor",
             lambda e, i=i, col=col: e.matmul(pg6[:, 2 + i:3 + i], onesr_f[:, :],
                                              g1[0:1, col:col + 1], start=True, stop=True),
             mark=("gbc", i), waits=(["gscal"] if i == 0 else []))
    for i, (col, dst) in enumerate(gcols):
        P.op(b, "scalar", lambda e, i=i, dst=dst: e.copy(dst[:, :], pg6[:, 2 + i:3 + i]),
             mark=("gbcev", i), waits=([("gbc", 5)] if i == 0 else []))

    def emit_wfd(oc):
        P.op(b, "sync",
             lambda e, oc=oc: e.dma_start(
                 wfw[oc % 4],
                 wfc[:, oc * 128:(oc + 1) * 128].rearrange("(dc p) f -> p dc f", p=128)),
             sem=f"fW{oc % 4}", k=16, mark=("wfd", oc),
             waits=([("nfc", oc - 4)] if oc >= 4 else []))

    def emit_trio(oc):
        P.op(b, "vector",
             lambda e, oc=oc: e.tensor_scalar(
                 pf[:, :], wfw[oc % 4].rearrange("p a c -> p (a c)"),
                 thf[:, :], None, ALU.is_gt),
             mark=("pfc", oc),
             waits=[("wfd", oc)] + ([("fmm", oc - 2)] if oc >= 2 else [("gbcev", 5)]))
        P.op(b, "vector",
             lambda e, oc=oc: e.tensor_scalar(
                 nf[:, :], wfw[oc % 4].rearrange("p a c -> p (a c)"),
                 nthf[:, :], None, ALU.is_lt),
             mark=("nfc", oc))
        P.op(b, "vector",
             lambda e, oc=oc: e.tensor_tensor(
                 tf[oc % 2].ap().rearrange("p a c -> p (a c)"),
                 pf[:, :], nf[:, :], ALU.subtract),
             mark=("tfc", oc))

    def emit_slow(j):
        P.op(b, "vector",
             lambda e, j=j: e.tensor_tensor(tp8[j % 2][:, :], pa[:, :], pb_[:, :],
                                            ALU.add),
             mark=("tpc", j),
             waits=[("sgb", j)] + ([("tpw", j - 2)] if j >= 2 else []))
        if j >= 1:
            P.op(b, "vector",
                 lambda e, j=j: e.tensor_tensor(h2t[:, :], hsl[(j - 1) % 2][:, :],
                                                hsl[(j - 1) % 2][:, :], ALU.mult),
                 mark=("h2", j - 1),
                 waits=[("hev", j - 1)] + ([("hst", j - 2)] if j >= 2 else []))
        if j == 0:
            P.op(b, "vector", lambda e: e.tensor_copy(rmax[:, :], hsl[0][:, :]),
                 waits=[("hev", 0)])
            P.op(b, "vector", lambda e: e.tensor_copy(rmin[:, :], hsl[0][:, :]),
                 mark=("htree", 0))
        else:
            P.op(b, "vector",
                 lambda e, j=j: e.tensor_tensor(rmax[:, :], rmax[:, :],
                                                hsl[j % 2][:, :], ALU.max),
                 waits=[("hev", j)])
            P.op(b, "vector",
                 lambda e, j=j: e.tensor_tensor(rmin[:, :], rmin[:, :],
                                                hsl[j % 2][:, :], ALU.min),
                 mark=("htree", j))

    emit_wfd(0)
    emit_wfd(1)
    for oc in range(64):
        if oc + 2 < 64:
            emit_wfd(oc + 2)
        P.op(b, "gpsimd",
             lambda e, oc=oc: e.dma_start(wps[oc % 4][:, :],
                                          wpr[oc * 128:(oc + 1) * 128, :]),
             sem=f"pW{oc % 4}", k=16, mark=("wpd", oc),
             waits=[("sgb", oc - 4)] if oc >= 4 else [])
        emit_trio(oc)
        if oc >= 1:
            emit_slow(oc - 1)
        P.op(b, "scalar",
             lambda e, oc=oc: e.activation(pa[:, :], wps[oc % 4][:, :], AF.Sign,
                                           bias=nthp[:, :]),
             waits=[("wpd", oc)] + ([("tpc", oc - 1)] if oc >= 1 else [("gbcev", 5)]))
        P.op(b, "scalar",
             lambda e, oc=oc: e.activation(pb_[:, :], wps[oc % 4][:, :], AF.Sign,
                                           bias=thp[:, :]),
             mark=("sgb", oc))

        def fmmf(e, oc=oc):
            ins = None
            for qh in range(2):
                for dc in range(16):
                    ins = e.matmul(ph[:, oc % 2, qh * 512:(qh + 1) * 512],
                                   tf[oc % 2][:, dc, :],
                                   xh[:, dc, qh * 512:(qh + 1) * 512],
                                   start=(dc == 0), stop=(dc == 15))
            return ins
        P.op(b, "tensor", fmmf, mark=("fmm", oc),
             waits=[("tfc", oc)] + ([("hev", oc - 2)] if oc >= 2 else []))
        if oc >= 1:
            def hstf(e, op=oc - 1):
                ins = None
                for qb in range(8):
                    ins = e.matmul(psh1[:, qb:qb + 1],
                                   hsl[op % 2][:, qb * 128:(qb + 1) * 128],
                                   ones_bf[:, :], start=(op == 0 and qb == 0), stop=False)
                    ins = e.matmul(psh2[:, qb:qb + 1],
                                   h2t[:, qb * 128:(qb + 1) * 128],
                                   ones_bf[:, :], start=(op == 0 and qb == 0), stop=False)
                return ins
            P.op(b, "tensor", hstf, mark=("hst", oc - 1), waits=[("h2", oc - 1)])
        P.op(b, "scalar",
             lambda e, oc=oc: e.activation(hsl[oc % 2][:, :],
                                           ph.ap()[:, oc % 2, :], AF.Gelu,
                                           scale=gfb[:, :]),
             mark=("hev", oc),
             waits=[("fmm", oc)] + ([("hsp", oc - 2), ("hst", oc - 2),
                                     ("htree", oc - 2)] if oc >= 2 else []))
        P.op(b, "sync",
             lambda e, oc=oc: e.dma_start(hspill[oc * 128:(oc + 1) * 128, :],
                                          hsl[oc % 2][:, :]),
             sem=f"hW{oc % 2}", k=16, mark=("hsp", oc), waits=[("hev", oc)])
        if oc >= 1:
            P.op(b, "sync",
                 lambda e, oc=oc: e.dma_start(tproj[(oc - 1) * 128:oc * 128, :],
                                              tp8[(oc - 1) % 2][:, :]),
                 sem=f"tW{(oc - 1) % 2}", k=16, mark=("tpw", oc - 1),
                 waits=[("tpc", oc - 1)])
    emit_slow(63)
    P.op(b, "sync",
         lambda e: e.dma_start(tproj[63 * 128:64 * 128, :], tp8[63 % 2][:, :]),
         sem=f"tW{63 % 2}", k=16, mark=("tpw", 63), waits=[("tpc", 63)])
    P.op(b, "vector",
         lambda e: e.tensor_tensor(h2t[:, :], hsl[63 % 2][:, :], hsl[63 % 2][:, :],
                                   ALU.mult),
         mark=("h2", 63), waits=[("hev", 63), ("hst", 62)])

    def hstl(e):
        ins = None
        for qb in range(8):
            ins = e.matmul(psh1[:, qb:qb + 1], hsl[63 % 2][:, qb * 128:(qb + 1) * 128],
                           ones_bf[:, :], start=False, stop=(qb == 7))
            ins = e.matmul(psh2[:, qb:qb + 1], h2t[:, qb * 128:(qb + 1) * 128],
                           ones_bf[:, :], start=False, stop=(qb == 7))
        return ins
    P.op(b, "tensor", hstl, mark=("hst", 63), waits=[("h2", 63)])
    P.op(b, "scalar", lambda e: e.copy(stt["sums1"][:, :], psh1[:, 0:8]),
         waits=[("hst", 63)])
    P.op(b, "scalar", lambda e: e.copy(stt["sumsq1"][:, :], psh2[:, 0:8]),
         mark="ln2ev")

    s6.close()

    s7 = ExitStack()
    hrb = [sb(f"hrb{i}", [128, 2, Q], BF16, s7) for i in range(2)]
    th1 = [sb(f"th1{i}", [128, 2, Q], F32, s7) for i in range(1)]
    tpb = [sb(f"tpb{i}", [128, 8, 256], BF16, s7) for i in range(2)]
    xnr = sb("xnr", [128, Q], F32, s7)
    t2t = sb("t2t", [128, Q], F32, s7)
    y2t = [sb(f"y2t{i}", [128, Q], F32, s7) for i in range(2)]
    y2b = sb("y2b", [128, Q], BF16, s7)
    y2b2 = sb("y2b2", [128, Q], BF16, s7)
    ytb = sb("ytb", [128, Q], F32, s7)
    py = psum("py", [128, 2, 2, 512], s7)
    psy1 = psum("psy1", [128, 512], s7)
    psy2 = psum("psy2", [128, 512], s7)
    pbc7 = psum("pbc7", [128, 1024], s7)

    b = P.block()
    ln_chain(FF, stt["mu2"], stt["R2"], False, "ln2", "ln2ev")
    trred(rmax, stt["mx"], ALU.max, "mx2", [("htree", 63)], pbc7)
    trred(rmin, stt["mn"], ALU.min, "mn2", ["mx2r"], pbc7)
    P.op(b, "vector", lambda e: e.tensor_tensor(stt["tmp1"][:, :], stt["mx"][:, :],
                                                stt["mu2"][:, :], ALU.subtract),
         waits=["mn2r", "ln2R"])
    P.op(b, "vector", lambda e: e.tensor_tensor(stt["tmp2"][:, :], stt["mu2"][:, :],
                                                stt["mn"][:, :], ALU.subtract))
    P.op(b, "vector", lambda e: e.tensor_tensor(stt["tmp1"][:, :], stt["tmp1"][:, :],
                                                stt["tmp2"][:, :], ALU.max))
    P.op(b, "vector", lambda e: e.tensor_tensor(stt["gx"][:, :], stt["tmp1"][:, :],
                                                stt["R2"][:, :], ALU.mult))
    P.op(b, "vector", lambda e: e.tensor_scalar(stt["c2"][:, :], stt["gx"][:, :],
                                                CLIPB, None, ALU.mult))
    P.op(b, "vector", lambda e: e.tensor_scalar(stt["nc2"][:, :], stt["gx"][:, :],
                                                -CLIPB, None, ALU.mult), mark="cx2")
    bcast(stt["mu2"], muB, "cmu", "cx2", pbc7)
    bcast(stt["R2"], RB, "cR", "cmub", pbc7)
    bcast(stt["c2"], cB, "ccx", "cRb", pbc7)
    bcast(stt["nc2"], ncB, "cncx", "ccxb", pbc7)

    def emit_tpr(dw, ocg):
        if ocg >= 2:
            wt = [("pmm", dw, ocg - 2)]
        elif dw >= 1:
            wt = [("pmm", dw - 1, 6 + ocg)]
        else:
            wt = [("tpw", 63)]
        P.op(b, "gpsimd",
             lambda e, dw=dw, ocg=ocg: e.dma_start(
                 tpb[ocg % 2].ap(),
                 tproj[ocg * 1024:(ocg + 1) * 1024, dw * 256:(dw + 1) * 256]
                 .rearrange("(g p) d -> p g d", p=128)),
             sem=f"tR{ocg % 2}", k=16, mark=("tpr", dw, ocg), waits=wt)

    BC = lambda t: t.ap().unsqueeze(1).broadcast_to([128, 2, Q])
    for tl in range(32):
        oc0 = tl * 2
        P.op(b, "sync",
             lambda e, tl=tl, oc0=oc0: e.dma_start(
                 hrb[tl % 2].ap(),
                 hspill[oc0 * 128:(oc0 + 2) * 128, :]
                 .rearrange("(a p) c -> p a c", p=128)),
             sem=f"hR{tl % 2}", k=16, mark=("hrd", tl),
             waits=([("x2a", tl - 2)] if tl >= 2 else []))
        buf = th1[0]
        P.op(b, "vector",
             lambda e, tl=tl, buf=buf: e.tensor_tensor(
                 buf[:, :, :], hrb[tl % 2][:, :, :], BC(muB), ALU.subtract),
             waits=[("hrd", tl), "cncxb"])
        P.op(b, "vector",
             lambda e, tl=tl, buf=buf: e.tensor_tensor(
                 hrb[tl % 2][:, :, :], buf[:, :, :], BC(RB), ALU.mult))
        P.op(b, "vector",
             lambda e, tl=tl: e.tensor_tensor(hrb[tl % 2][:, :, :],
                                              hrb[tl % 2][:, :, :], BC(cB),
                                              ALU.min))
        P.op(b, "vector",
             lambda e, tl=tl, oc0=oc0: e.tensor_tensor(
                 x2w(oc0), hrb[tl % 2][:, :, :], BC(ncB), ALU.max),
             mark=("x2a", tl))
        if tl % 4 == 3:
            emit_tpr(0, tl // 4)
    for dw in range(8):
        for ocg in range(8):
            if dw >= 1:
                emit_tpr(dw, ocg)

            def pmmf(e, dw=dw, ocg=ocg):
                ins = None
                for gi in range(8):
                    oc = ocg * 8 + gi
                    for dj in range(2):
                        for qh in range(2):
                            ins = e.matmul(py[:, dj, qh, :],
                                           tpb[ocg % 2][:, gi, dj * 128:(dj + 1) * 128],
                                           x2sl(oc, qh * 512, (qh + 1) * 512),
                                           start=(oc == 0), stop=(oc == 63))
                return ins
            wt2 = [("tpr", dw, ocg), ("x2a", ocg * 4 + 3)]
            if ocg == 0 and dw >= 1:
                wt2.append(("y2e", dw - 1, 1))
            P.op(b, "tensor", pmmf, mark=("pmm", dw, ocg), waits=wt2)
        for dj in range(2):
            doc = dw * 2 + dj
            P.op(b, "scalar",
                 lambda e, dj=dj: e.activation(
                     t2t[:, :], py.ap()[:, dj, :, :].rearrange("p a c -> p (a c)"),
                     AF.Identity, scale=gpb[:, :]),
                 mark=("y2e", dw, dj),
                 waits=[("pmm", dw, 7)] + ([("y2add", doc - 1)] if doc >= 1 else []))
            P.op(b, "sync",
                 lambda e, doc=doc: e.dma_start(xnr[:, :],
                                                xnspill[doc * 128:(doc + 1) * 128, :]),
                 sem="xR", k=16, mark=("xnr", doc),
                 waits=[("y2add", doc - 1)] if doc >= 1 else [])
            P.op(b, "vector",
                 lambda e, doc=doc: e.tensor_tensor(y2t[doc % 2][:, :], xnr[:, :],
                                                    rhoB[:, :], ALU.mult),
                 waits=[("xnr", doc), ("y2e", dw, dj)] +
                       ([("y2tr", doc - 2)] if doc >= 2 else []))
            P.op(b, "vector",
                 lambda e, doc=doc: e.tensor_tensor(y2t[doc % 2][:, :],
                                                    y2t[doc % 2][:, :], t2t[:, :],
                                                    ALU.add),
                 mark=("y2add", doc))
            P.op(b, "vector",
                 lambda e, doc=doc: e.tensor_copy(y2b[:, :], y2t[doc % 2][:, :]),
                 waits=[("y2st", doc - 1)] if doc >= 1 else [])
            P.op(b, "vector",
                 lambda e: e.tensor_tensor(y2b2[:, :], y2b[:, :], y2b[:, :], ALU.mult),
                 mark=("y2b2", doc))
            if doc == 0:
                P.op(b, "vector", lambda e: e.tensor_copy(rmax[:, :], y2b[:, :]),
                     waits=["mn2t"])
                P.op(b, "vector", lambda e: e.tensor_copy(rmin[:, :], y2b[:, :]),
                     mark=("y2tree", 0))
            else:
                P.op(b, "vector",
                     lambda e: e.tensor_tensor(rmax[:, :], rmax[:, :], y2b[:, :],
                                               ALU.max))
                P.op(b, "vector",
                     lambda e: e.tensor_tensor(rmin[:, :], rmin[:, :], y2b[:, :],
                                               ALU.min),
                     mark=("y2tree", doc))

            def y2stf(e, doc=doc):
                ins = None
                for qb in range(8):
                    ins = e.matmul(psy1[:, qb:qb + 1], y2b[:, qb * 128:(qb + 1) * 128],
                                   ones_bf[:, :], start=(doc == 0 and qb == 0),
                                   stop=(doc == 15 and qb == 7))
                    ins = e.matmul(psy2[:, qb:qb + 1], y2b2[:, qb * 128:(qb + 1) * 128],
                                   ones_bf[:, :], start=(doc == 0 and qb == 0),
                                   stop=(doc == 15 and qb == 7))
                return ins
            P.op(b, "tensor", y2stf, mark=("y2st", doc), waits=[("y2b2", doc)])
            def ytrf(e, doc=doc):
                ins = None
                for c in range(8):
                    ins = e.transpose(pbc7[:, c * 128:(c + 1) * 128],
                                      y2t[doc % 2][:, c * 128:(c + 1) * 128],
                                      id128[:, :])
                return ins
            P.op(b, "tensor", ytrf, mark=("y2tr", doc),
                 waits=[("y2add", doc)] +
                       ([("y2ev", doc - 1)] if doc >= 1 else ["cncxb"]))
            P.op(b, "scalar",
                 lambda e: e.copy(ytb[:, :], pbc7[:, :]),
                 mark=("y2ev", doc),
                 waits=[("y2tr", doc)] + ([("y2Td", doc - 1)] if doc >= 1 else []))
            P.op(b, "sync",
                 lambda e, doc=doc: e.dma_start(
                     y2Tsp.ap().rearrange("(c p) d -> p c d", p=128)
                     [:, :, doc * 128:(doc + 1) * 128],
                     ytb.ap().rearrange("p (c d) -> p c d", c=8)),
                 sem=f"yW{doc % 2}", k=16, mark=("y2Td", doc), waits=[("y2ev", doc)])

    P.op(b, "scalar", lambda e: e.copy(stt["sums1"][:, :], psy1[:, 0:8]),
         waits=[("y2st", 15)])
    P.op(b, "scalar", lambda e: e.copy(stt["sumsq1"][:, :], psy2[:, 0:8]),
         mark="ln3ev")

    s7.close()

    s9 = ExitStack()
    y2r = [sb(f"y2r{i}", [128, D], F32, s9) for i in range(2)]
    to1 = [sb(f"to1{i}", [128, D], F32, s9) for i in range(2)]

    b = P.block()
    ln_chain(D, stt["mu3"], stt["R3"], False, "ln3", "ln3ev")
    P.op(b, "vector", lambda e: e.tensor_tensor(stt["tmp1"][:, :], stt["mu3"][:, :],
                                                stt["R3"][:, :], ALU.mult),
         waits=["ln3R"])
    P.op(b, "vector", lambda e: e.tensor_scalar(stt["tmp2"][:, :], stt["tmp1"][:, :],
                                                -1.0, None, ALU.mult), mark="nmr3")

    for tq in range(8):
        P.op(b, "sync",
             lambda e, tq=tq: e.dma_start(y2r[tq % 2][:, :],
                                          y2Tsp[tq * 128:(tq + 1) * 128, :]),
             sem=f"yR{tq % 2}", k=16, mark=("y2r", tq),
             waits=[("outs", tq - 2)] if tq >= 2 else [])
        P.op(b, "scalar",
             lambda e, tq=tq: e.activation(to1[tq % 2][:, :], y2r[tq % 2][:, :],
                                           AF.Identity,
                                           scale=stt["R3"][:, tq:tq + 1],
                                           bias=stt["tmp2"][:, tq:tq + 1]),
             mark=("outs", tq),
             waits=[("y2r", tq), "nmr3"] + ([("outd", tq - 2)] if tq >= 2 else []))
        P.op(b, "sync",
             lambda e, tq=tq: e.dma_start(out[tq * 128:(tq + 1) * 128, :],
                                          to1[tq % 2][:, :]),
             sem=f"oD{tq % 2}", k=16, mark=("outd", tq), waits=[("outs", tq)])
    P.op(b, "sync", lambda e: e.nop(), sem="dX", waits=[("outd", 6), ("outd", 7)])

    s9.close()
    P.resolve()
    P.emit(nc, sems)
    gs.close()
    return nc


_NC_CACHE = {}


def _get_nc():
    if "nc" not in _NC_CACHE:
        _NC_CACHE["nc"] = build_program()
    return _NC_CACHE["nc"]


def kernel(L0_attn_x, L0_attn_wq_codes, L0_attn_wq_scales, L0_attn_bq,
           L0_attn_wk_codes, L0_attn_wk_scales, L0_attn_bk,
           L0_attn_wv_codes, L0_attn_wv_scales, L0_attn_bv,
           L0_attn_wo_codes, L0_attn_wo_scales, L0_attn_bo,
           L0_ffn_w_fc, L0_ffn_w_proj):
    x = np.asarray(L0_attn_x, dtype=np.float32)
    B = x.shape[0]
    rep = lambda s: np.ascontiguousarray(np.repeat(np.asarray(s, np.float32), 32, axis=0))
    bias_t = lambda v: np.ascontiguousarray(np.asarray(v, np.float32).reshape(16, 128).T)
    common = {
        "wq": np.ascontiguousarray(np.asarray(L0_attn_wq_codes, np.float32)),
        "wk": np.ascontiguousarray(np.asarray(L0_attn_wk_codes, np.float32)),
        "wv": np.ascontiguousarray(np.asarray(L0_attn_wv_codes, np.float32)),
        "wo": np.ascontiguousarray(np.asarray(L0_attn_wo_codes, np.float32)),
        "sq": rep(L0_attn_wq_scales), "sk": rep(L0_attn_wk_scales),
        "sv": rep(L0_attn_wv_scales), "so": rep(L0_attn_wo_scales),
        "bq": bias_t(L0_attn_bq), "bk": bias_t(L0_attn_bk),
        "bv": bias_t(L0_attn_bv), "bo": bias_t(L0_attn_bo),
        "wfc": np.ascontiguousarray(np.asarray(L0_ffn_w_fc, np.float32).T),
        "wpr": np.ascontiguousarray(np.asarray(L0_ffn_w_proj, np.float32).T),
    }
    in_maps = []
    for c in range(8):
        bb, half = c // 2, c % 2
        xT = x[bb].T
        xr = np.ascontiguousarray(np.concatenate(
            [xT[:, half * Q:], xT[:, :half * Q]], axis=1))
        m = dict(common)
        m["xt"] = xr
        m["xq"] = np.ascontiguousarray(xr[:, :Q])
        in_maps.append(m)
    nc = _get_nc()
    res = run_bass_kernel_spmd(nc, in_maps, core_ids=list(range(8)))
    y_out = np.empty((B, S, D), dtype=np.float32)
    for c in range(8):
        bb, half = c // 2, c % 2
        y_out[bb, half * Q:(half + 1) * Q, :] = np.asarray(res.results[c]["out"])
    return y_out
